# version 33
# baseline (speedup 1.0000x reference)
import contextlib
import math
import numpy as np
import ml_dtypes
import concourse.bass as bass
import concourse.mybir as mybir
from concourse.bass_utils import run_bass_kernel_spmd
from concourse.alu_op_type import AluOpType

F32 = mybir.dt.float32
BF16 = mybir.dt.bfloat16
I32 = mybir.dt.int32
AF = mybir.ActivationFunctionType
AX = mybir.AxisListType
BF = ml_dtypes.bfloat16

NCORES = 8
C = 1024
H = 16
D = 64
HID = 3072
DC = C // 128
HCN = HID // 128
HCP = HCN // 2

ENGS = ("sync", "gpsimd", "vector", "scalar", "tensor")


class KB:

    QSEMS = {"sync": ("dl0", "dl1"), "gpsimd": ("dg0", "dg1")}

    def __init__(self, nc, block, sem):
        self.nc = nc
        self.block = block
        self.sem = sem
        self.cnt = {k: 0 for k in sem}
        self.pend = []
        self.waited = {e: {} for e in ENGS}
        self.qidx = {"sync": 0, "gpsimd": 0}

    def dma(self, queue, fn, waits=()):
        sems = self.QSEMS[queue]
        i = self.qidx[queue]
        self.qidx[queue] ^= 1
        s = sems[i]
        auto = [(s, self.cnt[s])]
        cnt = self.op(queue, fn, waits=list(waits) + auto, inc=s, inc_n=16)
        return (s, cnt)

    def op(self, eng, fn, waits=(), inc=None, inc_n=1):
        waits = [(s, v) for (s, v) in waits if v and v > 0]
        incs = []
        if inc is not None:
            self.cnt[inc] += inc_n
            incs.append((inc, inc_n))
        self.pend.append((eng, fn, waits, incs))
        return self.cnt[inc] if inc is not None else 0

    def c(self, s):
        return self.cnt[s]

    def flush(self):
        for eng in ENGS:
            ops = [(fn, w, i) for (e, fn, w, i) in self.pend if e == eng]
            if not ops:
                continue
            waited = self.waited[eng]
            sem = self.sem

            def section(esec, ops=ops, waited=waited):
                for fn, waits, incs in ops:
                    for s, v in waits:
                        if v > waited.get(s, 0):
                            esec.wait_ge(sem[s], v)
                            waited[s] = v
                    instr = fn(esec)
                    for s, n in incs:
                        instr.then_inc(sem[s], n)

            getattr(self.block, eng)(section)
        self.pend = []


def build_program(T):
    NT = T // 128
    NTB = T // 512
    nc = bass.Bass("TRN2", target_bir_lowering=False)

    def din(name, shape, dt):
        return nc.dram_tensor(name, shape, dt, kind="ExternalInput").ap()

    xin = din("xin", [T, C], F32)
    wattn = din("wattn", [C, 3 * C], F32)
    battn_qk = din("battn_qk", [128, 16], F32)
    battn_v = din("battn_v", [1, C], F32)
    wproj = din("wproj", [C, C], F32)
    bproj = din("bproj", [1, C], F32)
    temp = din("temp", [1, 1], F32)
    wg_r = din("wg_r", [HCP, 128, 2048], I32)
    wu_r = din("wu_r", [HCP, 128, 2048], I32)
    wd_r = din("wd_r", [HCP, 128, 2048], I32)
    ident = din("ident", [128, 128], BF16)
    alibi_bc = din("alibi_bc", [H, 128, T], F32)
    ones_b = din("ones_b", [1, 128], BF16)
    ones_f = din("ones_f", [1, 128], F32)

    out_d = nc.dram_tensor("out", [T, C], F32, kind="ExternalOutput").ap()
    wg_sc = nc.dram_tensor("wg_sc", [HCP, 128, 2048], BF16, kind="Internal").ap()
    wu_sc = nc.dram_tensor("wu_sc", [HCP, 128, 2048], BF16, kind="Internal").ap()
    wd_sc = nc.dram_tensor("wd_sc", [HCP, 128, 2048], BF16, kind="Internal").ap()
    xf_sc = nc.dram_tensor("xf_sc", [T, C], F32, kind="Internal").ap()
    q_sc = nc.dram_tensor("q_sc", [DC, 128, T], BF16, kind="Internal").ap()
    wpj_sc = nc.dram_tensor("wpj_sc", [DC, 128, C], BF16, kind="Internal").ap()
    k_sc = nc.dram_tensor("k_sc", [DC, 128, T], BF16, kind="Internal").ap()

    with contextlib.ExitStack() as st:
        E = st.enter_context
        sem = {s: E(nc.semaphore("sem_" + s)) for s in
               ("dl0", "dl1", "dg0", "dg1", "pe", "v", "a")}
        block = E(nc.Block())
        S = KB(nc, block, sem)

        ident_s = E(nc.sbuf_tensor("ident_s", [128, 128], BF16))
        onesb_s = E(nc.sbuf_tensor("onesb_s", [1, 128], BF16))
        onesf_s = E(nc.sbuf_tensor("onesf_s", [1, 128], F32))
        bqk_s = E(nc.sbuf_tensor("bqk_s", [128, 16], F32))
        bv_b = E(nc.sbuf_tensor("bv_b", [1, C], BF16))
        bpj_b = E(nc.sbuf_tensor("bpj_b", [1, C], BF16))
        temp_s = E(nc.sbuf_tensor("temp_s", [1, 1], F32))
        t8_s = E(nc.sbuf_tensor("t8_s", [1, 1], F32))
        c11_s = E(nc.sbuf_tensor("c11_s", [1, 1], F32))
        cvec_s = E(nc.sbuf_tensor("cvec_s", [128, 1], F32))
        cbq_s = E(nc.sbuf_tensor("cbq_s", [128, 8], F32))
        Vt = E(nc.sbuf_tensor("Vt", [128, NT * C], BF16))

        cl = []
        for dst, src in [(ident_s, ident),
                         (onesb_s, ones_b), (onesf_s, ones_f),
                         (bqk_s, battn_qk), (temp_s, temp)]:
            cl.append(S.dma("sync", lambda e, d=dst, s=src: e.dma_start(out=d[:, :], in_=s)))
        c_consts = [cl[-1], cl[-2]]
        cv_t8 = S.op("vector", lambda e: e.tensor_scalar(
            t8_s[0:1, :], temp_s[0:1, :], 8.0, None, AluOpType.mult),
            waits=c_consts, inc="v")
        cv_c11 = S.op("vector", lambda e: e.reciprocal(c11_s[0:1, :], t8_s[0:1, :]),
                      waits=[("v", cv_t8)], inc="v")

        with contextlib.ExitStack() as sA:
            xnT = sA.enter_context(nc.sbuf_tensor("xnT", [128, DC * T], BF16, side="right"))
            wdres = sA.enter_context(nc.sbuf_tensor("wdres", [128, HCP * 2048], BF16, side="right"))

            with contextlib.ExitStack() as sB:
                E1 = sB.enter_context
                xstage = [E1(nc.sbuf_tensor(f"xstage{i}", [128, C], F32, side="right")) for i in range(4)]
                xn_st = [E1(nc.sbuf_tensor(f"xn_st{i}", [128, C], BF16, side="right")) for i in range(4)]
                sq_sc = [E1(nc.sbuf_tensor(f"sq_sc{i}", [128, C], F32, side="right")) for i in range(2)]
                wistage = [E1(nc.sbuf_tensor(f"wistage{i}", [128, 2048], I32, side="right")) for i in range(3)]
                cbuf = [E1(nc.sbuf_tensor(f"cbuf{i}", [128, 2048], BF16, side="right")) for i in range(2)]
                cbuf32 = [E1(nc.sbuf_tensor(f"cbuf32_{i}", [128, 2048], F32, side="right")) for i in range(2)]
                stat = {k: [E1(nc.sbuf_tensor(f"st_{k}{i}", [128, 1], F32, side="right")) for i in range(4)]
                        for k in ("s1", "s2", "m", "v", "r", "n")}
                bst_f = E1(nc.sbuf_tensor("bst_f", [1, C], F32, side="right"))
                wpst = E1(nc.sbuf_tensor("wpst", [128, C], F32, side="right"))
                ps1 = E1(nc.psum_tensor("ps1", [128, 4096], F32))

                bld1 = S.dma("sync", lambda e: e.dma_start(out=bst_f[0:1, 0:C], in_=battn_v))
                cv_bv = S.op("vector", lambda e: e.tensor_copy(bv_b[0:1, :], bst_f[0:1, 0:C]),
                             waits=[bld1], inc="v")
                bld2 = S.dma("sync", lambda e: e.dma_start(out=bst_f[0:1, 0:C], in_=bproj),
                             waits=[("v", cv_bv)])
                cv_bp = S.op("vector", lambda e: e.tensor_copy(bpj_b[0:1, :], bst_f[0:1, 0:C]),
                             waits=[bld2], inc="v")

                conv_tasks = []
                for m, src, dst_sc, width in (
                        [(0, wg_r, wg_sc, 2048), (1, wu_r, wu_sc, 2048),
                         (2, wd_r, None, 2048)]):
                    for j in range(HCP):
                        conv_tasks.append((src, dst_sc, j, width))
                for j in range(DC):
                    conv_tasks.append(("wproj", wpj_sc, j, 1024))
                n_tasks = len(conv_tasks)
                conv_i = [0]
                cbuf_store = [None, None, None]
                wist_conv = [0, 0, 0]
                c32_war = [0, 0, 0]
                wpst_war = [0]
                conv_stores = []

                def emit_conv(k):
                    src, dst_sc, j, width = conv_tasks[k]
                    bi = k % 3
                    b = k % 2
                    lq = "sync"
                    is_pj = (src == "wproj")
                    is_wd = dst_sc is None
                    if is_pj:
                        ld = S.dma("sync", lambda e, j=j: e.dma_start(
                            out=wpst[:, :], in_=wproj[j * 128:(j + 1) * 128, :]),
                            waits=[("a", wpst_war[0])])
                        cv = S.op("scalar", lambda e, b=b: e.activation(
                            cbuf[b][:, 0:1024], wpst[:, :], AF.Copy),
                            waits=[ld] + ([cbuf_store[b]] if cbuf_store[b] else []), inc="a")
                        wpst_war[0] = cv
                    else:
                        ld = S.dma(lq, lambda e, src=src, j=j, bi=bi, width=width: e.dma_start(
                            out=wistage[bi][:, 0:width], in_=src[j, :, :]),
                            waits=[("v", wist_conv[bi])])
                        cv0 = S.op("vector", lambda e, b=b, bi=bi, width=width: e.tensor_scalar(
                            cbuf32[b][:, 0:width], wistage[bi][:, 0:width], 1.0,
                            None, AluOpType.mult),
                            waits=[ld, ("a", c32_war[b])]
                                  + ([cbuf_store[b]] if cbuf_store[b] else []), inc="v")
                        cdst = (lambda e, j=j: wdres[:, j * 2048:(j + 1) * 2048]) if is_wd \
                            else (lambda e, b=b, width=width: cbuf[b][:, 0:width])
                        cv = S.op("scalar", lambda e, b=b, width=width, cdst=cdst: e.activation(
                            cdst(e), cbuf32[b][:, 0:width], AF.Copy, scale=1.0 / 127.0),
                            waits=[("v", cv0)], inc="a")
                        wist_conv[bi] = cv0
                        c32_war[b] = cv
                    if is_wd:
                        return
                    sto = S.dma("gpsimd", lambda e, dst_sc=dst_sc, j=j, b=b, width=width: e.dma_start(
                        out=dst_sc[j, :, :], in_=cbuf[b][:, 0:width]),
                        waits=[("a", cv)])
                    cbuf_store[b] = sto
                    conv_stores.append(sto)

                ln_xn = [0] * NT
                stile_tr = [0] * (NT // 4)
                tr_evac = [[(None, 0)] * DC for _ in range(NT // 4)]
                prev_xuse = [0, 0, 0, 0]
                stat_war = [0, 0, 0, 0]
                sq_war = [0, 0, 0, 0]
                for i in range(NT):
                    for _ in range(3):
                        if conv_i[0] < n_tasks:
                            emit_conv(conv_i[0])
                            conv_i[0] += 1
                    b = i % 4
                    p = i % 4
                    ld = S.dma("gpsimd", lambda e, i=i, b=b: e.dma_start(
                        out=xstage[b][:, :], in_=xin[i * 128:(i + 1) * 128, :]),
                        waits=[("v", prev_xuse[b])])
                    rs = S.op("vector", lambda e, b=b, p=p: e.tensor_reduce(
                        stat["s1"][p][:, :], xstage[b][:, :], AX.X, AluOpType.add),
                        waits=[ld, ("v", stat_war[p])], inc="v")
                    sq = S.op("scalar", lambda e, b=b, p=p: e.activation(
                        sq_sc[p % 2][:, :], xstage[b][:, :], AF.Square,
                        accum_out=stat["s2"][p][:, :]),
                        waits=[ld, ("v", sq_war[p]), ("v", sq_war[(p + 2) % 4])], inc="a")
                    mn = S.op("vector", lambda e, p=p: e.tensor_scalar(
                        stat["m"][p][:, :], stat["s1"][p][:, :], 1.0 / C, None,
                        AluOpType.mult), waits=[("v", rs)], inc="v")
                    v1 = S.op("vector", lambda e, p=p: e.tensor_tensor(
                        stat["v"][p][:, :], stat["m"][p][:, :], stat["m"][p][:, :],
                        AluOpType.mult), waits=[("v", mn), ("a", sq)], inc="v")
                    v2 = S.op("vector", lambda e, p=p: e.tensor_scalar(
                        stat["s2"][p][:, :], stat["s2"][p][:, :], 1.0 / C, 1e-5,
                        AluOpType.mult, AluOpType.add), waits=[("a", sq)], inc="v")
                    v3 = S.op("vector", lambda e, p=p: e.tensor_tensor(
                        stat["v"][p][:, :], stat["s2"][p][:, :], stat["v"][p][:, :],
                        AluOpType.subtract), waits=[("v", v2)], inc="v")
                    sq_war[p] = v3
                    sd = S.op("scalar", lambda e, p=p: e.activation(
                        stat["v"][p][:, :], stat["v"][p][:, :], AF.Sqrt),
                        waits=[("v", v3)], inc="a")
                    rc = S.op("vector", lambda e, p=p: e.reciprocal(
                        stat["r"][p][:, :], stat["v"][p][:, :]),
                        waits=[("a", sd)], inc="v")
                    ng = S.op("vector", lambda e, p=p: e.tensor_scalar(
                        stat["n"][p][:, :], stat["m"][p][:, :], stat["r"][p][:, :],
                        -1.0, AluOpType.mult, AluOpType.mult),
                        waits=[("v", rc)], inc="v")
                    stile = i // 4
                    xn = S.op("vector", lambda e, i=i, b=b, p=p: e.tensor_scalar(
                        xn_st[i % 4][:, :], xstage[b][:, :], stat["r"][p][:, :],
                        stat["n"][p][:, :], AluOpType.mult, AluOpType.add),
                        waits=[("v", ng),
                               ("pe", stile_tr[stile - 1] if stile >= 1 else 0)],
                        inc="v")
                    prev_xuse[b] = xn
                    stat_war[p] = xn
                    ln_xn[i] = xn
                    if i % 4 == 3:
                        last = 0
                        for dc in range(DC):
                            for j in range(4):
                                wts = [("v", ln_xn[stile * 4 + j])]
                                if stile >= 1:
                                    es, ev = tr_evac[stile - 1][dc]
                                    wts.append((es, ev))
                                last = S.op("tensor", lambda e, dc=dc, j=j, stile=stile: e.matmul(
                                    ps1[:, dc * 512 + j * 128: dc * 512 + (j + 1) * 128],
                                    xn_st[(stile * 4 + j) % 4][:, dc * 128:(dc + 1) * 128],
                                    ident_s[:, :], start=(j == 0), stop=(j == 3)),
                                    waits=wts, inc="pe" if j == 3 else None)
                            if dc % 2 == 0:
                                ev = S.op("vector", lambda e, dc=dc, stile=stile: e.tensor_copy(
                                    xnT[:, dc * T + stile * 512: dc * T + (stile + 1) * 512],
                                    ps1[:, dc * 512:(dc + 1) * 512]),
                                    waits=[("pe", last)], inc="v")
                                tr_evac[stile][dc] = ("v", ev)
                            else:
                                ev = S.op("scalar", lambda e, dc=dc, stile=stile: e.activation(
                                    xnT[:, dc * T + stile * 512: dc * T + (stile + 1) * 512],
                                    ps1[:, dc * 512:(dc + 1) * 512], AF.Copy),
                                    waits=[("pe", last)], inc="a")
                                tr_evac[stile][dc] = ("a", ev)
                        stile_tr[stile] = last
                while conv_i[0] < n_tasks:
                    emit_conv(conv_i[0])
                    conv_i[0] += 1
                conv_stores_fin = [("dg0", S.c("dg0")), ("dg1", S.c("dg1"))]
                p1_v, p1_a, p1_pe = S.c("v"), S.c("a"), S.c("pe")
                S.flush()

            with contextlib.ExitStack() as sF:
                E2 = sF.enter_context
                wgbuf = [E2(nc.sbuf_tensor(f"wgbuf{i}", [128, 2048], BF16, side="right")) for i in range(2)]
                wubuf = [E2(nc.sbuf_tensor(f"wubuf{i}", [128, 2048], BF16, side="right")) for i in range(2)]
                hbuf = E2(nc.sbuf_tensor("hbuf", [128, HCN * 512], BF16, side="right"))
                gs_s = [E2(nc.sbuf_tensor(f"gs_s{i}", [128, 512], BF16, side="right")) for i in range(2)]
                us_s = [E2(nc.sbuf_tensor(f"us_s{i}", [128, 512], BF16, side="right")) for i in range(2)]
                x2st = [E2(nc.sbuf_tensor("x2st0", [128, C], F32, side="right"))] * 2
                xfout = [E2(nc.sbuf_tensor(f"xfout{i}", [128, C], F32, side="right")) for i in range(2)]
                ps_g = E2(nc.psum_tensor("ps_g", [128, 1024], F32))
                ps_u = E2(nc.psum_tensor("ps_u", [128, 1024], F32))
                ps_dn = E2(nc.psum_tensor("ps_dn", [128, 2048], F32))

                wgu_ld = {}
                wgld_war = [0, 0]
                grel = [p1_a, p1_a]
                urel = [p1_a, p1_a]
                gs_war = [0, 0]
                dnrel = [0, 0, 0, 0]
                hbuf_war = 0
                x2_war = [0, 0]
                xf_war = [None, None]
                for tb in range(NTB):
                    for hcp in range(HCP):
                        b = hcp % 2
                        wts = conv_stores_fin + [("pe", wgld_war[b])]
                        if tb == 0 and hcp == 0:
                            wts += [("pe", p1_pe), ("v", p1_v), ("a", p1_a)]
                        ldg = S.dma("sync", lambda e, hcp=hcp, b=b: e.dma_start(
                            out=wgbuf[b][:, :], in_=wg_sc[hcp, :, :]), waits=wts)
                        ldu = S.dma("sync", lambda e, hcp=hcp, b=b: e.dma_start(
                            out=wubuf[b][:, :], in_=wu_sc[hcp, :, :]))
                        wgu_ld[(tb, hcp)] = [ldg, ldu]
                        for hc2 in range(2):
                            hc = hcp * 2 + hc2
                            gp = hc % 2
                            wts_g = [ldg, ldu, ("a", grel[gp])]
                            if tb == 0 and hc == 0:
                                wts_g += [("v", p1_v), ("a", p1_a), ("pe", p1_pe)]
                            lg = lu = 0
                            for dc in range(DC):
                                lg = S.op("tensor", lambda e, hc2=hc2, dc=dc, gp=gp, b=b, tb=tb: e.matmul(
                                    ps_g[:, gp * 512:(gp + 1) * 512],
                                    wgbuf[b][:, hc2 * 1024 + dc * 128: hc2 * 1024 + (dc + 1) * 128],
                                    xnT[:, dc * T + tb * 512: dc * T + tb * 512 + 512],
                                    start=(dc == 0), stop=(dc == DC - 1)),
                                    waits=wts_g if dc == 0 else (),
                                    inc="pe" if dc == DC - 1 else None)
                            for dc in range(DC):
                                lu = S.op("tensor", lambda e, hc2=hc2, dc=dc, gp=gp, b=b, tb=tb: e.matmul(
                                    ps_u[:, gp * 512:(gp + 1) * 512],
                                    wubuf[b][:, hc2 * 1024 + dc * 128: hc2 * 1024 + (dc + 1) * 128],
                                    xnT[:, dc * T + tb * 512: dc * T + tb * 512 + 512],
                                    start=(dc == 0), stop=(dc == DC - 1)),
                                    waits=[("a", urel[gp])] if dc == 0 else (),
                                    inc="pe" if dc == DC - 1 else None)
                            wgld_war[b] = lu
                            sl = S.op("scalar", lambda e, gp=gp: e.activation(
                                gs_s[gp][:, :], ps_g[:, gp * 512:(gp + 1) * 512], AF.Silu),
                                waits=[("pe", lg), ("v", gs_war[gp])], inc="a")
                            grel[gp] = sl
                            ue = S.op("scalar", lambda e, gp=gp: e.activation(
                                us_s[gp][:, :], ps_u[:, gp * 512:(gp + 1) * 512], AF.Copy),
                                waits=[("pe", lu), ("v", gs_war[gp])], inc="a")
                            urel[gp] = ue
                            wts_m = [("a", ue)]
                            if tb > 0 and hc == 0:
                                wts_m.append(("pe", hbuf_war))
                            ml = S.op("vector", lambda e, hc=hc, gp=gp: e.tensor_tensor(
                                hbuf[:, hc * 512:(hc + 1) * 512], gs_s[gp][:, :],
                                us_s[gp][:, :], AluOpType.mult),
                                waits=wts_m, inc="v")
                            gs_war[gp] = ml
                    mults_done = S.c("v")
                    for pas in range(2):
                        lastd = [0] * 4
                        for hcp2 in range(HCN // 2):
                            for hc2 in range(2):
                                hc = 2 * hcp2 + hc2
                                for tsh in range(4):
                                    ts = pas * 2 + tsh // 2
                                    half = tsh % 2
                                    wts_d = []
                                    if hc == 0:
                                        wts_d += [("v", mults_done), ("v", dnrel[tsh])]
                                    r = S.op("tensor", lambda e, hc=hc, hcp2=hcp2, hc2=hc2, ts=ts, half=half, tsh=tsh: e.matmul(
                                        ps_dn[:, tsh * 512:(tsh + 1) * 512],
                                        hbuf[:, hc * 512 + ts * 128: hc * 512 + ts * 128 + 128],
                                        wdres[:, hcp2 * 2048 + hc2 * 1024 + half * 512: hcp2 * 2048 + hc2 * 1024 + (half + 1) * 512],
                                        start=(hc == 0), stop=(hc == HCN - 1)),
                                        waits=wts_d, inc="pe" if tsh == 3 else None)
                                    if tsh == 3:
                                        if hc == HCN - 1:
                                            lastd = [r] * 4
                        for tsh in range(4):
                            ts = pas * 2 + tsh // 2
                            half = tsh % 2
                            tt = tb * 4 + ts
                            xb = ts % 2
                            if half == 0:
                                wts_x2 = [("v", x2_war[xb])]
                                if tb == 0 and pas == 0 and tsh == 0:
                                    wts_x2 += [("pe", p1_pe), ("v", p1_v), ("a", p1_a)] + conv_stores_fin
                                x2_ld = S.dma("gpsimd", lambda e, tt=tt, xb=xb: e.dma_start(
                                    out=x2st[xb][:, :], in_=xin[tt * 128:(tt + 1) * 128, :]),
                                    waits=wts_x2)
                            xa = S.op("vector", lambda e, xb=xb, tsh=tsh, half=half: e.tensor_tensor(
                                xfout[xb][:, half * 512:(half + 1) * 512],
                                ps_dn[:, tsh * 512:(tsh + 1) * 512],
                                x2st[xb][:, half * 512:(half + 1) * 512], AluOpType.add),
                                waits=[("pe", lastd[tsh]), x2_ld]
                                      + ([xf_war[xb]] if xf_war[xb] else []), inc="v")
                            dnrel[tsh] = xa
                            if half == 1:
                                x2_war[xb] = xa
                                xf_war[xb] = S.dma("gpsimd", lambda e, tt=tt, xb=xb: e.dma_start(
                                    out=xf_sc[tt * 128:(tt + 1) * 128, :], in_=xfout[xb][:, :]),
                                    waits=[("v", xa)])
                    hbuf_war = S.c("pe")
                p2_v, p2_a, p2_pe = S.c("v"), S.c("a"), S.c("pe")
                p2_xf_st = [("dg0", S.c("dg0")), ("dg1", S.c("dg1"))]
                S.flush()

            with contextlib.ExitStack() as sQ:
                E3 = sQ.enter_context
                wattn_b = E3(nc.sbuf_tensor("wattn_b", [128, DC * 3 * C], BF16, side="right"))
                qkst = [E3(nc.sbuf_tensor(f"qkst{i}", [128, 512], BF16, side="right")) for i in range(4)]
                wst3 = [E3(nc.sbuf_tensor(f"wst3_{i}", [128, 3 * C], F32, side="right")) for i in range(2)]
                ps_qk = E3(nc.psum_tensor("ps_qk", [128, 2048], F32))
                ps_v = E3(nc.psum_tensor("ps_v", [128, 2048], F32))

                mc = S.op("tensor", lambda e: e.matmul(
                    ps_qk[:, 0:1], onesf_s[0:1, :], c11_s[0:1, :], start=True, stop=True),
                    waits=[("v", cv_c11), ("v", p2_v), ("a", p2_a), ("pe", p2_pe)]
                          + p2_xf_st,
                    inc="pe")
                ce = S.op("scalar", lambda e: e.activation(
                    cvec_s[:, :], ps_qk[:, 0:1], AF.Copy),
                    waits=[("pe", mc)], inc="a")
                cq = S.op("vector", lambda e: e.tensor_scalar(
                    cbq_s[:, :], bqk_s[:, 0:8], cvec_s[:, :], None, AluOpType.mult),
                    waits=[("a", ce)], inc="v")

                wat_cv = 0
                wat_cv_of = [0] * DC
                wst_war = [0, 0]
                for dc in range(DC):
                    wb = dc % 2
                    ld = S.dma("sync", lambda e, dc=dc, wb=wb: e.dma_start(
                        out=wst3[wb][:, :], in_=wattn[dc * 128:(dc + 1) * 128, :]),
                        waits=[("v", wst_war[wb])]
                              + ([("pe", p2_pe), ("v", p2_v), ("a", p2_a)] + p2_xf_st
                                 if dc < 2 else []))
                    wat_cv = S.op("vector", lambda e, dc=dc, wb=wb: e.tensor_copy(
                        wattn_b[:, dc * 3072:(dc + 1) * 3072], wst3[wb][:, :]),
                        waits=[ld], inc="v")
                    wat_cv_of[dc] = wat_cv
                    wst_war[wb] = wat_cv

                qk_rel = [("a", ce), ("v", 0), ("v", 0), ("v", 0)]
                qk_st_war = [("v", 0)] * 4
                store_of_fc = {}
                qk_st_fin = None
                for fc in range(16):
                    for tc in range(NTB):
                        bank = (fc * NTB + tc) % 4
                        lastm = 0
                        for dc in range(DC):
                            wts = [("v", wat_cv_of[dc])]
                            if dc == 0:
                                wts.append(qk_rel[bank])
                            lastm = S.op("tensor", lambda e, fc=fc, tc=tc, dc=dc, bank=bank: e.matmul(
                                ps_qk[:, bank * 512:(bank + 1) * 512],
                                wattn_b[:, dc * 3072 + fc * 128: dc * 3072 + (fc + 1) * 128],
                                xnT[:, dc * T + tc * 512: dc * T + (tc + 1) * 512],
                                start=(dc == 0), stop=(dc == DC - 1)),
                                waits=wts, inc="pe" if dc == DC - 1 else None)
                        stb = qkst[bank]
                        if fc < 8:
                            ev = S.op("vector", lambda e, stb=stb, bank=bank, fc=fc: e.tensor_scalar(
                                stb[:, :], ps_qk[:, bank * 512:(bank + 1) * 512],
                                cvec_s[:, :], cbq_s[:, fc:fc + 1],
                                AluOpType.mult, AluOpType.add),
                                waits=[("pe", lastm), ("v", cq), ("a", ce),
                                       qk_st_war[bank]], inc="v")
                            sto = S.dma("gpsimd", lambda e, stb=stb, fc=fc, tc=tc: e.dma_start(
                                out=q_sc[fc, :, tc * 512:(tc + 1) * 512], in_=stb[:, :]),
                                waits=[("v", ev)])
                        else:
                            ev = S.op("vector", lambda e, stb=stb, bank=bank, fc=fc: e.tensor_scalar(
                                stb[:, :], ps_qk[:, bank * 512:(bank + 1) * 512],
                                bqk_s[:, fc:fc + 1], None, AluOpType.add),
                                waits=[("pe", lastm), qk_st_war[bank]], inc="v")
                            sto = S.dma("gpsimd", lambda e, stb=stb, fc=fc, tc=tc: e.dma_start(
                                out=k_sc[fc - 8, :, tc * 512:(tc + 1) * 512], in_=stb[:, :]),
                                waits=[("v", ev)])
                        qk_st_war[bank] = sto
                        store_of_fc.setdefault(fc, []).append(sto)
                        store_of_fc[fc] = store_of_fc[fc][-2:]
                        qk_rel[bank] = ("v", ev)
                v_rel = [0, 0, 0, 0]
                for tt in range(NT):
                    for half in range(2):
                        bank = (tt * 2 + half) % 4
                        wts0 = [("a", v_rel[bank]), ("v", cv_bv)]
                        for dc in range(DC):
                            wts0dc = [("v", wat_cv_of[dc])] + (wts0 if dc == 0 else [])
                            S.op("tensor", lambda e, tt=tt, half=half, dc=dc, bank=bank: e.matmul(
                                ps_v[:, bank * 512:(bank + 1) * 512],
                                xnT[:, dc * T + tt * 128: dc * T + (tt + 1) * 128],
                                wattn_b[:, dc * 3072 + 2048 + half * 512: dc * 3072 + 2048 + (half + 1) * 512],
                                start=(dc == 0), stop=False),
                                waits=wts0dc)
                        lastm = S.op("tensor", lambda e, half=half, bank=bank: e.matmul(
                            ps_v[:, bank * 512:(bank + 1) * 512],
                            onesb_s[0:1, :], bv_b[0:1, half * 512:(half + 1) * 512],
                            start=False, stop=True), inc="pe")
                        ev = S.op("scalar", lambda e, tt=tt, half=half, bank=bank: e.activation(
                            Vt[:, tt * C + half * 512: tt * C + (half + 1) * 512],
                            ps_v[:, bank * 512:(bank + 1) * 512], AF.Copy),
                            waits=[("pe", lastm)], inc="a")
                        v_rel[bank] = ev
                qk_st_fin = [("dg0", S.c("dg0")), ("dg1", S.c("dg1"))]
                p3_a, p3_pe, p3_v = S.c("a"), S.c("pe"), S.c("v")
                S.flush()

        with contextlib.ExitStack() as s4:
            E4 = s4.enter_context
            ytile = E4(nc.sbuf_tensor("ytile", [128, NT * C], BF16, side="right"))
            s4b = contextlib.ExitStack()
            E4b = s4b.enter_context
            snegp = [E4b(nc.sbuf_tensor(f"snegp{p}", [128, 2 * T], F32, side="right")) for p in range(3)]
            sneg = [[snegp[p][:, j * T:(j + 1) * T] for j in range(2)] for p in range(3)]
            pbuf = [[E4b(nc.sbuf_tensor(f"pbuf{p}_{j}", [128, T], BF16, side="right")) for j in range(2)] for p in range(2)]
            ptb2 = [E4b(nc.sbuf_tensor(f"ptb2_{i}", [128, 1024], BF16, side="right")) for i in range(8)]
            ncmx = E4b(nc.sbuf_tensor("ncmx", [128, 24], F32, side="right"))
            negmax = E4b(nc.sbuf_tensor("negmax", [128, 6], F32, side="right"))
            sumr = E4b(nc.sbuf_tensor("sumr", [128, 6], F32, side="right"))
            rsum = E4b(nc.sbuf_tensor("rsum", [128, 6], F32, side="right"))
            ps_s = E4(nc.psum_tensor("ps_s", [128, 2048], F32))
            ps_tr = E4(nc.psum_tensor("ps_tr", [128, 1024], F32))
            ps_y = E4(nc.psum_tensor("ps_y", [128, 1024], F32))

            iters = [(hp, qt) for hp in range(H // 2) for qt in range(NT)]
            NIT = len(iters)
            sbank_rel = [0, 0]
            gchunk = [0]
            trslot = [("v", 0), ("v", 0)]
            ptb_war = [0] * 16
            exp_done = [0] * NIT
            mm_done = {}
            sneg_rel = [0, 0, 0]
            pbuf_war = [[0, 0], [0, 0]]
            ysc_done = {}
            av_of = [0] * NIT
            recip_done = [0] * NIT
            abc_war = [0, 0]
            qk_war = [0, 0]
            group_ld = [None, None]
            out_war = [None]

            def chunks_of(L):
                out = []
                c0 = 0
                while c0 < L:
                    w = min(512, L - c0)
                    out.append((c0, w))
                    c0 += w
                return out

            def emit_group_load(hp, early=False):
                hb = hp % 2
                if early:
                    wts = [wat_fin, qk_st_war[2], qk_st_war[3]]
                    wq = [wat_fin] + store_of_fc[0] + store_of_fc[8]
                else:
                    wts = [("v", abc_war[0])]
                    if hp == 0:
                        wts += [("pe", p3_pe), ("v", p3_v), ("a", p3_a)] + qk_st_fin
                    wq = [("pe", qk_war[hb])] if qk_war[hb] else []
                l1 = S.dma("sync", lambda e, hp=hp: e.dma_start(
                    out=abcp0[:, 0:T], in_=alibi_bc[2 * hp, :, :]), waits=wts)
                l2 = S.dma("sync", lambda e, hp=hp: e.dma_start(
                    out=abcp0[:, T:2 * T], in_=alibi_bc[2 * hp + 1, :, :]))
                l3 = S.dma("sync", lambda e, hp=hp, hb=hb: e.dma_start(
                    out=q_hp[hb][:, :], in_=q_sc[hp, :, :]), waits=wq)
                l4 = S.dma("sync", lambda e, hp=hp, hb=hb: e.dma_start(
                    out=k_hp[hb][:, :], in_=k_sc[hp, :, :]))
                group_ld[hb] = [l1, l2, l3, l4]

            def emit_scores_softmax(it):
                hp, qt = iters[it]
                L = (qt + 1) * 128
                par = it % 3
                ch = chunks_of(L)
                nch = len(ch)
                cd = 0
                pend_red = []

                def emit_red():
                    nonlocal cd
                    if not pend_red:
                        return
                    ci, c0, w, dep = pend_red.pop(0)
                    cd = S.op("vector", lambda e, par=par, c0=c0, w=w, ci=ci: e.tensor_reduce(
                        ncmx[:, par * 8:par * 8 + 8].rearrange("p (j c) -> p j c", j=2)[:, :, ci:ci + 1],
                        snegp[par].rearrange("p (j t) -> p j t", j=2)[:, :, c0:c0 + w],
                        AX.X, AluOpType.max),
                        waits=[("v", dep)], inc="v")

                for ci, (c0, w) in enumerate(ch):
                    cs = gchunk[0] % 2
                    gchunk[0] += 1
                    rA = cs * 1024
                    rB = cs * 1024 + 512
                    wts = []
                    if it == 0 and ci == 0:
                        wts += [("a", p3_a), ("v", p3_v)]
                    if sbank_rel[cs]:
                        wts.append(("v", sbank_rel[cs]))
                    hb = hp % 2
                    if qt == 0 and ci == 0:
                        wts += group_ld[hb][2:4]
                    S.op("tensor", lambda e, hb=hb, qt=qt, c0=c0, w=w, rA=rA: e.matmul(
                        ps_s[:, rA: rA + w],
                        q_hp[hb][0:64, qt * 128:(qt + 1) * 128],
                        k_hp[hb][0:64, c0:c0 + w],
                        start=True, stop=True, tile_position=(0, 0)),
                        waits=wts)
                    md = S.op("tensor", lambda e, hb=hb, qt=qt, c0=c0, w=w, rB=rB: e.matmul(
                        ps_s[:, rB: rB + w],
                        q_hp[hb][64:128, qt * 128:(qt + 1) * 128],
                        k_hp[hb][64:128, c0:c0 + w],
                        start=True, stop=True, tile_position=(64, 0)), inc="pe")
                    mm_done[(it, ci)] = md
                    qk_war[hb] = md
                    wts2 = [("pe", md)]
                    if qt == 0 and ci == 0:
                        wts2 += group_ld[hp % 2][0:2]
                    if sneg_rel[par]:
                        wts2.append(("a", sneg_rel[par]))
                    aoff = T - 128 + c0 - qt * 128
                    stt = S.op("vector", lambda e, par=par, hb=hp % 2, aoff=aoff, c0=c0, w=w, rA=rA: e.scalar_tensor_tensor(
                        snegp[par].rearrange("p (j t) -> p j t", j=2)[:, :, c0:c0 + w],
                        ps_s[:, rA:rA + 1024].rearrange("p (j t) -> p j t", j=2)[:, :, 0:w],
                        1.0,
                        abcp[hb].rearrange("p (j t) -> p j t", j=2)[:, :, aoff:aoff + w],
                        AluOpType.mult, AluOpType.add),
                        waits=wts2, inc="v")
                    abc_war[0] = S.c("v")
                    sbank_rel[cs] = stt
                    pend_red.append((ci, c0, w, stt))
                    if ci == nch - 1:
                        while pend_red:
                            emit_red()
                    elif len(pend_red) >= 2:
                        emit_red()
                nms = []
                for j in range(2):
                    nms.append(S.op("vector", lambda e, par=par, j=j, nch=nch: e.tensor_reduce(
                        negmax[:, par * 2 + j: par * 2 + j + 1],
                        ncmx[:, par * 8 + j * 4: par * 8 + j * 4 + nch], AX.X,
                        AluOpType.max, negate=True),
                        waits=[("v", cd), ("a", exp_done[it - 3] if it >= 3 else 0)], inc="v"))
                return nms

            def emit_exp(it, nms):
                hp, qt = iters[it]
                L = (qt + 1) * 128
                par = it % 3
                pb = it % 2
                ed = 0
                for j in range(2):
                    wts = [("v", nms[j])]
                    if it >= 2:
                        wts.append(("pe", pbuf_war[pb][j]))
                    if it >= 3:
                        wts.append(("v", recip_done[it - 3]))
                    ed = S.op("scalar", lambda e, par=par, pb=pb, j=j, L=L: e.activation(
                        pbuf[pb][j][:, 0:L], sneg[par][j][:, 0:L], AF.Exp,
                        bias=negmax[:, par * 2 + j: par * 2 + j + 1],
                        accum_out=sumr[:, par * 2 + j: par * 2 + j + 1]),
                        waits=wts, inc="a")
                exp_done[it] = ed
                sneg_rel[par] = ed

            def emit_recip(it):
                par = it % 3
                r = 0
                for j in range(2):
                    r = S.op("vector", lambda e, par=par, j=j: e.reciprocal(
                        rsum[:, par * 2 + j: par * 2 + j + 1],
                        sumr[:, par * 2 + j: par * 2 + j + 1]),
                        waits=[("a", exp_done[it]), ("a", ysc_done.get(it - 3, 0))], inc="v")
                recip_done[it] = r

            tr_evs = {}
            evac_cnt = [0]

            def emit_tr(it):
                hp, qt = iters[it]
                par = it % 2
                nkc = qt + 1
                nbat = (nkc + 3) // 4
                evs = []
                pend = []
                gb = [0]

                def flush_evac():
                    if not pend:
                        return
                    use_v = False
                    evac_cnt[0] += 1
                    eng, tag = ("vector", "v") if use_v else ("scalar", "a")

                    def copy_op(e, dst, src):
                        if use_v:
                            return e.tensor_copy(dst, src)
                        return e.activation(dst, src, AF.Copy)

                    if False and len(pend) == 2 and pend[0][2] == 0 and pend[1][2] == 1 and \
                       pend[0][4] % 2 == 0 and pend[0][4] + 1 == pend[1][4]:
                        (j0, k0, s0, kcs0, p0, lt0), (j1, k1, s1, kcs1, p1, lt1) = pend
                        wb1 = len(kcs1) * 128
                        ev = S.op(eng, lambda e, p0=p0, wb1=wb1: copy_op(
                            e, ptb2[p0 // 2][:, 0:512 + wb1], ps_tr[:, 0:512 + wb1]),
                            waits=[("pe", lt1)], inc=tag)
                        trslot[0] = (tag, ev)
                        trslot[1] = (tag, ev)
                        evs.append((j0, k0, (tag, ev), kcs0, p0))
                        evs.append((j1, k1, (tag, ev), kcs1, p1))
                    else:
                        for (j, kcb, slot, kcs, pslot, lt) in pend:
                            wb = len(kcs) * 128
                            ev = S.op(eng, lambda e, pslot=pslot, slot=slot, wb=wb: copy_op(
                                e, ptb2[pslot // 2][:, (pslot % 2) * 512: (pslot % 2) * 512 + wb],
                                ps_tr[:, slot * 512: slot * 512 + wb]),
                                waits=[("pe", lt)], inc=tag)
                            trslot[slot] = (tag, ev)
                            evs.append((j, kcb, (tag, ev), kcs, pslot))
                    pend.clear()

                for j in range(2):
                    lasttr = 0
                    for kcb in range(nbat):
                        slot = gb[0] % 2
                        gb[0] += 1
                        pslot = par * 8 + j * 4 + kcb
                        kcs = list(range(kcb * 4, min(kcb * 4 + 4, nkc)))
                        for idx, kc in enumerate(kcs):
                            wts = []
                            if idx == 0:
                                wts = [("a", exp_done[it]), trslot[slot],
                                       ("pe", ptb_war[pslot])]
                            lasttr = S.op("tensor", lambda e, par=par, j=j, kc=kc, slot=slot, idx=idx, n=len(kcs): e.matmul(
                                ps_tr[:, slot * 512 + (kc % 4) * 128: slot * 512 + (kc % 4) * 128 + 128],
                                pbuf[par][j][:, kc * 128:(kc + 1) * 128], ident_s[:, :],
                                start=(idx == 0), stop=(idx == n - 1)),
                                waits=wts, inc="pe" if idx == len(kcs) - 1 else None)
                        pend.append((j, kcb, slot, kcs, pslot, lasttr))
                        if slot == 1:
                            flush_evac()
                    pbuf_war[par][j] = lasttr
                flush_evac()
                tr_evs[it] = evs

            def emit_av(it):
                hp, qt = iters[it]
                ypar = it % 2
                nkc = qt + 1
                evs = tr_evs.pop(it)
                for (j, kcb, ev, kcs, pslot) in evs:
                    h = 2 * hp + j
                    for idx, kc in enumerate(kcs):
                        wts = [ev] if idx == 0 else []
                        if j == 0 and kc == 0:
                            wts = list(wts) + [("a", ysc_done.get(it - 2, 0))]
                        first = (j == 0 and kc == 0)
                        last = (j == 1 and kc == nkc - 1)
                        r = S.op("tensor", lambda e, pslot=pslot, ypar=ypar, j=j, kc=kc, h=h, first=first, last=last: e.matmul(
                            ps_y[:, ypar * 512 + j * 64: ypar * 512 + j * 64 + 64],
                            ptb2[pslot // 2][:, (pslot % 2) * 512 + (kc % 4) * 128: (pslot % 2) * 512 + (kc % 4) * 128 + 128],
                            Vt[:, kc * C + h * 64: kc * C + h * 64 + 64],
                            start=first, stop=last),
                            waits=wts, inc="pe" if last else None)
                        if last:
                            av_of[it] = r
                    ptb_war[pslot] = S.c("pe")

            def emit_ysc(it):
                hp, qt = iters[it]
                par = it % 3
                ypar = it % 2
                y = 0
                for j in range(2):
                    h = 2 * hp + j
                    wts_y = [("pe", av_of[it]), ("v", recip_done[it])]
                    y = S.op("scalar", lambda e, par=par, ypar=ypar, j=j, qt=qt, h=h: e.activation(
                        ytile[:, qt * C + h * 64: qt * C + (h + 1) * 64],
                        ps_y[:, ypar * 512 + j * 64: ypar * 512 + j * 64 + 64],
                        AF.Copy, scale=rsum[:, par * 2 + j: par * 2 + j + 1]),
                        waits=wts_y, inc="a")
                ysc_done[it] = y

            wpj_b = E4b(nc.sbuf_tensor("wpj_b", [128, DC * C], BF16, side="right"))
            yT_sb = E4b(nc.sbuf_tensor("yT_sb", [128, C], BF16, side="right"))
            xfst = E4b(nc.sbuf_tensor("xfst", [128, C], F32, side="right"))
            outsb = E4b(nc.sbuf_tensor("outsb", [128, C], F32, side="right"))
            q_hp = [E4b(nc.sbuf_tensor(f"q_hp{i}", [128, T], BF16, side="right")) for i in range(2)]
            k_hp = [E4b(nc.sbuf_tensor(f"k_hp{i}", [128, T], BF16, side="right")) for i in range(2)]
            abcp0 = E4b(nc.sbuf_tensor("abcp0", [128, 2 * T], F32, side="right"))
            abcp = [abcp0, abcp0]
            wat_fin = ("v", wat_cv)
            emit_group_load(0, early=(T >= 2048))
            pj_rel = [0]
            yT_war = [0]
            xf_war = [0]
            for cc in range(DC):
                S.dma("sync", lambda e, cc=cc: e.dma_start(
                    out=wpj_b[:, cc * 1024:(cc + 1) * 1024], in_=wpj_sc[cc, :, :]),
                    waits=conv_stores_fin + (qk_st_fin + [("pe", p3_pe), ("v", p3_v), ("a", p3_a)]
                                             if cc == 0 else []))
            wpj_ld = [("dl0", S.c("dl0")), ("dl1", S.c("dl1"))]

            def emit_proj_unit(qt, ydep):
                for b2 in range(2):
                    slot = b2
                    last = 0
                    for j in range(4):
                        cc = b2 * 4 + j
                        wts = []
                        if j == 0:
                            wts = [("a", ydep), trslot[slot]]
                        last = S.op("tensor", lambda e, qt=qt, cc=cc, slot=slot, j=j: e.matmul(
                            ps_tr[:, slot * 512 + j * 128: slot * 512 + (j + 1) * 128],
                            ytile[:, qt * C + cc * 128: qt * C + (cc + 1) * 128],
                            ident_s[:, :], start=(j == 0), stop=(j == 3)),
                            waits=wts, inc="pe" if j == 3 else None)
                    ev = S.op("vector", lambda e, b2=b2, slot=slot: e.tensor_copy(
                        yT_sb[:, b2 * 512:(b2 + 1) * 512],
                        ps_tr[:, slot * 512:(slot + 1) * 512]),
                        waits=[("pe", last), ("pe", yT_war[0])], inc="v")
                    trslot[slot] = ("v", ev)
                yT_done = S.c("v")
                xl = S.dma("sync", lambda e, qt=qt: e.dma_start(
                    out=xfst[:, :], in_=xf_sc[qt * 128:(qt + 1) * 128, :]),
                    waits=p2_xf_st + [("v", xf_war[0])])
                lastp = 0
                for cc in range(DC):
                    for half in range(2):
                        slot = half
                        wts = []
                        if cc == 0:
                            wts = wpj_ld + [("v", yT_done), trslot[slot],
                                            ("v", pj_rel[0]), trslot[1 - slot]]
                        lastp = S.op("tensor", lambda e, slot=slot, cc=cc, half=half: e.matmul(
                            ps_tr[:, slot * 512:(slot + 1) * 512],
                            yT_sb[:, cc * 128:(cc + 1) * 128],
                            wpj_b[:, cc * 1024 + half * 512: cc * 1024 + half * 512 + 512],
                            start=(cc == 0), stop=False),
                            waits=wts, inc="pe" if half == 1 else None)
                oas = []
                for half in range(2):
                    slot = half
                    lastb = S.op("tensor", lambda e, slot=slot, half=half: e.matmul(
                        ps_tr[:, slot * 512:(slot + 1) * 512], onesb_s[0:1, :],
                        bpj_b[0:1, half * 512:(half + 1) * 512],
                        start=False, stop=True),
                        waits=[("v", cv_bp)], inc="pe")
                    oa = S.op("vector", lambda e, slot=slot, half=half: e.tensor_tensor(
                        outsb[:, half * 512:(half + 1) * 512],
                        ps_tr[:, slot * 512:(slot + 1) * 512],
                        xfst[:, half * 512:(half + 1) * 512], AluOpType.add),
                        waits=[("pe", lastb), xl]
                              + ([out_war[0]] if out_war[0] else []), inc="v")
                    pj_rel[0] = oa
                    trslot[slot] = ("v", oa)
                yT_war[0] = S.c("pe")
                xf_war[0] = pj_rel[0]
                out_war[0] = S.dma("gpsimd", lambda e, qt=qt: e.dma_start(
                    out=out_d[qt * 128:(qt + 1) * 128, :], in_=outsb[:, :]),
                    waits=[("v", pj_rel[0])])

            nm_of = {}
            NHP = H // 2
            for step in range(NIT + 3):
                if step < NIT:
                    hp, qt = iters[step]
                    if qt == 0 and hp > 0:
                        emit_group_load(hp)
                    nm_of[step] = emit_scores_softmax(step)
                if 3 <= step <= NIT + 2:
                    emit_ysc(step - 3)
                if 1 <= step <= NIT:
                    emit_recip(step - 1)
                    emit_tr(step - 1)
                if 2 <= step <= NIT + 1:
                    emit_av(step - 2)
                if step < NIT:
                    emit_exp(step, nm_of[step])
                if 3 <= step <= NIT + 2:
                    it3 = step - 3
                    hp3, qt3 = iters[it3]
                    if hp3 == NHP - 1:
                        emit_proj_unit(qt3, ysc_done[it3])
            S.flush()
            s4b.close()

            fin = [("dg0", S.c("dg0")), ("dg1", S.c("dg1"))]
            S.op("gpsimd", lambda e: None, waits=fin)
            S.op("sync", lambda e: None, waits=fin)
            S.flush()

    return nc



def _consts(T):
    start = 2.0 ** (-(2.0 ** -(math.log2(H) - 3)))
    slopes = np.array([start * start ** i for i in range(H)], dtype=np.float32)
    d_rel = (np.arange(T, dtype=np.float32)[None, :] + np.float32(128 - T)
             - np.arange(128, dtype=np.float32)[:, None])
    alibi_bc = np.ascontiguousarray(slopes[:, None, None] * d_rel[None, :, :])
    alibi_bc[:, d_rel > 0] = np.float32(-1e38)
    return {
        "ident": np.eye(128, dtype=BF),
        "alibi_bc": alibi_bc,
        "ones_b": np.ones((1, 128), dtype=BF),
        "ones_f": np.ones((1, 128), dtype=np.float32),
    }


def make_inputs(xb, W_attn, b_attn, W_proj, b_proj, temperature, wg, wu, wd, T):
    con = _consts(T)
    wgr = np.ascontiguousarray(
        wg.reshape(HCP, 2, 128, DC, 128).transpose(0, 4, 1, 3, 2).reshape(HCP, 128, 2048))
    wur = np.ascontiguousarray(
        wu.reshape(HCP, 2, 128, DC, 128).transpose(0, 4, 1, 3, 2).reshape(HCP, 128, 2048))
    wdr = np.ascontiguousarray(
        wd.T.reshape(HCP, 2, 128, C).transpose(0, 2, 1, 3).reshape(HCP, 128, 2048))
    d = {
        "xin": np.ascontiguousarray(xb),
        "wattn": np.ascontiguousarray(W_attn),
        "battn_qk": np.ascontiguousarray(b_attn[:2048].reshape(16, 128).T),
        "battn_v": np.ascontiguousarray(b_attn[2048:].reshape(1, C)),
        "wproj": np.ascontiguousarray(W_proj),
        "bproj": np.ascontiguousarray(b_proj.reshape(1, C)),
        "temp": np.ascontiguousarray(temperature.reshape(1, 1)),
        "wg_r": wgr,
        "wu_r": wur,
        "wd_r": wdr,
    }
    d.update(con)
    return d


def kernel(L0_attn_x, L0_attn_W_attn, L0_attn_b_attn, L0_attn_W_proj,
           L0_attn_b_proj, L0_attn_temperature, L0_ffn_w_gate, L0_ffn_w_up,
           L0_ffn_w_down):
    T = int(L0_attn_x.shape[1])
    x = np.ascontiguousarray(np.asarray(L0_attn_x, dtype=np.float32))
    W_attn = np.asarray(L0_attn_W_attn, dtype=np.float32)
    b_attn = np.asarray(L0_attn_b_attn, dtype=np.float32)
    W_proj = np.asarray(L0_attn_W_proj, dtype=np.float32)
    b_proj = np.asarray(L0_attn_b_proj, dtype=np.float32)
    tempv = np.asarray(L0_attn_temperature, dtype=np.float32)
    wg = np.asarray(L0_ffn_w_gate, dtype=np.int32)
    wu = np.asarray(L0_ffn_w_up, dtype=np.int32)
    wd = np.asarray(L0_ffn_w_down, dtype=np.int32)

    nc = build_program(T)
    in_maps = [make_inputs(x[b], W_attn, b_attn, W_proj, b_proj, tempv, wg, wu, wd, T)
               for b in range(x.shape[0])]
    res = run_bass_kernel_spmd(nc, in_maps, core_ids=list(range(x.shape[0])))
    return np.stack([r["out"] for r in res.results], axis=0)


# version 34
# speedup vs baseline: 1.0022x; 1.0022x over previous
import contextlib
import math
import numpy as np
import ml_dtypes
import concourse.bass as bass
import concourse.mybir as mybir
from concourse.bass_utils import run_bass_kernel_spmd
from concourse.alu_op_type import AluOpType

F32 = mybir.dt.float32
BF16 = mybir.dt.bfloat16
I32 = mybir.dt.int32
AF = mybir.ActivationFunctionType
AX = mybir.AxisListType
BF = ml_dtypes.bfloat16

NCORES = 8
C = 1024
H = 16
D = 64
HID = 3072
DC = C // 128
HCN = HID // 128
HCP = HCN // 2

ENGS = ("sync", "gpsimd", "vector", "scalar", "tensor")


class KB:

    QSEMS = {"sync": ("dl0", "dl1"), "gpsimd": ("dg0", "dg1"),
             "scalar": ("ds0", "ds1")}

    def __init__(self, nc, block, sem):
        self.nc = nc
        self.block = block
        self.sem = sem
        self.cnt = {k: 0 for k in sem}
        self.pend = []
        self.waited = {e: {} for e in ENGS}
        self.qidx = {"sync": 0, "gpsimd": 0, "scalar": 0}

    def dma(self, queue, fn, waits=()):
        sems = self.QSEMS[queue]
        i = self.qidx[queue]
        self.qidx[queue] ^= 1
        s = sems[i]
        auto = [(s, self.cnt[s])]
        cnt = self.op(queue, fn, waits=list(waits) + auto, inc=s, inc_n=16)
        return (s, cnt)

    def op(self, eng, fn, waits=(), inc=None, inc_n=1):
        waits = [(s, v) for (s, v) in waits if v and v > 0]
        incs = []
        if inc is not None:
            self.cnt[inc] += inc_n
            incs.append((inc, inc_n))
        self.pend.append((eng, fn, waits, incs))
        return self.cnt[inc] if inc is not None else 0

    def c(self, s):
        return self.cnt[s]

    def flush(self):
        for eng in ENGS:
            ops = [(fn, w, i) for (e, fn, w, i) in self.pend if e == eng]
            if not ops:
                continue
            waited = self.waited[eng]
            sem = self.sem

            def section(esec, ops=ops, waited=waited):
                for fn, waits, incs in ops:
                    for s, v in waits:
                        if v > waited.get(s, 0):
                            esec.wait_ge(sem[s], v)
                            waited[s] = v
                    instr = fn(esec)
                    for s, n in incs:
                        instr.then_inc(sem[s], n)

            getattr(self.block, eng)(section)
        self.pend = []


def build_program(T):
    NT = T // 128
    NTB = T // 512
    nc = bass.Bass("TRN2", target_bir_lowering=False)

    def din(name, shape, dt):
        return nc.dram_tensor(name, shape, dt, kind="ExternalInput").ap()

    xin = din("xin", [T, C], F32)
    wattn = din("wattn", [C, 3 * C], F32)
    battn_qk = din("battn_qk", [128, 16], F32)
    battn_v = din("battn_v", [1, C], F32)
    wproj = din("wproj", [C, C], F32)
    bproj = din("bproj", [1, C], F32)
    temp = din("temp", [1, 1], F32)
    wg_r = din("wg_r", [HCP, 128, 2048], I32)
    wu_r = din("wu_r", [HCP, 128, 2048], I32)
    wd_r = din("wd_r", [HCP, 128, 2048], I32)
    ident = din("ident", [128, 128], BF16)
    alibi_bc = din("alibi_bc", [H, 128, T], F32)
    ones_b = din("ones_b", [1, 128], BF16)
    ones_f = din("ones_f", [1, 128], F32)

    out_d = nc.dram_tensor("out", [T, C], F32, kind="ExternalOutput").ap()
    wg_sc = nc.dram_tensor("wg_sc", [HCP, 128, 2048], BF16, kind="Internal").ap()
    wu_sc = nc.dram_tensor("wu_sc", [HCP, 128, 2048], BF16, kind="Internal").ap()
    wd_sc = nc.dram_tensor("wd_sc", [HCP, 128, 2048], BF16, kind="Internal").ap()
    xf_sc = nc.dram_tensor("xf_sc", [T, C], F32, kind="Internal").ap()
    q_sc = nc.dram_tensor("q_sc", [DC, 128, T], BF16, kind="Internal").ap()
    wpj_sc = nc.dram_tensor("wpj_sc", [DC, 128, C], BF16, kind="Internal").ap()
    k_sc = nc.dram_tensor("k_sc", [DC, 128, T], BF16, kind="Internal").ap()

    with contextlib.ExitStack() as st:
        E = st.enter_context
        sem = {s: E(nc.semaphore("sem_" + s)) for s in
               ("dl0", "dl1", "dg0", "dg1", "ds0", "ds1", "pe", "v", "a")}
        block = E(nc.Block())
        S = KB(nc, block, sem)

        ident_s = E(nc.sbuf_tensor("ident_s", [128, 128], BF16))
        onesb_s = E(nc.sbuf_tensor("onesb_s", [1, 128], BF16))
        onesf_s = E(nc.sbuf_tensor("onesf_s", [1, 128], F32))
        bqk_s = E(nc.sbuf_tensor("bqk_s", [128, 16], F32))
        bv_b = E(nc.sbuf_tensor("bv_b", [1, C], BF16))
        bpj_b = E(nc.sbuf_tensor("bpj_b", [1, C], BF16))
        temp_s = E(nc.sbuf_tensor("temp_s", [1, 1], F32))
        t8_s = E(nc.sbuf_tensor("t8_s", [1, 1], F32))
        c11_s = E(nc.sbuf_tensor("c11_s", [1, 1], F32))
        cvec_s = E(nc.sbuf_tensor("cvec_s", [128, 1], F32))
        cbq_s = E(nc.sbuf_tensor("cbq_s", [128, 8], F32))
        Vt = E(nc.sbuf_tensor("Vt", [128, NT * C], BF16))

        cl = []
        for dst, src in [(ident_s, ident),
                         (onesb_s, ones_b), (onesf_s, ones_f),
                         (bqk_s, battn_qk), (temp_s, temp)]:
            cl.append(S.dma("sync", lambda e, d=dst, s=src: e.dma_start(out=d[:, :], in_=s)))
        c_consts = [cl[-1], cl[-2]]
        cv_t8 = S.op("vector", lambda e: e.tensor_scalar(
            t8_s[0:1, :], temp_s[0:1, :], 8.0, None, AluOpType.mult),
            waits=c_consts, inc="v")
        cv_c11 = S.op("vector", lambda e: e.reciprocal(c11_s[0:1, :], t8_s[0:1, :]),
                      waits=[("v", cv_t8)], inc="v")

        with contextlib.ExitStack() as sA:
            xnT = sA.enter_context(nc.sbuf_tensor("xnT", [128, DC * T], BF16, side="right"))
            wdres = sA.enter_context(nc.sbuf_tensor("wdres", [128, HCP * 2048], BF16, side="right"))

            with contextlib.ExitStack() as sB:
                E1 = sB.enter_context
                xstage = [E1(nc.sbuf_tensor(f"xstage{i}", [128, C], F32, side="right")) for i in range(4)]
                xn_st = [E1(nc.sbuf_tensor(f"xn_st{i}", [128, C], BF16, side="right")) for i in range(4)]
                sq_sc = [E1(nc.sbuf_tensor(f"sq_sc{i}", [128, C], F32, side="right")) for i in range(2)]
                wistage = [E1(nc.sbuf_tensor(f"wistage{i}", [128, 2048], I32, side="right")) for i in range(3)]
                cbuf = [E1(nc.sbuf_tensor(f"cbuf{i}", [128, 2048], BF16, side="right")) for i in range(2)]
                cbuf32 = [E1(nc.sbuf_tensor(f"cbuf32_{i}", [128, 2048], F32, side="right")) for i in range(2)]
                stat = {k: [E1(nc.sbuf_tensor(f"st_{k}{i}", [128, 1], F32, side="right")) for i in range(4)]
                        for k in ("s1", "s2", "m", "v", "r", "n")}
                bst_f = E1(nc.sbuf_tensor("bst_f", [1, C], F32, side="right"))
                wpst = E1(nc.sbuf_tensor("wpst", [128, C], F32, side="right"))
                ps1 = E1(nc.psum_tensor("ps1", [128, 4096], F32))

                bld1 = S.dma("sync", lambda e: e.dma_start(out=bst_f[0:1, 0:C], in_=battn_v))
                cv_bv = S.op("vector", lambda e: e.tensor_copy(bv_b[0:1, :], bst_f[0:1, 0:C]),
                             waits=[bld1], inc="v")
                bld2 = S.dma("sync", lambda e: e.dma_start(out=bst_f[0:1, 0:C], in_=bproj),
                             waits=[("v", cv_bv)])
                cv_bp = S.op("vector", lambda e: e.tensor_copy(bpj_b[0:1, :], bst_f[0:1, 0:C]),
                             waits=[bld2], inc="v")

                conv_tasks = []
                for m, src, dst_sc, width in (
                        [(0, wg_r, wg_sc, 2048), (1, wu_r, wu_sc, 2048),
                         (2, wd_r, None, 2048)]):
                    for j in range(HCP):
                        conv_tasks.append((src, dst_sc, j, width))
                for j in range(DC):
                    conv_tasks.append(("wproj", wpj_sc, j, 1024))
                n_tasks = len(conv_tasks)
                conv_i = [0]
                cbuf_store = [None, None, None]
                wist_conv = [0, 0, 0]
                c32_war = [0, 0, 0]
                wpst_war = [0]
                conv_stores = []

                def emit_conv(k):
                    src, dst_sc, j, width = conv_tasks[k]
                    bi = k % 3
                    b = k % 2
                    lq = "sync" if k % 2 == 0 else "scalar"
                    is_pj = (src == "wproj")
                    is_wd = dst_sc is None
                    if is_pj:
                        ld = S.dma("sync", lambda e, j=j: e.dma_start(
                            out=wpst[:, :], in_=wproj[j * 128:(j + 1) * 128, :]),
                            waits=[("a", wpst_war[0])])
                        cv = S.op("scalar", lambda e, b=b: e.activation(
                            cbuf[b][:, 0:1024], wpst[:, :], AF.Copy),
                            waits=[ld] + ([cbuf_store[b]] if cbuf_store[b] else []), inc="a")
                        wpst_war[0] = cv
                    else:
                        ld = S.dma(lq, lambda e, src=src, j=j, bi=bi, width=width: e.dma_start(
                            out=wistage[bi][:, 0:width], in_=src[j, :, :]),
                            waits=[("v", wist_conv[bi])])
                        cv0 = S.op("vector", lambda e, b=b, bi=bi, width=width: e.tensor_scalar(
                            cbuf32[b][:, 0:width], wistage[bi][:, 0:width], 1.0,
                            None, AluOpType.mult),
                            waits=[ld, ("a", c32_war[b])]
                                  + ([cbuf_store[b]] if cbuf_store[b] else []), inc="v")
                        cdst = (lambda e, j=j: wdres[:, j * 2048:(j + 1) * 2048]) if is_wd \
                            else (lambda e, b=b, width=width: cbuf[b][:, 0:width])
                        cv = S.op("scalar", lambda e, b=b, width=width, cdst=cdst: e.activation(
                            cdst(e), cbuf32[b][:, 0:width], AF.Copy, scale=1.0 / 127.0),
                            waits=[("v", cv0)], inc="a")
                        wist_conv[bi] = cv0
                        c32_war[b] = cv
                    if is_wd:
                        return
                    sto = S.dma("gpsimd", lambda e, dst_sc=dst_sc, j=j, b=b, width=width: e.dma_start(
                        out=dst_sc[j, :, :], in_=cbuf[b][:, 0:width]),
                        waits=[("a", cv)])
                    cbuf_store[b] = sto
                    conv_stores.append(sto)

                ln_xn = [0] * NT
                stile_tr = [0] * (NT // 4)
                tr_evac = [[(None, 0)] * DC for _ in range(NT // 4)]
                prev_xuse = [0, 0, 0, 0]
                stat_war = [0, 0, 0, 0]
                sq_war = [0, 0, 0, 0]
                for i in range(NT):
                    for _ in range(3):
                        if conv_i[0] < n_tasks:
                            emit_conv(conv_i[0])
                            conv_i[0] += 1
                    b = i % 4
                    p = i % 4
                    ld = S.dma("sync", lambda e, i=i, b=b: e.dma_start(
                        out=xstage[b][:, :], in_=xin[i * 128:(i + 1) * 128, :]),
                        waits=[("v", prev_xuse[b])])
                    rs = S.op("vector", lambda e, b=b, p=p: e.tensor_reduce(
                        stat["s1"][p][:, :], xstage[b][:, :], AX.X, AluOpType.add),
                        waits=[ld, ("v", stat_war[p])], inc="v")
                    sq = S.op("scalar", lambda e, b=b, p=p: e.activation(
                        sq_sc[p % 2][:, :], xstage[b][:, :], AF.Square,
                        accum_out=stat["s2"][p][:, :]),
                        waits=[ld, ("v", sq_war[p]), ("v", sq_war[(p + 2) % 4])], inc="a")
                    mn = S.op("vector", lambda e, p=p: e.tensor_scalar(
                        stat["m"][p][:, :], stat["s1"][p][:, :], 1.0 / C, None,
                        AluOpType.mult), waits=[("v", rs)], inc="v")
                    v1 = S.op("vector", lambda e, p=p: e.tensor_tensor(
                        stat["v"][p][:, :], stat["m"][p][:, :], stat["m"][p][:, :],
                        AluOpType.mult), waits=[("v", mn), ("a", sq)], inc="v")
                    v2 = S.op("vector", lambda e, p=p: e.tensor_scalar(
                        stat["s2"][p][:, :], stat["s2"][p][:, :], 1.0 / C, 1e-5,
                        AluOpType.mult, AluOpType.add), waits=[("a", sq)], inc="v")
                    v3 = S.op("vector", lambda e, p=p: e.tensor_tensor(
                        stat["v"][p][:, :], stat["s2"][p][:, :], stat["v"][p][:, :],
                        AluOpType.subtract), waits=[("v", v2)], inc="v")
                    sq_war[p] = v3
                    sd = S.op("scalar", lambda e, p=p: e.activation(
                        stat["v"][p][:, :], stat["v"][p][:, :], AF.Sqrt),
                        waits=[("v", v3)], inc="a")
                    rc = S.op("vector", lambda e, p=p: e.reciprocal(
                        stat["r"][p][:, :], stat["v"][p][:, :]),
                        waits=[("a", sd)], inc="v")
                    ng = S.op("vector", lambda e, p=p: e.tensor_scalar(
                        stat["n"][p][:, :], stat["m"][p][:, :], stat["r"][p][:, :],
                        -1.0, AluOpType.mult, AluOpType.mult),
                        waits=[("v", rc)], inc="v")
                    stile = i // 4
                    xn = S.op("vector", lambda e, i=i, b=b, p=p: e.tensor_scalar(
                        xn_st[i % 4][:, :], xstage[b][:, :], stat["r"][p][:, :],
                        stat["n"][p][:, :], AluOpType.mult, AluOpType.add),
                        waits=[("v", ng),
                               ("pe", stile_tr[stile - 1] if stile >= 1 else 0)],
                        inc="v")
                    prev_xuse[b] = xn
                    stat_war[p] = xn
                    ln_xn[i] = xn
                    if i % 4 == 3:
                        last = 0
                        for dc in range(DC):
                            for j in range(4):
                                wts = [("v", ln_xn[stile * 4 + j])]
                                if stile >= 1:
                                    es, ev = tr_evac[stile - 1][dc]
                                    wts.append((es, ev))
                                last = S.op("tensor", lambda e, dc=dc, j=j, stile=stile: e.matmul(
                                    ps1[:, dc * 512 + j * 128: dc * 512 + (j + 1) * 128],
                                    xn_st[(stile * 4 + j) % 4][:, dc * 128:(dc + 1) * 128],
                                    ident_s[:, :], start=(j == 0), stop=(j == 3)),
                                    waits=wts, inc="pe" if j == 3 else None)
                            if dc % 2 == 0:
                                ev = S.op("vector", lambda e, dc=dc, stile=stile: e.tensor_copy(
                                    xnT[:, dc * T + stile * 512: dc * T + (stile + 1) * 512],
                                    ps1[:, dc * 512:(dc + 1) * 512]),
                                    waits=[("pe", last)], inc="v")
                                tr_evac[stile][dc] = ("v", ev)
                            else:
                                ev = S.op("scalar", lambda e, dc=dc, stile=stile: e.activation(
                                    xnT[:, dc * T + stile * 512: dc * T + (stile + 1) * 512],
                                    ps1[:, dc * 512:(dc + 1) * 512], AF.Copy),
                                    waits=[("pe", last)], inc="a")
                                tr_evac[stile][dc] = ("a", ev)
                        stile_tr[stile] = last
                while conv_i[0] < n_tasks:
                    emit_conv(conv_i[0])
                    conv_i[0] += 1
                conv_stores_fin = [("dg0", S.c("dg0")), ("dg1", S.c("dg1"))]
                p1_v, p1_a, p1_pe = S.c("v"), S.c("a"), S.c("pe")
                S.flush()

            with contextlib.ExitStack() as sF:
                E2 = sF.enter_context
                wgbuf = [E2(nc.sbuf_tensor(f"wgbuf{i}", [128, 2048], BF16, side="right")) for i in range(2)]
                wubuf = [E2(nc.sbuf_tensor(f"wubuf{i}", [128, 2048], BF16, side="right")) for i in range(2)]
                hbuf = E2(nc.sbuf_tensor("hbuf", [128, HCN * 512], BF16, side="right"))
                gs_s = [E2(nc.sbuf_tensor(f"gs_s{i}", [128, 512], BF16, side="right")) for i in range(2)]
                us_s = [E2(nc.sbuf_tensor(f"us_s{i}", [128, 512], BF16, side="right")) for i in range(2)]
                x2st = [E2(nc.sbuf_tensor("x2st0", [128, C], F32, side="right"))] * 2
                xfout = [E2(nc.sbuf_tensor(f"xfout{i}", [128, C], F32, side="right")) for i in range(2)]
                ps_g = E2(nc.psum_tensor("ps_g", [128, 1024], F32))
                ps_u = E2(nc.psum_tensor("ps_u", [128, 1024], F32))
                ps_dn = E2(nc.psum_tensor("ps_dn", [128, 2048], F32))

                wgu_ld = {}
                wgld_war = [0, 0]
                grel = [p1_a, p1_a]
                urel = [p1_a, p1_a]
                gs_war = [0, 0]
                dnrel = [0, 0, 0, 0]
                hbuf_war = 0
                x2_war = [0, 0]
                xf_war = [None, None]
                for tb in range(NTB):
                    for hcp in range(HCP):
                        b = hcp % 2
                        wts = conv_stores_fin + [("pe", wgld_war[b])]
                        if tb == 0 and hcp == 0:
                            wts += [("pe", p1_pe), ("v", p1_v), ("a", p1_a)]
                        ldg = S.dma("sync", lambda e, hcp=hcp, b=b: e.dma_start(
                            out=wgbuf[b][:, :], in_=wg_sc[hcp, :, :]), waits=wts)
                        ldu = S.dma("sync", lambda e, hcp=hcp, b=b: e.dma_start(
                            out=wubuf[b][:, :], in_=wu_sc[hcp, :, :]))
                        wgu_ld[(tb, hcp)] = [ldg, ldu]
                        for hc2 in range(2):
                            hc = hcp * 2 + hc2
                            gp = hc % 2
                            wts_g = [ldg, ldu, ("a", grel[gp])]
                            if tb == 0 and hc == 0:
                                wts_g += [("v", p1_v), ("a", p1_a), ("pe", p1_pe)]
                            lg = lu = 0
                            for dc in range(DC):
                                lg = S.op("tensor", lambda e, hc2=hc2, dc=dc, gp=gp, b=b, tb=tb: e.matmul(
                                    ps_g[:, gp * 512:(gp + 1) * 512],
                                    wgbuf[b][:, hc2 * 1024 + dc * 128: hc2 * 1024 + (dc + 1) * 128],
                                    xnT[:, dc * T + tb * 512: dc * T + tb * 512 + 512],
                                    start=(dc == 0), stop=(dc == DC - 1)),
                                    waits=wts_g if dc == 0 else (),
                                    inc="pe" if dc == DC - 1 else None)
                            for dc in range(DC):
                                lu = S.op("tensor", lambda e, hc2=hc2, dc=dc, gp=gp, b=b, tb=tb: e.matmul(
                                    ps_u[:, gp * 512:(gp + 1) * 512],
                                    wubuf[b][:, hc2 * 1024 + dc * 128: hc2 * 1024 + (dc + 1) * 128],
                                    xnT[:, dc * T + tb * 512: dc * T + tb * 512 + 512],
                                    start=(dc == 0), stop=(dc == DC - 1)),
                                    waits=[("a", urel[gp])] if dc == 0 else (),
                                    inc="pe" if dc == DC - 1 else None)
                            wgld_war[b] = lu
                            sl = S.op("scalar", lambda e, gp=gp: e.activation(
                                gs_s[gp][:, :], ps_g[:, gp * 512:(gp + 1) * 512], AF.Silu),
                                waits=[("pe", lg), ("v", gs_war[gp])], inc="a")
                            grel[gp] = sl
                            ue = S.op("scalar", lambda e, gp=gp: e.activation(
                                us_s[gp][:, :], ps_u[:, gp * 512:(gp + 1) * 512], AF.Copy),
                                waits=[("pe", lu), ("v", gs_war[gp])], inc="a")
                            urel[gp] = ue
                            wts_m = [("a", ue)]
                            if tb > 0 and hc == 0:
                                wts_m.append(("pe", hbuf_war))
                            ml = S.op("vector", lambda e, hc=hc, gp=gp: e.tensor_tensor(
                                hbuf[:, hc * 512:(hc + 1) * 512], gs_s[gp][:, :],
                                us_s[gp][:, :], AluOpType.mult),
                                waits=wts_m, inc="v")
                            gs_war[gp] = ml
                    mults_done = S.c("v")
                    for pas in range(2):
                        lastd = [0] * 4
                        for hcp2 in range(HCN // 2):
                            for hc2 in range(2):
                                hc = 2 * hcp2 + hc2
                                for tsh in range(4):
                                    ts = pas * 2 + tsh // 2
                                    half = tsh % 2
                                    wts_d = []
                                    if hc == 0:
                                        wts_d += [("v", mults_done), ("v", dnrel[tsh])]
                                    r = S.op("tensor", lambda e, hc=hc, hcp2=hcp2, hc2=hc2, ts=ts, half=half, tsh=tsh: e.matmul(
                                        ps_dn[:, tsh * 512:(tsh + 1) * 512],
                                        hbuf[:, hc * 512 + ts * 128: hc * 512 + ts * 128 + 128],
                                        wdres[:, hcp2 * 2048 + hc2 * 1024 + half * 512: hcp2 * 2048 + hc2 * 1024 + (half + 1) * 512],
                                        start=(hc == 0), stop=(hc == HCN - 1)),
                                        waits=wts_d, inc="pe" if tsh == 3 else None)
                                    if tsh == 3:
                                        if hc == HCN - 1:
                                            lastd = [r] * 4
                        for tsh in range(4):
                            ts = pas * 2 + tsh // 2
                            half = tsh % 2
                            tt = tb * 4 + ts
                            xb = ts % 2
                            if half == 0:
                                wts_x2 = [("v", x2_war[xb])]
                                if tb == 0 and pas == 0 and tsh == 0:
                                    wts_x2 += [("pe", p1_pe), ("v", p1_v), ("a", p1_a)] + conv_stores_fin
                                x2_ld = S.dma("gpsimd", lambda e, tt=tt, xb=xb: e.dma_start(
                                    out=x2st[xb][:, :], in_=xin[tt * 128:(tt + 1) * 128, :]),
                                    waits=wts_x2)
                            xa = S.op("vector", lambda e, xb=xb, tsh=tsh, half=half: e.tensor_tensor(
                                xfout[xb][:, half * 512:(half + 1) * 512],
                                ps_dn[:, tsh * 512:(tsh + 1) * 512],
                                x2st[xb][:, half * 512:(half + 1) * 512], AluOpType.add),
                                waits=[("pe", lastd[tsh]), x2_ld]
                                      + ([xf_war[xb]] if xf_war[xb] else []), inc="v")
                            dnrel[tsh] = xa
                            if half == 1:
                                x2_war[xb] = xa
                                xf_war[xb] = S.dma("gpsimd", lambda e, tt=tt, xb=xb: e.dma_start(
                                    out=xf_sc[tt * 128:(tt + 1) * 128, :], in_=xfout[xb][:, :]),
                                    waits=[("v", xa)])
                    hbuf_war = S.c("pe")
                p2_v, p2_a, p2_pe = S.c("v"), S.c("a"), S.c("pe")
                p2_xf_st = [("dg0", S.c("dg0")), ("dg1", S.c("dg1"))]
                S.flush()

            with contextlib.ExitStack() as sQ:
                E3 = sQ.enter_context
                wattn_b = E3(nc.sbuf_tensor("wattn_b", [128, DC * 3 * C], BF16, side="right"))
                qkst = [E3(nc.sbuf_tensor(f"qkst{i}", [128, 512], BF16, side="right")) for i in range(4)]
                wst3 = [E3(nc.sbuf_tensor(f"wst3_{i}", [128, 3 * C], F32, side="right")) for i in range(2)]
                ps_qk = E3(nc.psum_tensor("ps_qk", [128, 2048], F32))
                ps_v = E3(nc.psum_tensor("ps_v", [128, 2048], F32))

                mc = S.op("tensor", lambda e: e.matmul(
                    ps_qk[:, 0:1], onesf_s[0:1, :], c11_s[0:1, :], start=True, stop=True),
                    waits=[("v", cv_c11), ("v", p2_v), ("a", p2_a), ("pe", p2_pe)]
                          + p2_xf_st,
                    inc="pe")
                ce = S.op("scalar", lambda e: e.activation(
                    cvec_s[:, :], ps_qk[:, 0:1], AF.Copy),
                    waits=[("pe", mc)], inc="a")
                cq = S.op("vector", lambda e: e.tensor_scalar(
                    cbq_s[:, :], bqk_s[:, 0:8], cvec_s[:, :], None, AluOpType.mult),
                    waits=[("a", ce)], inc="v")

                wat_cv = 0
                wat_cv_of = [0] * DC
                wst_war = [0, 0]
                for dc in range(DC):
                    wb = dc % 2
                    ld = S.dma("sync", lambda e, dc=dc, wb=wb: e.dma_start(
                        out=wst3[wb][:, :], in_=wattn[dc * 128:(dc + 1) * 128, :]),
                        waits=[("v", wst_war[wb])]
                              + ([("pe", p2_pe), ("v", p2_v), ("a", p2_a)] + p2_xf_st
                                 if dc < 2 else []))
                    wat_cv = S.op("vector", lambda e, dc=dc, wb=wb: e.tensor_copy(
                        wattn_b[:, dc * 3072:(dc + 1) * 3072], wst3[wb][:, :]),
                        waits=[ld], inc="v")
                    wat_cv_of[dc] = wat_cv
                    wst_war[wb] = wat_cv

                qk_rel = [("a", ce), ("v", 0), ("v", 0), ("v", 0)]
                qk_st_war = [("v", 0)] * 4
                store_of_fc = {}
                qk_st_fin = None
                for fc in range(16):
                    for tc in range(NTB):
                        bank = (fc * NTB + tc) % 4
                        lastm = 0
                        for dc in range(DC):
                            wts = [("v", wat_cv_of[dc])]
                            if dc == 0:
                                wts.append(qk_rel[bank])
                            lastm = S.op("tensor", lambda e, fc=fc, tc=tc, dc=dc, bank=bank: e.matmul(
                                ps_qk[:, bank * 512:(bank + 1) * 512],
                                wattn_b[:, dc * 3072 + fc * 128: dc * 3072 + (fc + 1) * 128],
                                xnT[:, dc * T + tc * 512: dc * T + (tc + 1) * 512],
                                start=(dc == 0), stop=(dc == DC - 1)),
                                waits=wts, inc="pe" if dc == DC - 1 else None)
                        stb = qkst[bank]
                        if fc < 8:
                            ev = S.op("vector", lambda e, stb=stb, bank=bank, fc=fc: e.tensor_scalar(
                                stb[:, :], ps_qk[:, bank * 512:(bank + 1) * 512],
                                cvec_s[:, :], cbq_s[:, fc:fc + 1],
                                AluOpType.mult, AluOpType.add),
                                waits=[("pe", lastm), ("v", cq), ("a", ce),
                                       qk_st_war[bank]], inc="v")
                            sto = S.dma("gpsimd", lambda e, stb=stb, fc=fc, tc=tc: e.dma_start(
                                out=q_sc[fc, :, tc * 512:(tc + 1) * 512], in_=stb[:, :]),
                                waits=[("v", ev)])
                        else:
                            ev = S.op("vector", lambda e, stb=stb, bank=bank, fc=fc: e.tensor_scalar(
                                stb[:, :], ps_qk[:, bank * 512:(bank + 1) * 512],
                                bqk_s[:, fc:fc + 1], None, AluOpType.add),
                                waits=[("pe", lastm), qk_st_war[bank]], inc="v")
                            sto = S.dma("gpsimd", lambda e, stb=stb, fc=fc, tc=tc: e.dma_start(
                                out=k_sc[fc - 8, :, tc * 512:(tc + 1) * 512], in_=stb[:, :]),
                                waits=[("v", ev)])
                        qk_st_war[bank] = sto
                        store_of_fc.setdefault(fc, []).append(sto)
                        store_of_fc[fc] = store_of_fc[fc][-2:]
                        qk_rel[bank] = ("v", ev)
                v_rel = [0, 0, 0, 0]
                for tt in range(NT):
                    for half in range(2):
                        bank = (tt * 2 + half) % 4
                        wts0 = [("a", v_rel[bank]), ("v", cv_bv)]
                        for dc in range(DC):
                            wts0dc = [("v", wat_cv_of[dc])] + (wts0 if dc == 0 else [])
                            S.op("tensor", lambda e, tt=tt, half=half, dc=dc, bank=bank: e.matmul(
                                ps_v[:, bank * 512:(bank + 1) * 512],
                                xnT[:, dc * T + tt * 128: dc * T + (tt + 1) * 128],
                                wattn_b[:, dc * 3072 + 2048 + half * 512: dc * 3072 + 2048 + (half + 1) * 512],
                                start=(dc == 0), stop=False),
                                waits=wts0dc)
                        lastm = S.op("tensor", lambda e, half=half, bank=bank: e.matmul(
                            ps_v[:, bank * 512:(bank + 1) * 512],
                            onesb_s[0:1, :], bv_b[0:1, half * 512:(half + 1) * 512],
                            start=False, stop=True), inc="pe")
                        ev = S.op("scalar", lambda e, tt=tt, half=half, bank=bank: e.activation(
                            Vt[:, tt * C + half * 512: tt * C + (half + 1) * 512],
                            ps_v[:, bank * 512:(bank + 1) * 512], AF.Copy),
                            waits=[("pe", lastm)], inc="a")
                        v_rel[bank] = ev
                qk_st_fin = [("dg0", S.c("dg0")), ("dg1", S.c("dg1"))]
                p3_a, p3_pe, p3_v = S.c("a"), S.c("pe"), S.c("v")
                S.flush()

        with contextlib.ExitStack() as s4:
            E4 = s4.enter_context
            ytile = E4(nc.sbuf_tensor("ytile", [128, NT * C], BF16, side="right"))
            s4b = contextlib.ExitStack()
            E4b = s4b.enter_context
            snegp = [E4b(nc.sbuf_tensor(f"snegp{p}", [128, 2 * T], F32, side="right")) for p in range(3)]
            sneg = [[snegp[p][:, j * T:(j + 1) * T] for j in range(2)] for p in range(3)]
            pbuf = [[E4b(nc.sbuf_tensor(f"pbuf{p}_{j}", [128, T], BF16, side="right")) for j in range(2)] for p in range(2)]
            ptb2 = [E4b(nc.sbuf_tensor(f"ptb2_{i}", [128, 1024], BF16, side="right")) for i in range(8)]
            ncmx = E4b(nc.sbuf_tensor("ncmx", [128, 24], F32, side="right"))
            negmax = E4b(nc.sbuf_tensor("negmax", [128, 6], F32, side="right"))
            sumr = E4b(nc.sbuf_tensor("sumr", [128, 6], F32, side="right"))
            rsum = E4b(nc.sbuf_tensor("rsum", [128, 6], F32, side="right"))
            ps_s = E4(nc.psum_tensor("ps_s", [128, 2048], F32))
            ps_tr = E4(nc.psum_tensor("ps_tr", [128, 1024], F32))
            ps_y = E4(nc.psum_tensor("ps_y", [128, 1024], F32))

            iters = [(hp, qt) for hp in range(H // 2) for qt in range(NT)]
            NIT = len(iters)
            sbank_rel = [0, 0]
            gchunk = [0]
            trslot = [("v", 0), ("v", 0)]
            ptb_war = [0] * 16
            exp_done = [0] * NIT
            mm_done = {}
            sneg_rel = [0, 0, 0]
            pbuf_war = [[0, 0], [0, 0]]
            ysc_done = {}
            av_of = [0] * NIT
            recip_done = [0] * NIT
            abc_war = [0, 0]
            qk_war = [0, 0]
            group_ld = [None, None]
            out_war = [None]

            def chunks_of(L):
                out = []
                c0 = 0
                while c0 < L:
                    w = min(512, L - c0)
                    out.append((c0, w))
                    c0 += w
                return out

            def emit_group_load(hp, early=False):
                hb = hp % 2
                if early:
                    wts = [wat_fin, qk_st_war[2], qk_st_war[3]]
                    wq = [wat_fin] + store_of_fc[0] + store_of_fc[8]
                else:
                    wts = [("v", abc_war[0])]
                    if hp == 0:
                        wts += [("pe", p3_pe), ("v", p3_v), ("a", p3_a)] + qk_st_fin
                    wq = [("pe", qk_war[hb])] if qk_war[hb] else []
                l1 = S.dma("sync", lambda e, hp=hp: e.dma_start(
                    out=abcp0[:, 0:T], in_=alibi_bc[2 * hp, :, :]), waits=wts)
                l2 = S.dma("sync", lambda e, hp=hp: e.dma_start(
                    out=abcp0[:, T:2 * T], in_=alibi_bc[2 * hp + 1, :, :]))
                l3 = S.dma("sync", lambda e, hp=hp, hb=hb: e.dma_start(
                    out=q_hp[hb][:, :], in_=q_sc[hp, :, :]), waits=wq)
                l4 = S.dma("sync", lambda e, hp=hp, hb=hb: e.dma_start(
                    out=k_hp[hb][:, :], in_=k_sc[hp, :, :]))
                group_ld[hb] = [l1, l2, l3, l4]

            def emit_scores_softmax(it):
                hp, qt = iters[it]
                L = (qt + 1) * 128
                par = it % 3
                ch = chunks_of(L)
                nch = len(ch)
                cd = 0
                pend_red = []

                def emit_red():
                    nonlocal cd
                    if not pend_red:
                        return
                    ci, c0, w, dep = pend_red.pop(0)
                    cd = S.op("vector", lambda e, par=par, c0=c0, w=w, ci=ci: e.tensor_reduce(
                        ncmx[:, par * 8:par * 8 + 8].rearrange("p (j c) -> p j c", j=2)[:, :, ci:ci + 1],
                        snegp[par].rearrange("p (j t) -> p j t", j=2)[:, :, c0:c0 + w],
                        AX.X, AluOpType.max),
                        waits=[("v", dep)], inc="v")

                for ci, (c0, w) in enumerate(ch):
                    cs = gchunk[0] % 2
                    gchunk[0] += 1
                    rA = cs * 1024
                    rB = cs * 1024 + 512
                    wts = []
                    if it == 0 and ci == 0:
                        wts += [("a", p3_a), ("v", p3_v)]
                    if sbank_rel[cs]:
                        wts.append(("v", sbank_rel[cs]))
                    hb = hp % 2
                    if qt == 0 and ci == 0:
                        wts += group_ld[hb][2:4]
                    S.op("tensor", lambda e, hb=hb, qt=qt, c0=c0, w=w, rA=rA: e.matmul(
                        ps_s[:, rA: rA + w],
                        q_hp[hb][0:64, qt * 128:(qt + 1) * 128],
                        k_hp[hb][0:64, c0:c0 + w],
                        start=True, stop=True, tile_position=(0, 0)),
                        waits=wts)
                    md = S.op("tensor", lambda e, hb=hb, qt=qt, c0=c0, w=w, rB=rB: e.matmul(
                        ps_s[:, rB: rB + w],
                        q_hp[hb][64:128, qt * 128:(qt + 1) * 128],
                        k_hp[hb][64:128, c0:c0 + w],
                        start=True, stop=True, tile_position=(64, 0)), inc="pe")
                    mm_done[(it, ci)] = md
                    qk_war[hb] = md
                    wts2 = [("pe", md)]
                    if qt == 0 and ci == 0:
                        wts2 += group_ld[hp % 2][0:2]
                    if sneg_rel[par]:
                        wts2.append(("a", sneg_rel[par]))
                    aoff = T - 128 + c0 - qt * 128
                    stt = S.op("vector", lambda e, par=par, hb=hp % 2, aoff=aoff, c0=c0, w=w, rA=rA: e.scalar_tensor_tensor(
                        snegp[par].rearrange("p (j t) -> p j t", j=2)[:, :, c0:c0 + w],
                        ps_s[:, rA:rA + 1024].rearrange("p (j t) -> p j t", j=2)[:, :, 0:w],
                        1.0,
                        abcp[hb].rearrange("p (j t) -> p j t", j=2)[:, :, aoff:aoff + w],
                        AluOpType.mult, AluOpType.add),
                        waits=wts2, inc="v")
                    abc_war[0] = S.c("v")
                    sbank_rel[cs] = stt
                    pend_red.append((ci, c0, w, stt))
                    if ci == nch - 1:
                        while pend_red:
                            emit_red()
                    elif len(pend_red) >= 2:
                        emit_red()
                nms = []
                for j in range(2):
                    nms.append(S.op("vector", lambda e, par=par, j=j, nch=nch: e.tensor_reduce(
                        negmax[:, par * 2 + j: par * 2 + j + 1],
                        ncmx[:, par * 8 + j * 4: par * 8 + j * 4 + nch], AX.X,
                        AluOpType.max, negate=True),
                        waits=[("v", cd), ("a", exp_done[it - 3] if it >= 3 else 0)], inc="v"))
                return nms

            def emit_exp(it, nms):
                hp, qt = iters[it]
                L = (qt + 1) * 128
                par = it % 3
                pb = it % 2
                ed = 0
                for j in range(2):
                    wts = [("v", nms[j])]
                    if it >= 2:
                        wts.append(("pe", pbuf_war[pb][j]))
                    if it >= 3:
                        wts.append(("v", recip_done[it - 3]))
                    ed = S.op("scalar", lambda e, par=par, pb=pb, j=j, L=L: e.activation(
                        pbuf[pb][j][:, 0:L], sneg[par][j][:, 0:L], AF.Exp,
                        bias=negmax[:, par * 2 + j: par * 2 + j + 1],
                        accum_out=sumr[:, par * 2 + j: par * 2 + j + 1]),
                        waits=wts, inc="a")
                exp_done[it] = ed
                sneg_rel[par] = ed

            def emit_recip(it):
                par = it % 3
                r = 0
                for j in range(2):
                    r = S.op("vector", lambda e, par=par, j=j: e.reciprocal(
                        rsum[:, par * 2 + j: par * 2 + j + 1],
                        sumr[:, par * 2 + j: par * 2 + j + 1]),
                        waits=[("a", exp_done[it]), ("a", ysc_done.get(it - 3, 0))], inc="v")
                recip_done[it] = r

            tr_evs = {}
            evac_cnt = [0]

            def emit_tr(it):
                hp, qt = iters[it]
                par = it % 2
                nkc = qt + 1
                nbat = (nkc + 3) // 4
                evs = []
                pend = []
                gb = [0]

                def flush_evac():
                    if not pend:
                        return
                    use_v = False
                    evac_cnt[0] += 1
                    eng, tag = ("vector", "v") if use_v else ("scalar", "a")

                    def copy_op(e, dst, src):
                        if use_v:
                            return e.tensor_copy(dst, src)
                        return e.activation(dst, src, AF.Copy)

                    if False and len(pend) == 2 and pend[0][2] == 0 and pend[1][2] == 1 and \
                       pend[0][4] % 2 == 0 and pend[0][4] + 1 == pend[1][4]:
                        (j0, k0, s0, kcs0, p0, lt0), (j1, k1, s1, kcs1, p1, lt1) = pend
                        wb1 = len(kcs1) * 128
                        ev = S.op(eng, lambda e, p0=p0, wb1=wb1: copy_op(
                            e, ptb2[p0 // 2][:, 0:512 + wb1], ps_tr[:, 0:512 + wb1]),
                            waits=[("pe", lt1)], inc=tag)
                        trslot[0] = (tag, ev)
                        trslot[1] = (tag, ev)
                        evs.append((j0, k0, (tag, ev), kcs0, p0))
                        evs.append((j1, k1, (tag, ev), kcs1, p1))
                    else:
                        for (j, kcb, slot, kcs, pslot, lt) in pend:
                            wb = len(kcs) * 128
                            ev = S.op(eng, lambda e, pslot=pslot, slot=slot, wb=wb: copy_op(
                                e, ptb2[pslot // 2][:, (pslot % 2) * 512: (pslot % 2) * 512 + wb],
                                ps_tr[:, slot * 512: slot * 512 + wb]),
                                waits=[("pe", lt)], inc=tag)
                            trslot[slot] = (tag, ev)
                            evs.append((j, kcb, (tag, ev), kcs, pslot))
                    pend.clear()

                for j in range(2):
                    lasttr = 0
                    for kcb in range(nbat):
                        slot = gb[0] % 2
                        gb[0] += 1
                        pslot = par * 8 + j * 4 + kcb
                        kcs = list(range(kcb * 4, min(kcb * 4 + 4, nkc)))
                        for idx, kc in enumerate(kcs):
                            wts = []
                            if idx == 0:
                                wts = [("a", exp_done[it]), trslot[slot],
                                       ("pe", ptb_war[pslot])]
                            lasttr = S.op("tensor", lambda e, par=par, j=j, kc=kc, slot=slot, idx=idx, n=len(kcs): e.matmul(
                                ps_tr[:, slot * 512 + (kc % 4) * 128: slot * 512 + (kc % 4) * 128 + 128],
                                pbuf[par][j][:, kc * 128:(kc + 1) * 128], ident_s[:, :],
                                start=(idx == 0), stop=(idx == n - 1)),
                                waits=wts, inc="pe" if idx == len(kcs) - 1 else None)
                        pend.append((j, kcb, slot, kcs, pslot, lasttr))
                        if slot == 1:
                            flush_evac()
                    pbuf_war[par][j] = lasttr
                flush_evac()
                tr_evs[it] = evs

            def emit_av(it):
                hp, qt = iters[it]
                ypar = it % 2
                nkc = qt + 1
                evs = tr_evs.pop(it)
                for (j, kcb, ev, kcs, pslot) in evs:
                    h = 2 * hp + j
                    for idx, kc in enumerate(kcs):
                        wts = [ev] if idx == 0 else []
                        if j == 0 and kc == 0:
                            wts = list(wts) + [("a", ysc_done.get(it - 2, 0))]
                        first = (j == 0 and kc == 0)
                        last = (j == 1 and kc == nkc - 1)
                        r = S.op("tensor", lambda e, pslot=pslot, ypar=ypar, j=j, kc=kc, h=h, first=first, last=last: e.matmul(
                            ps_y[:, ypar * 512 + j * 64: ypar * 512 + j * 64 + 64],
                            ptb2[pslot // 2][:, (pslot % 2) * 512 + (kc % 4) * 128: (pslot % 2) * 512 + (kc % 4) * 128 + 128],
                            Vt[:, kc * C + h * 64: kc * C + h * 64 + 64],
                            start=first, stop=last),
                            waits=wts, inc="pe" if last else None)
                        if last:
                            av_of[it] = r
                    ptb_war[pslot] = S.c("pe")

            def emit_ysc(it):
                hp, qt = iters[it]
                par = it % 3
                ypar = it % 2
                y = 0
                for j in range(2):
                    h = 2 * hp + j
                    wts_y = [("pe", av_of[it]), ("v", recip_done[it])]
                    y = S.op("scalar", lambda e, par=par, ypar=ypar, j=j, qt=qt, h=h: e.activation(
                        ytile[:, qt * C + h * 64: qt * C + (h + 1) * 64],
                        ps_y[:, ypar * 512 + j * 64: ypar * 512 + j * 64 + 64],
                        AF.Copy, scale=rsum[:, par * 2 + j: par * 2 + j + 1]),
                        waits=wts_y, inc="a")
                ysc_done[it] = y

            wpj_b = E4b(nc.sbuf_tensor("wpj_b", [128, DC * C], BF16, side="right"))
            yT_sb = E4b(nc.sbuf_tensor("yT_sb", [128, C], BF16, side="right"))
            xfst = E4b(nc.sbuf_tensor("xfst", [128, C], F32, side="right"))
            outsb = E4b(nc.sbuf_tensor("outsb", [128, C], F32, side="right"))
            q_hp = [E4b(nc.sbuf_tensor(f"q_hp{i}", [128, T], BF16, side="right")) for i in range(2)]
            k_hp = [E4b(nc.sbuf_tensor(f"k_hp{i}", [128, T], BF16, side="right")) for i in range(2)]
            abcp0 = E4b(nc.sbuf_tensor("abcp0", [128, 2 * T], F32, side="right"))
            abcp = [abcp0, abcp0]
            wat_fin = ("v", wat_cv)
            emit_group_load(0, early=(T >= 2048))
            pj_rel = [0]
            yT_war = [0]
            xf_war = [0]
            for cc in range(DC):
                S.dma("sync", lambda e, cc=cc: e.dma_start(
                    out=wpj_b[:, cc * 1024:(cc + 1) * 1024], in_=wpj_sc[cc, :, :]),
                    waits=conv_stores_fin + (qk_st_fin + [("pe", p3_pe), ("v", p3_v), ("a", p3_a)]
                                             if cc == 0 else []))
            wpj_ld = [("dl0", S.c("dl0")), ("dl1", S.c("dl1"))]

            def emit_proj_unit(qt, ydep):
                for b2 in range(2):
                    slot = b2
                    last = 0
                    for j in range(4):
                        cc = b2 * 4 + j
                        wts = []
                        if j == 0:
                            wts = [("a", ydep), trslot[slot]]
                        last = S.op("tensor", lambda e, qt=qt, cc=cc, slot=slot, j=j: e.matmul(
                            ps_tr[:, slot * 512 + j * 128: slot * 512 + (j + 1) * 128],
                            ytile[:, qt * C + cc * 128: qt * C + (cc + 1) * 128],
                            ident_s[:, :], start=(j == 0), stop=(j == 3)),
                            waits=wts, inc="pe" if j == 3 else None)
                    ev = S.op("vector", lambda e, b2=b2, slot=slot: e.tensor_copy(
                        yT_sb[:, b2 * 512:(b2 + 1) * 512],
                        ps_tr[:, slot * 512:(slot + 1) * 512]),
                        waits=[("pe", last), ("pe", yT_war[0])], inc="v")
                    trslot[slot] = ("v", ev)
                yT_done = S.c("v")
                xl = S.dma("sync", lambda e, qt=qt: e.dma_start(
                    out=xfst[:, :], in_=xf_sc[qt * 128:(qt + 1) * 128, :]),
                    waits=p2_xf_st + [("v", xf_war[0])])
                lastp = 0
                for cc in range(DC):
                    for half in range(2):
                        slot = half
                        wts = []
                        if cc == 0:
                            wts = wpj_ld + [("v", yT_done), trslot[slot],
                                            ("v", pj_rel[0]), trslot[1 - slot]]
                        lastp = S.op("tensor", lambda e, slot=slot, cc=cc, half=half: e.matmul(
                            ps_tr[:, slot * 512:(slot + 1) * 512],
                            yT_sb[:, cc * 128:(cc + 1) * 128],
                            wpj_b[:, cc * 1024 + half * 512: cc * 1024 + half * 512 + 512],
                            start=(cc == 0), stop=False),
                            waits=wts, inc="pe" if half == 1 else None)
                oas = []
                for half in range(2):
                    slot = half
                    lastb = S.op("tensor", lambda e, slot=slot, half=half: e.matmul(
                        ps_tr[:, slot * 512:(slot + 1) * 512], onesb_s[0:1, :],
                        bpj_b[0:1, half * 512:(half + 1) * 512],
                        start=False, stop=True),
                        waits=[("v", cv_bp)], inc="pe")
                    oa = S.op("vector", lambda e, slot=slot, half=half: e.tensor_tensor(
                        outsb[:, half * 512:(half + 1) * 512],
                        ps_tr[:, slot * 512:(slot + 1) * 512],
                        xfst[:, half * 512:(half + 1) * 512], AluOpType.add),
                        waits=[("pe", lastb), xl]
                              + ([out_war[0]] if out_war[0] else []), inc="v")
                    pj_rel[0] = oa
                    trslot[slot] = ("v", oa)
                yT_war[0] = S.c("pe")
                xf_war[0] = pj_rel[0]
                out_war[0] = S.dma("gpsimd", lambda e, qt=qt: e.dma_start(
                    out=out_d[qt * 128:(qt + 1) * 128, :], in_=outsb[:, :]),
                    waits=[("v", pj_rel[0])])

            nm_of = {}
            NHP = H // 2
            for step in range(NIT + 3):
                if step < NIT:
                    hp, qt = iters[step]
                    if qt == 0 and hp > 0:
                        emit_group_load(hp)
                    nm_of[step] = emit_scores_softmax(step)
                if 3 <= step <= NIT + 2:
                    emit_ysc(step - 3)
                if 1 <= step <= NIT:
                    emit_recip(step - 1)
                    emit_tr(step - 1)
                if 2 <= step <= NIT + 1:
                    emit_av(step - 2)
                if step < NIT:
                    emit_exp(step, nm_of[step])
                if 3 <= step <= NIT + 2:
                    it3 = step - 3
                    hp3, qt3 = iters[it3]
                    if hp3 == NHP - 1:
                        emit_proj_unit(qt3, ysc_done[it3])
            S.flush()
            s4b.close()

            fin = [("dg0", S.c("dg0")), ("dg1", S.c("dg1"))]
            S.op("gpsimd", lambda e: None, waits=fin)
            S.op("sync", lambda e: None, waits=fin)
            S.flush()

    return nc



def _consts(T):
    start = 2.0 ** (-(2.0 ** -(math.log2(H) - 3)))
    slopes = np.array([start * start ** i for i in range(H)], dtype=np.float32)
    d_rel = (np.arange(T, dtype=np.float32)[None, :] + np.float32(128 - T)
             - np.arange(128, dtype=np.float32)[:, None])
    alibi_bc = np.ascontiguousarray(slopes[:, None, None] * d_rel[None, :, :])
    alibi_bc[:, d_rel > 0] = np.float32(-1e38)
    return {
        "ident": np.eye(128, dtype=BF),
        "alibi_bc": alibi_bc,
        "ones_b": np.ones((1, 128), dtype=BF),
        "ones_f": np.ones((1, 128), dtype=np.float32),
    }


def make_inputs(xb, W_attn, b_attn, W_proj, b_proj, temperature, wg, wu, wd, T):
    con = _consts(T)
    wgr = np.ascontiguousarray(
        wg.reshape(HCP, 2, 128, DC, 128).transpose(0, 4, 1, 3, 2).reshape(HCP, 128, 2048))
    wur = np.ascontiguousarray(
        wu.reshape(HCP, 2, 128, DC, 128).transpose(0, 4, 1, 3, 2).reshape(HCP, 128, 2048))
    wdr = np.ascontiguousarray(
        wd.T.reshape(HCP, 2, 128, C).transpose(0, 2, 1, 3).reshape(HCP, 128, 2048))
    d = {
        "xin": np.ascontiguousarray(xb),
        "wattn": np.ascontiguousarray(W_attn),
        "battn_qk": np.ascontiguousarray(b_attn[:2048].reshape(16, 128).T),
        "battn_v": np.ascontiguousarray(b_attn[2048:].reshape(1, C)),
        "wproj": np.ascontiguousarray(W_proj),
        "bproj": np.ascontiguousarray(b_proj.reshape(1, C)),
        "temp": np.ascontiguousarray(temperature.reshape(1, 1)),
        "wg_r": wgr,
        "wu_r": wur,
        "wd_r": wdr,
    }
    d.update(con)
    return d


def kernel(L0_attn_x, L0_attn_W_attn, L0_attn_b_attn, L0_attn_W_proj,
           L0_attn_b_proj, L0_attn_temperature, L0_ffn_w_gate, L0_ffn_w_up,
           L0_ffn_w_down):
    T = int(L0_attn_x.shape[1])
    x = np.ascontiguousarray(np.asarray(L0_attn_x, dtype=np.float32))
    W_attn = np.asarray(L0_attn_W_attn, dtype=np.float32)
    b_attn = np.asarray(L0_attn_b_attn, dtype=np.float32)
    W_proj = np.asarray(L0_attn_W_proj, dtype=np.float32)
    b_proj = np.asarray(L0_attn_b_proj, dtype=np.float32)
    tempv = np.asarray(L0_attn_temperature, dtype=np.float32)
    wg = np.asarray(L0_ffn_w_gate, dtype=np.int32)
    wu = np.asarray(L0_ffn_w_up, dtype=np.int32)
    wd = np.asarray(L0_ffn_w_down, dtype=np.int32)

    nc = build_program(T)
    in_maps = [make_inputs(x[b], W_attn, b_attn, W_proj, b_proj, tempv, wg, wu, wd, T)
               for b in range(x.shape[0])]
    res = run_bass_kernel_spmd(nc, in_maps, core_ids=list(range(x.shape[0])))
    return np.stack([r["out"] for r in res.results], axis=0)


# version 35
# speedup vs baseline: 1.0279x; 1.0256x over previous
import contextlib
import math
import numpy as np
import ml_dtypes
import concourse.bass as bass
import concourse.mybir as mybir
from concourse.bass_utils import run_bass_kernel_spmd
from concourse.alu_op_type import AluOpType

F32 = mybir.dt.float32
BF16 = mybir.dt.bfloat16
I32 = mybir.dt.int32
AF = mybir.ActivationFunctionType
AX = mybir.AxisListType
BF = ml_dtypes.bfloat16

NCORES = 8
C = 1024
H = 16
D = 64
HID = 3072
DC = C // 128
HCN = HID // 128
HCP = HCN // 2

ENGS = ("sync", "gpsimd", "vector", "scalar", "tensor")


class KB:

    QSEMS = {"sync": ("dl0", "dl1"), "gpsimd": ("dg0", "dg1"),
             "scalar": ("ds0", "ds1")}

    def __init__(self, nc, block, sem):
        self.nc = nc
        self.block = block
        self.sem = sem
        self.cnt = {k: 0 for k in sem}
        self.pend = []
        self.waited = {e: {} for e in ENGS}
        self.qidx = {"sync": 0, "gpsimd": 0, "scalar": 0}

    def dma(self, queue, fn, waits=()):
        sems = self.QSEMS[queue]
        i = self.qidx[queue]
        self.qidx[queue] ^= 1
        s = sems[i]
        auto = [(s, self.cnt[s])]
        cnt = self.op(queue, fn, waits=list(waits) + auto, inc=s, inc_n=16)
        return (s, cnt)

    def op(self, eng, fn, waits=(), inc=None, inc_n=1):
        waits = [(s, v) for (s, v) in waits if v and v > 0]
        incs = []
        if inc is not None:
            self.cnt[inc] += inc_n
            incs.append((inc, inc_n))
        self.pend.append((eng, fn, waits, incs))
        return self.cnt[inc] if inc is not None else 0

    def c(self, s):
        return self.cnt[s]

    def flush(self):
        for eng in ENGS:
            ops = [(fn, w, i) for (e, fn, w, i) in self.pend if e == eng]
            if not ops:
                continue
            waited = self.waited[eng]
            sem = self.sem

            def section(esec, ops=ops, waited=waited):
                for fn, waits, incs in ops:
                    for s, v in waits:
                        if v > waited.get(s, 0):
                            esec.wait_ge(sem[s], v)
                            waited[s] = v
                    instr = fn(esec)
                    for s, n in incs:
                        instr.then_inc(sem[s], n)

            getattr(self.block, eng)(section)
        self.pend = []


def build_program(T):
    NT = T // 128
    NTB = T // 512
    nc = bass.Bass("TRN2", target_bir_lowering=False)

    def din(name, shape, dt):
        return nc.dram_tensor(name, shape, dt, kind="ExternalInput").ap()

    xin = din("xin", [T, C], F32)
    wattn = din("wattn", [C, 3 * C], F32)
    battn_qk = din("battn_qk", [128, 16], F32)
    battn_v = din("battn_v", [1, C], F32)
    wproj = din("wproj", [C, C], F32)
    bproj = din("bproj", [1, C], F32)
    temp = din("temp", [1, 1], F32)
    wg_r = din("wg_r", [HCP, 128, 2048], I32)
    wu_r = din("wu_r", [HCP, 128, 2048], I32)
    wd_r = din("wd_r", [HCP, 128, 2048], I32)
    ident = din("ident", [128, 128], BF16)
    alibi_bc = din("alibi_bc", [H, 128, T], F32)
    ones_b = din("ones_b", [1, 128], BF16)
    ones_f = din("ones_f", [1, 128], F32)

    out_d = nc.dram_tensor("out", [T, C], F32, kind="ExternalOutput").ap()
    wg_sc = nc.dram_tensor("wg_sc", [HCP, 128, 2048], BF16, kind="Internal").ap()
    wu_sc = nc.dram_tensor("wu_sc", [HCP, 128, 2048], BF16, kind="Internal").ap()
    wd_sc = nc.dram_tensor("wd_sc", [HCP, 128, 2048], BF16, kind="Internal").ap()
    xf_sc = nc.dram_tensor("xf_sc", [T, C], F32, kind="Internal").ap()
    q_sc = nc.dram_tensor("q_sc", [DC, 128, T], BF16, kind="Internal").ap()
    wpj_sc = nc.dram_tensor("wpj_sc", [DC, 128, C], BF16, kind="Internal").ap()
    k_sc = nc.dram_tensor("k_sc", [DC, 128, T], BF16, kind="Internal").ap()

    with contextlib.ExitStack() as st:
        E = st.enter_context
        sem = {s: E(nc.semaphore("sem_" + s)) for s in
               ("dl0", "dl1", "dg0", "dg1", "ds0", "ds1", "pe", "v", "a")}
        block = E(nc.Block())
        S = KB(nc, block, sem)

        ident_s = E(nc.sbuf_tensor("ident_s", [128, 128], BF16))
        onesb_s = E(nc.sbuf_tensor("onesb_s", [1, 128], BF16))
        onesf_s = E(nc.sbuf_tensor("onesf_s", [1, 128], F32))
        bqk_s = E(nc.sbuf_tensor("bqk_s", [128, 16], F32))
        bv_b = E(nc.sbuf_tensor("bv_b", [1, C], BF16))
        bpj_b = E(nc.sbuf_tensor("bpj_b", [1, C], BF16))
        temp_s = E(nc.sbuf_tensor("temp_s", [1, 1], F32))
        t8_s = E(nc.sbuf_tensor("t8_s", [1, 1], F32))
        c11_s = E(nc.sbuf_tensor("c11_s", [1, 1], F32))
        cvec_s = E(nc.sbuf_tensor("cvec_s", [128, 1], F32))
        cbq_s = E(nc.sbuf_tensor("cbq_s", [128, 8], F32))
        Vt = E(nc.sbuf_tensor("Vt", [128, NT * C], BF16))

        cl = []
        for dst, src in [(ident_s, ident),
                         (onesb_s, ones_b), (onesf_s, ones_f),
                         (bqk_s, battn_qk), (temp_s, temp)]:
            cl.append(S.dma("sync", lambda e, d=dst, s=src: e.dma_start(out=d[:, :], in_=s)))
        c_consts = [cl[-1], cl[-2]]
        cv_t8 = S.op("vector", lambda e: e.tensor_scalar(
            t8_s[0:1, :], temp_s[0:1, :], 8.0, None, AluOpType.mult),
            waits=c_consts, inc="v")
        cv_c11 = S.op("vector", lambda e: e.reciprocal(c11_s[0:1, :], t8_s[0:1, :]),
                      waits=[("v", cv_t8)], inc="v")

        with contextlib.ExitStack() as sA:
            xnT = sA.enter_context(nc.sbuf_tensor("xnT", [128, DC * T], BF16, side="right"))
            wdres = sA.enter_context(nc.sbuf_tensor("wdres", [128, HCP * 2048], BF16, side="right"))

            with contextlib.ExitStack() as sB:
                E1 = sB.enter_context
                xstage = [E1(nc.sbuf_tensor(f"xstage{i}", [128, C], F32, side="right")) for i in range(4)]
                xn_st = [E1(nc.sbuf_tensor(f"xn_st{i}", [128, C], BF16, side="right")) for i in range(4)]
                sq_sc = [E1(nc.sbuf_tensor(f"sq_sc{i}", [128, C], F32, side="right")) for i in range(2)]
                wistage = [E1(nc.sbuf_tensor(f"wistage{i}", [128, 2048], I32, side="right")) for i in range(3)]
                cbuf = [E1(nc.sbuf_tensor(f"cbuf{i}", [128, 2048], BF16, side="right")) for i in range(2)]
                cbuf32 = [E1(nc.sbuf_tensor(f"cbuf32_{i}", [128, 2048], F32, side="right")) for i in range(2)]
                stat = {k: [E1(nc.sbuf_tensor(f"st_{k}{i}", [128, 1], F32, side="right")) for i in range(4)]
                        for k in ("s1", "s2", "m", "v", "r", "n")}
                bst_f = E1(nc.sbuf_tensor("bst_f", [1, C], F32, side="right"))
                wpst = E1(nc.sbuf_tensor("wpst", [128, C], F32, side="right"))
                ps1 = E1(nc.psum_tensor("ps1", [128, 4096], F32))

                bld1 = S.dma("sync", lambda e: e.dma_start(out=bst_f[0:1, 0:C], in_=battn_v))
                cv_bv = S.op("vector", lambda e: e.tensor_copy(bv_b[0:1, :], bst_f[0:1, 0:C]),
                             waits=[bld1], inc="v")
                bld2 = S.dma("sync", lambda e: e.dma_start(out=bst_f[0:1, 0:C], in_=bproj),
                             waits=[("v", cv_bv)])
                cv_bp = S.op("vector", lambda e: e.tensor_copy(bpj_b[0:1, :], bst_f[0:1, 0:C]),
                             waits=[bld2], inc="v")

                conv_tasks = []
                for m, src, dst_sc, width in (
                        [(0, wg_r, wg_sc, 2048), (1, wu_r, wu_sc, 2048),
                         (2, wd_r, None, 2048)]):
                    for j in range(HCP):
                        conv_tasks.append((src, dst_sc, j, width))
                for j in range(DC):
                    conv_tasks.append(("wproj", wpj_sc, j, 1024))
                n_tasks = len(conv_tasks)
                conv_i = [0]
                cbuf_store = [None, None, None]
                wist_conv = [0, 0, 0]
                c32_war = [0, 0, 0]
                wpst_war = [0]
                conv_stores = []

                def emit_conv(k):
                    src, dst_sc, j, width = conv_tasks[k]
                    bi = k % 3
                    b = k % 2
                    lq = "sync"
                    is_pj = (src == "wproj")
                    is_wd = dst_sc is None
                    if is_pj:
                        ld = S.dma("sync", lambda e, j=j: e.dma_start(
                            out=wpst[:, :], in_=wproj[j * 128:(j + 1) * 128, :]),
                            waits=[("a", wpst_war[0])])
                        cv = S.op("scalar", lambda e, b=b: e.activation(
                            cbuf[b][:, 0:1024], wpst[:, :], AF.Copy),
                            waits=[ld] + ([cbuf_store[b]] if cbuf_store[b] else []), inc="a")
                        wpst_war[0] = cv
                    else:
                        ld = S.dma(lq, lambda e, src=src, j=j, bi=bi, width=width: e.dma_start(
                            out=wistage[bi][:, 0:width], in_=src[j, :, :]),
                            waits=[("v", wist_conv[bi])])
                        cv0 = S.op("vector", lambda e, b=b, bi=bi, width=width: e.tensor_scalar(
                            cbuf32[b][:, 0:width], wistage[bi][:, 0:width], 1.0,
                            None, AluOpType.mult),
                            waits=[ld, ("a", c32_war[b])]
                                  + ([cbuf_store[b]] if cbuf_store[b] else []), inc="v")
                        cdst = (lambda e, j=j: wdres[:, j * 2048:(j + 1) * 2048]) if is_wd \
                            else (lambda e, b=b, width=width: cbuf[b][:, 0:width])
                        cv = S.op("scalar", lambda e, b=b, width=width, cdst=cdst: e.activation(
                            cdst(e), cbuf32[b][:, 0:width], AF.Copy, scale=1.0 / 127.0),
                            waits=[("v", cv0)], inc="a")
                        wist_conv[bi] = cv0
                        c32_war[b] = cv
                    if is_wd:
                        return
                    sto = S.dma("gpsimd", lambda e, dst_sc=dst_sc, j=j, b=b, width=width: e.dma_start(
                        out=dst_sc[j, :, :], in_=cbuf[b][:, 0:width]),
                        waits=[("a", cv)])
                    cbuf_store[b] = sto
                    conv_stores.append(sto)

                ln_xn = [0] * NT
                stile_tr = [0] * (NT // 4)
                tr_evac = [[(None, 0)] * DC for _ in range(NT // 4)]
                prev_xuse = [0, 0, 0, 0]
                stat_war = [0, 0, 0, 0]
                sq_war = [0, 0, 0, 0]
                for i in range(NT):
                    for _ in range(3):
                        if conv_i[0] < n_tasks:
                            emit_conv(conv_i[0])
                            conv_i[0] += 1
                    b = i % 4
                    p = i % 4
                    ld = S.dma("sync", lambda e, i=i, b=b: e.dma_start(
                        out=xstage[b][:, :], in_=xin[i * 128:(i + 1) * 128, :]),
                        waits=[("v", prev_xuse[b])])
                    rs = S.op("vector", lambda e, b=b, p=p: e.tensor_reduce(
                        stat["s1"][p][:, :], xstage[b][:, :], AX.X, AluOpType.add),
                        waits=[ld, ("v", stat_war[p])], inc="v")
                    sq = S.op("scalar", lambda e, b=b, p=p: e.activation(
                        sq_sc[p % 2][:, :], xstage[b][:, :], AF.Square,
                        accum_out=stat["s2"][p][:, :]),
                        waits=[ld, ("v", sq_war[p]), ("v", sq_war[(p + 2) % 4])], inc="a")
                    mn = S.op("vector", lambda e, p=p: e.tensor_scalar(
                        stat["m"][p][:, :], stat["s1"][p][:, :], 1.0 / C, None,
                        AluOpType.mult), waits=[("v", rs)], inc="v")
                    v1 = S.op("vector", lambda e, p=p: e.tensor_tensor(
                        stat["v"][p][:, :], stat["m"][p][:, :], stat["m"][p][:, :],
                        AluOpType.mult), waits=[("v", mn), ("a", sq)], inc="v")
                    v2 = S.op("vector", lambda e, p=p: e.tensor_scalar(
                        stat["s2"][p][:, :], stat["s2"][p][:, :], 1.0 / C, 1e-5,
                        AluOpType.mult, AluOpType.add), waits=[("a", sq)], inc="v")
                    v3 = S.op("vector", lambda e, p=p: e.tensor_tensor(
                        stat["v"][p][:, :], stat["s2"][p][:, :], stat["v"][p][:, :],
                        AluOpType.subtract), waits=[("v", v2)], inc="v")
                    sq_war[p] = v3
                    sd = S.op("scalar", lambda e, p=p: e.activation(
                        stat["v"][p][:, :], stat["v"][p][:, :], AF.Sqrt),
                        waits=[("v", v3)], inc="a")
                    rc = S.op("vector", lambda e, p=p: e.reciprocal(
                        stat["r"][p][:, :], stat["v"][p][:, :]),
                        waits=[("a", sd)], inc="v")
                    ng = S.op("vector", lambda e, p=p: e.tensor_scalar(
                        stat["n"][p][:, :], stat["m"][p][:, :], stat["r"][p][:, :],
                        -1.0, AluOpType.mult, AluOpType.mult),
                        waits=[("v", rc)], inc="v")
                    stile = i // 4
                    xn = S.op("vector", lambda e, i=i, b=b, p=p: e.tensor_scalar(
                        xn_st[i % 4][:, :], xstage[b][:, :], stat["r"][p][:, :],
                        stat["n"][p][:, :], AluOpType.mult, AluOpType.add),
                        waits=[("v", ng),
                               ("pe", stile_tr[stile - 1] if stile >= 1 else 0)],
                        inc="v")
                    prev_xuse[b] = xn
                    stat_war[p] = xn
                    ln_xn[i] = xn
                    if i % 4 == 3:
                        last = 0
                        for dc in range(DC):
                            for j in range(4):
                                wts = [("v", ln_xn[stile * 4 + j])]
                                if stile >= 1:
                                    es, ev = tr_evac[stile - 1][dc]
                                    wts.append((es, ev))
                                last = S.op("tensor", lambda e, dc=dc, j=j, stile=stile: e.matmul(
                                    ps1[:, dc * 512 + j * 128: dc * 512 + (j + 1) * 128],
                                    xn_st[(stile * 4 + j) % 4][:, dc * 128:(dc + 1) * 128],
                                    ident_s[:, :], start=(j == 0), stop=(j == 3)),
                                    waits=wts, inc="pe" if j == 3 else None)
                            if dc % 2 == 0:
                                ev = S.op("vector", lambda e, dc=dc, stile=stile: e.tensor_copy(
                                    xnT[:, dc * T + stile * 512: dc * T + (stile + 1) * 512],
                                    ps1[:, dc * 512:(dc + 1) * 512]),
                                    waits=[("pe", last)], inc="v")
                                tr_evac[stile][dc] = ("v", ev)
                            else:
                                ev = S.op("scalar", lambda e, dc=dc, stile=stile: e.activation(
                                    xnT[:, dc * T + stile * 512: dc * T + (stile + 1) * 512],
                                    ps1[:, dc * 512:(dc + 1) * 512], AF.Copy),
                                    waits=[("pe", last)], inc="a")
                                tr_evac[stile][dc] = ("a", ev)
                        stile_tr[stile] = last
                while conv_i[0] < n_tasks:
                    emit_conv(conv_i[0])
                    conv_i[0] += 1
                conv_stores_fin = [("dg0", S.c("dg0")), ("dg1", S.c("dg1"))]
                p1_v, p1_a, p1_pe = S.c("v"), S.c("a"), S.c("pe")
                S.flush()

            with contextlib.ExitStack() as sF:
                E2 = sF.enter_context
                wgbuf = [E2(nc.sbuf_tensor(f"wgbuf{i}", [128, 2048], BF16, side="right")) for i in range(2)]
                wubuf = [E2(nc.sbuf_tensor(f"wubuf{i}", [128, 2048], BF16, side="right")) for i in range(2)]
                hbuf = E2(nc.sbuf_tensor("hbuf", [128, HCN * 512], BF16, side="right"))
                gs_s = [E2(nc.sbuf_tensor(f"gs_s{i}", [128, 512], BF16, side="right")) for i in range(2)]
                us_s = [E2(nc.sbuf_tensor(f"us_s{i}", [128, 512], BF16, side="right")) for i in range(2)]
                x2st = [E2(nc.sbuf_tensor("x2st0", [128, C], F32, side="right"))] * 2
                xfout = [E2(nc.sbuf_tensor(f"xfout{i}", [128, C], F32, side="right")) for i in range(2)]
                ps_g = E2(nc.psum_tensor("ps_g", [128, 1024], F32))
                ps_u = E2(nc.psum_tensor("ps_u", [128, 1024], F32))
                ps_dn = E2(nc.psum_tensor("ps_dn", [128, 2048], F32))

                wgu_ld = {}
                wgld_war = [0, 0]
                grel = [p1_a, p1_a]
                urel = [p1_a, p1_a]
                gs_war = [0, 0]
                dnrel = [0, 0, 0, 0]
                hbuf_war = 0
                x2_war = [0, 0]
                xf_war = [None, None]
                for tb in range(NTB):
                    for hcp in range(HCP):
                        b = hcp % 2
                        wts = conv_stores_fin + [("pe", wgld_war[b])]
                        if tb == 0 and hcp == 0:
                            wts += [("pe", p1_pe), ("v", p1_v), ("a", p1_a)]
                        ldg = S.dma("sync", lambda e, hcp=hcp, b=b: e.dma_start(
                            out=wgbuf[b][:, :], in_=wg_sc[hcp, :, :]), waits=wts)
                        ldu = S.dma("sync", lambda e, hcp=hcp, b=b: e.dma_start(
                            out=wubuf[b][:, :], in_=wu_sc[hcp, :, :]))
                        wgu_ld[(tb, hcp)] = [ldg, ldu]
                        for hc2 in range(2):
                            hc = hcp * 2 + hc2
                            gp = hc % 2
                            wts_g = [ldg, ldu, ("a", grel[gp])]
                            if tb == 0 and hc == 0:
                                wts_g += [("v", p1_v), ("a", p1_a), ("pe", p1_pe)]
                            lg = lu = 0
                            for dc in range(DC):
                                lg = S.op("tensor", lambda e, hc2=hc2, dc=dc, gp=gp, b=b, tb=tb: e.matmul(
                                    ps_g[:, gp * 512:(gp + 1) * 512],
                                    wgbuf[b][:, hc2 * 1024 + dc * 128: hc2 * 1024 + (dc + 1) * 128],
                                    xnT[:, dc * T + tb * 512: dc * T + tb * 512 + 512],
                                    start=(dc == 0), stop=(dc == DC - 1)),
                                    waits=wts_g if dc == 0 else (),
                                    inc="pe" if dc == DC - 1 else None)
                            for dc in range(DC):
                                lu = S.op("tensor", lambda e, hc2=hc2, dc=dc, gp=gp, b=b, tb=tb: e.matmul(
                                    ps_u[:, gp * 512:(gp + 1) * 512],
                                    wubuf[b][:, hc2 * 1024 + dc * 128: hc2 * 1024 + (dc + 1) * 128],
                                    xnT[:, dc * T + tb * 512: dc * T + tb * 512 + 512],
                                    start=(dc == 0), stop=(dc == DC - 1)),
                                    waits=[("a", urel[gp])] if dc == 0 else (),
                                    inc="pe" if dc == DC - 1 else None)
                            wgld_war[b] = lu
                            sl = S.op("scalar", lambda e, gp=gp: e.activation(
                                gs_s[gp][:, :], ps_g[:, gp * 512:(gp + 1) * 512], AF.Silu),
                                waits=[("pe", lg), ("v", gs_war[gp])], inc="a")
                            grel[gp] = sl
                            ue = S.op("scalar", lambda e, gp=gp: e.activation(
                                us_s[gp][:, :], ps_u[:, gp * 512:(gp + 1) * 512], AF.Copy),
                                waits=[("pe", lu), ("v", gs_war[gp])], inc="a")
                            urel[gp] = ue
                            wts_m = [("a", ue)]
                            if tb > 0 and hc == 0:
                                wts_m.append(("pe", hbuf_war))
                            ml = S.op("vector", lambda e, hc=hc, gp=gp: e.tensor_tensor(
                                hbuf[:, hc * 512:(hc + 1) * 512], gs_s[gp][:, :],
                                us_s[gp][:, :], AluOpType.mult),
                                waits=wts_m, inc="v")
                            gs_war[gp] = ml
                    mults_done = S.c("v")
                    for pas in range(2):
                        lastd = [0] * 4
                        for hcp2 in range(HCN // 2):
                            for hc2 in range(2):
                                hc = 2 * hcp2 + hc2
                                for tsh in range(4):
                                    ts = pas * 2 + tsh // 2
                                    half = tsh % 2
                                    wts_d = []
                                    if hc == 0:
                                        wts_d += [("v", mults_done), ("v", dnrel[tsh])]
                                    r = S.op("tensor", lambda e, hc=hc, hcp2=hcp2, hc2=hc2, ts=ts, half=half, tsh=tsh: e.matmul(
                                        ps_dn[:, tsh * 512:(tsh + 1) * 512],
                                        hbuf[:, hc * 512 + ts * 128: hc * 512 + ts * 128 + 128],
                                        wdres[:, hcp2 * 2048 + hc2 * 1024 + half * 512: hcp2 * 2048 + hc2 * 1024 + (half + 1) * 512],
                                        start=(hc == 0), stop=(hc == HCN - 1)),
                                        waits=wts_d, inc="pe" if tsh == 3 else None)
                                    if tsh == 3:
                                        if hc == HCN - 1:
                                            lastd = [r] * 4
                        for tsh in range(4):
                            ts = pas * 2 + tsh // 2
                            half = tsh % 2
                            tt = tb * 4 + ts
                            xb = ts % 2
                            if half == 0:
                                wts_x2 = [("v", x2_war[xb])]
                                if tb == 0 and pas == 0 and tsh == 0:
                                    wts_x2 += [("pe", p1_pe), ("v", p1_v), ("a", p1_a)] + conv_stores_fin
                                x2_ld = S.dma("gpsimd", lambda e, tt=tt, xb=xb: e.dma_start(
                                    out=x2st[xb][:, :], in_=xin[tt * 128:(tt + 1) * 128, :]),
                                    waits=wts_x2)
                            xa = S.op("vector", lambda e, xb=xb, tsh=tsh, half=half: e.tensor_tensor(
                                xfout[xb][:, half * 512:(half + 1) * 512],
                                ps_dn[:, tsh * 512:(tsh + 1) * 512],
                                x2st[xb][:, half * 512:(half + 1) * 512], AluOpType.add),
                                waits=[("pe", lastd[tsh]), x2_ld]
                                      + ([xf_war[xb]] if xf_war[xb] else []), inc="v")
                            dnrel[tsh] = xa
                            if half == 1:
                                x2_war[xb] = xa
                                xf_war[xb] = S.dma("gpsimd", lambda e, tt=tt, xb=xb: e.dma_start(
                                    out=xf_sc[tt * 128:(tt + 1) * 128, :], in_=xfout[xb][:, :]),
                                    waits=[("v", xa)])
                    hbuf_war = S.c("pe")
                p2_v, p2_a, p2_pe = S.c("v"), S.c("a"), S.c("pe")
                p2_xf_st = [("dg0", S.c("dg0")), ("dg1", S.c("dg1"))]
                S.flush()

            with contextlib.ExitStack() as sQ:
                E3 = sQ.enter_context
                wattn_b = E3(nc.sbuf_tensor("wattn_b", [128, DC * 3 * C], BF16, side="right"))
                qkst = [E3(nc.sbuf_tensor(f"qkst{i}", [128, 512], BF16, side="right")) for i in range(4)]
                wst3 = [E3(nc.sbuf_tensor(f"wst3_{i}", [128, 3 * C], F32, side="right")) for i in range(2)]
                ps_qk = E3(nc.psum_tensor("ps_qk", [128, 2048], F32))
                ps_v = E3(nc.psum_tensor("ps_v", [128, 2048], F32))

                mc = S.op("tensor", lambda e: e.matmul(
                    ps_qk[:, 0:1], onesf_s[0:1, :], c11_s[0:1, :], start=True, stop=True),
                    waits=[("v", cv_c11), ("v", p2_v), ("a", p2_a), ("pe", p2_pe)]
                          + p2_xf_st,
                    inc="pe")
                ce = S.op("scalar", lambda e: e.activation(
                    cvec_s[:, :], ps_qk[:, 0:1], AF.Copy),
                    waits=[("pe", mc)], inc="a")
                cq = S.op("vector", lambda e: e.tensor_scalar(
                    cbq_s[:, :], bqk_s[:, 0:8], cvec_s[:, :], None, AluOpType.mult),
                    waits=[("a", ce)], inc="v")

                wat_cv = 0
                wat_cv_of = [0] * DC
                wst_war = [0, 0]
                for dc in range(DC):
                    wb = dc % 2
                    ld = S.dma("sync" if wb == 0 else "scalar",
                        lambda e, dc=dc, wb=wb: e.dma_start(
                        out=wst3[wb][:, :], in_=wattn[dc * 128:(dc + 1) * 128, :]),
                        waits=[("v", wst_war[wb])]
                              + ([("pe", p2_pe), ("v", p2_v), ("a", p2_a)] + p2_xf_st
                                 if dc < 2 else []))
                    wat_cv = S.op("vector", lambda e, dc=dc, wb=wb: e.tensor_copy(
                        wattn_b[:, dc * 3072:(dc + 1) * 3072], wst3[wb][:, :]),
                        waits=[ld], inc="v")
                    wat_cv_of[dc] = wat_cv
                    wst_war[wb] = wat_cv

                qk_rel = [("a", ce), ("v", 0), ("v", 0), ("v", 0)]
                qk_st_war = [("v", 0)] * 4
                store_of_fc = {}
                qk_st_fin = None
                for fc in range(16):
                    for tc in range(NTB):
                        bank = (fc * NTB + tc) % 4
                        lastm = 0
                        for dc in range(DC):
                            wts = [("v", wat_cv_of[dc])]
                            if dc == 0:
                                wts.append(qk_rel[bank])
                            lastm = S.op("tensor", lambda e, fc=fc, tc=tc, dc=dc, bank=bank: e.matmul(
                                ps_qk[:, bank * 512:(bank + 1) * 512],
                                wattn_b[:, dc * 3072 + fc * 128: dc * 3072 + (fc + 1) * 128],
                                xnT[:, dc * T + tc * 512: dc * T + (tc + 1) * 512],
                                start=(dc == 0), stop=(dc == DC - 1)),
                                waits=wts, inc="pe" if dc == DC - 1 else None)
                        stb = qkst[bank]
                        if fc < 8:
                            ev = S.op("vector", lambda e, stb=stb, bank=bank, fc=fc: e.tensor_scalar(
                                stb[:, :], ps_qk[:, bank * 512:(bank + 1) * 512],
                                cvec_s[:, :], cbq_s[:, fc:fc + 1],
                                AluOpType.mult, AluOpType.add),
                                waits=[("pe", lastm), ("v", cq), ("a", ce),
                                       qk_st_war[bank]], inc="v")
                            sto = S.dma("gpsimd", lambda e, stb=stb, fc=fc, tc=tc: e.dma_start(
                                out=q_sc[fc, :, tc * 512:(tc + 1) * 512], in_=stb[:, :]),
                                waits=[("v", ev)])
                        else:
                            ev = S.op("vector", lambda e, stb=stb, bank=bank, fc=fc: e.tensor_scalar(
                                stb[:, :], ps_qk[:, bank * 512:(bank + 1) * 512],
                                bqk_s[:, fc:fc + 1], None, AluOpType.add),
                                waits=[("pe", lastm), qk_st_war[bank]], inc="v")
                            sto = S.dma("gpsimd", lambda e, stb=stb, fc=fc, tc=tc: e.dma_start(
                                out=k_sc[fc - 8, :, tc * 512:(tc + 1) * 512], in_=stb[:, :]),
                                waits=[("v", ev)])
                        qk_st_war[bank] = sto
                        store_of_fc.setdefault(fc, []).append(sto)
                        store_of_fc[fc] = store_of_fc[fc][-2:]
                        qk_rel[bank] = ("v", ev)
                v_rel = [0, 0, 0, 0]
                for tt in range(NT):
                    for half in range(2):
                        bank = (tt * 2 + half) % 4
                        wts0 = [("a", v_rel[bank]), ("v", cv_bv)]
                        for dc in range(DC):
                            wts0dc = [("v", wat_cv_of[dc])] + (wts0 if dc == 0 else [])
                            S.op("tensor", lambda e, tt=tt, half=half, dc=dc, bank=bank: e.matmul(
                                ps_v[:, bank * 512:(bank + 1) * 512],
                                xnT[:, dc * T + tt * 128: dc * T + (tt + 1) * 128],
                                wattn_b[:, dc * 3072 + 2048 + half * 512: dc * 3072 + 2048 + (half + 1) * 512],
                                start=(dc == 0), stop=False),
                                waits=wts0dc)
                        lastm = S.op("tensor", lambda e, half=half, bank=bank: e.matmul(
                            ps_v[:, bank * 512:(bank + 1) * 512],
                            onesb_s[0:1, :], bv_b[0:1, half * 512:(half + 1) * 512],
                            start=False, stop=True), inc="pe")
                        ev = S.op("scalar", lambda e, tt=tt, half=half, bank=bank: e.activation(
                            Vt[:, tt * C + half * 512: tt * C + (half + 1) * 512],
                            ps_v[:, bank * 512:(bank + 1) * 512], AF.Copy),
                            waits=[("pe", lastm)], inc="a")
                        v_rel[bank] = ev
                qk_st_fin = [("dg0", S.c("dg0")), ("dg1", S.c("dg1"))]
                p3_a, p3_pe, p3_v = S.c("a"), S.c("pe"), S.c("v")
                S.flush()

        with contextlib.ExitStack() as s4:
            E4 = s4.enter_context
            ytile = E4(nc.sbuf_tensor("ytile", [128, NT * C], BF16, side="right"))
            s4b = contextlib.ExitStack()
            E4b = s4b.enter_context
            snegp = [E4b(nc.sbuf_tensor(f"snegp{p}", [128, 2 * T], F32, side="right")) for p in range(3)]
            sneg = [[snegp[p][:, j * T:(j + 1) * T] for j in range(2)] for p in range(3)]
            pbuf = [[E4b(nc.sbuf_tensor(f"pbuf{p}_{j}", [128, T], BF16, side="right")) for j in range(2)] for p in range(2)]
            ptb2 = [E4b(nc.sbuf_tensor(f"ptb2_{i}", [128, 1024], BF16, side="right")) for i in range(8)]
            ncmx = E4b(nc.sbuf_tensor("ncmx", [128, 24], F32, side="right"))
            negmax = E4b(nc.sbuf_tensor("negmax", [128, 6], F32, side="right"))
            sumr = E4b(nc.sbuf_tensor("sumr", [128, 6], F32, side="right"))
            rsum = E4b(nc.sbuf_tensor("rsum", [128, 6], F32, side="right"))
            ps_s = E4(nc.psum_tensor("ps_s", [128, 2048], F32))
            ps_tr = E4(nc.psum_tensor("ps_tr", [128, 1024], F32))
            ps_y = E4(nc.psum_tensor("ps_y", [128, 1024], F32))

            iters = [(hp, qt) for hp in range(H // 2) for qt in range(NT)]
            NIT = len(iters)
            sbank_rel = [0, 0]
            gchunk = [0]
            trslot = [("v", 0), ("v", 0)]
            ptb_war = [0] * 16
            exp_done = [0] * NIT
            mm_done = {}
            sneg_rel = [0, 0, 0]
            pbuf_war = [[0, 0], [0, 0]]
            ysc_done = {}
            av_of = [0] * NIT
            recip_done = [0] * NIT
            abc_war = [0, 0]
            qk_war = [0, 0]
            group_ld = [None, None]
            out_war = [None]

            def chunks_of(L):
                out = []
                c0 = 0
                while c0 < L:
                    w = min(512, L - c0)
                    out.append((c0, w))
                    c0 += w
                return out

            def emit_group_load(hp, early=False):
                hb = hp % 2
                if early:
                    wts = [wat_fin, qk_st_war[2], qk_st_war[3]]
                    wq = [wat_fin] + store_of_fc[0] + store_of_fc[8]
                else:
                    wts = [("v", abc_war[0])]
                    if hp == 0:
                        wts += [("pe", p3_pe), ("v", p3_v), ("a", p3_a)] + qk_st_fin
                    wq = [("pe", qk_war[hb])] if qk_war[hb] else []
                l1 = S.dma("sync", lambda e, hp=hp: e.dma_start(
                    out=abcp0[:, 0:T], in_=alibi_bc[2 * hp, :, :]), waits=wts)
                l2 = S.dma("sync", lambda e, hp=hp: e.dma_start(
                    out=abcp0[:, T:2 * T], in_=alibi_bc[2 * hp + 1, :, :]))
                l3 = S.dma("sync", lambda e, hp=hp, hb=hb: e.dma_start(
                    out=q_hp[hb][:, :], in_=q_sc[hp, :, :]), waits=wq)
                l4 = S.dma("sync", lambda e, hp=hp, hb=hb: e.dma_start(
                    out=k_hp[hb][:, :], in_=k_sc[hp, :, :]))
                group_ld[hb] = [l1, l2, l3, l4]

            def emit_scores_softmax(it):
                hp, qt = iters[it]
                L = (qt + 1) * 128
                par = it % 3
                ch = chunks_of(L)
                nch = len(ch)
                cd = 0
                pend_red = []

                def emit_red():
                    nonlocal cd
                    if not pend_red:
                        return
                    ci, c0, w, dep = pend_red.pop(0)
                    cd = S.op("vector", lambda e, par=par, c0=c0, w=w, ci=ci: e.tensor_reduce(
                        ncmx[:, par * 8:par * 8 + 8].rearrange("p (j c) -> p j c", j=2)[:, :, ci:ci + 1],
                        snegp[par].rearrange("p (j t) -> p j t", j=2)[:, :, c0:c0 + w],
                        AX.X, AluOpType.max),
                        waits=[("v", dep)], inc="v")

                for ci, (c0, w) in enumerate(ch):
                    cs = gchunk[0] % 2
                    gchunk[0] += 1
                    rA = cs * 1024
                    rB = cs * 1024 + 512
                    wts = []
                    if it == 0 and ci == 0:
                        wts += [("a", p3_a), ("v", p3_v)]
                    if sbank_rel[cs]:
                        wts.append(("v", sbank_rel[cs]))
                    hb = hp % 2
                    if qt == 0 and ci == 0:
                        wts += group_ld[hb][2:4]
                    S.op("tensor", lambda e, hb=hb, qt=qt, c0=c0, w=w, rA=rA: e.matmul(
                        ps_s[:, rA: rA + w],
                        q_hp[hb][0:64, qt * 128:(qt + 1) * 128],
                        k_hp[hb][0:64, c0:c0 + w],
                        start=True, stop=True, tile_position=(0, 0)),
                        waits=wts)
                    md = S.op("tensor", lambda e, hb=hb, qt=qt, c0=c0, w=w, rB=rB: e.matmul(
                        ps_s[:, rB: rB + w],
                        q_hp[hb][64:128, qt * 128:(qt + 1) * 128],
                        k_hp[hb][64:128, c0:c0 + w],
                        start=True, stop=True, tile_position=(64, 0)), inc="pe")
                    mm_done[(it, ci)] = md
                    qk_war[hb] = md
                    wts2 = [("pe", md)]
                    if qt == 0 and ci == 0:
                        wts2 += group_ld[hp % 2][0:2]
                    if sneg_rel[par]:
                        wts2.append(("a", sneg_rel[par]))
                    aoff = T - 128 + c0 - qt * 128
                    stt = S.op("vector", lambda e, par=par, hb=hp % 2, aoff=aoff, c0=c0, w=w, rA=rA: e.scalar_tensor_tensor(
                        snegp[par].rearrange("p (j t) -> p j t", j=2)[:, :, c0:c0 + w],
                        ps_s[:, rA:rA + 1024].rearrange("p (j t) -> p j t", j=2)[:, :, 0:w],
                        1.0,
                        abcp[hb].rearrange("p (j t) -> p j t", j=2)[:, :, aoff:aoff + w],
                        AluOpType.mult, AluOpType.add),
                        waits=wts2, inc="v")
                    abc_war[0] = S.c("v")
                    sbank_rel[cs] = stt
                    pend_red.append((ci, c0, w, stt))
                    if ci == nch - 1:
                        while pend_red:
                            emit_red()
                    elif len(pend_red) >= 2:
                        emit_red()
                nms = []
                for j in range(2):
                    nms.append(S.op("vector", lambda e, par=par, j=j, nch=nch: e.tensor_reduce(
                        negmax[:, par * 2 + j: par * 2 + j + 1],
                        ncmx[:, par * 8 + j * 4: par * 8 + j * 4 + nch], AX.X,
                        AluOpType.max, negate=True),
                        waits=[("v", cd), ("a", exp_done[it - 3] if it >= 3 else 0)], inc="v"))
                return nms

            def emit_exp(it, nms):
                hp, qt = iters[it]
                L = (qt + 1) * 128
                par = it % 3
                pb = it % 2
                ed = 0
                for j in range(2):
                    wts = [("v", nms[j])]
                    if it >= 2:
                        wts.append(("pe", pbuf_war[pb][j]))
                    if it >= 3:
                        wts.append(("v", recip_done[it - 3]))
                    ed = S.op("scalar", lambda e, par=par, pb=pb, j=j, L=L: e.activation(
                        pbuf[pb][j][:, 0:L], sneg[par][j][:, 0:L], AF.Exp,
                        bias=negmax[:, par * 2 + j: par * 2 + j + 1],
                        accum_out=sumr[:, par * 2 + j: par * 2 + j + 1]),
                        waits=wts, inc="a")
                exp_done[it] = ed
                sneg_rel[par] = ed

            def emit_recip(it):
                par = it % 3
                r = 0
                for j in range(2):
                    r = S.op("vector", lambda e, par=par, j=j: e.reciprocal(
                        rsum[:, par * 2 + j: par * 2 + j + 1],
                        sumr[:, par * 2 + j: par * 2 + j + 1]),
                        waits=[("a", exp_done[it]), ("a", ysc_done.get(it - 3, 0))], inc="v")
                recip_done[it] = r

            tr_evs = {}
            evac_cnt = [0]

            def emit_tr(it):
                hp, qt = iters[it]
                par = it % 2
                nkc = qt + 1
                nbat = (nkc + 3) // 4
                evs = []
                pend = []
                gb = [0]

                def flush_evac():
                    if not pend:
                        return
                    use_v = False
                    evac_cnt[0] += 1
                    eng, tag = ("vector", "v") if use_v else ("scalar", "a")

                    def copy_op(e, dst, src):
                        if use_v:
                            return e.tensor_copy(dst, src)
                        return e.activation(dst, src, AF.Copy)

                    if False and len(pend) == 2 and pend[0][2] == 0 and pend[1][2] == 1 and \
                       pend[0][4] % 2 == 0 and pend[0][4] + 1 == pend[1][4]:
                        (j0, k0, s0, kcs0, p0, lt0), (j1, k1, s1, kcs1, p1, lt1) = pend
                        wb1 = len(kcs1) * 128
                        ev = S.op(eng, lambda e, p0=p0, wb1=wb1: copy_op(
                            e, ptb2[p0 // 2][:, 0:512 + wb1], ps_tr[:, 0:512 + wb1]),
                            waits=[("pe", lt1)], inc=tag)
                        trslot[0] = (tag, ev)
                        trslot[1] = (tag, ev)
                        evs.append((j0, k0, (tag, ev), kcs0, p0))
                        evs.append((j1, k1, (tag, ev), kcs1, p1))
                    else:
                        for (j, kcb, slot, kcs, pslot, lt) in pend:
                            wb = len(kcs) * 128
                            ev = S.op(eng, lambda e, pslot=pslot, slot=slot, wb=wb: copy_op(
                                e, ptb2[pslot // 2][:, (pslot % 2) * 512: (pslot % 2) * 512 + wb],
                                ps_tr[:, slot * 512: slot * 512 + wb]),
                                waits=[("pe", lt)], inc=tag)
                            trslot[slot] = (tag, ev)
                            evs.append((j, kcb, (tag, ev), kcs, pslot))
                    pend.clear()

                for j in range(2):
                    lasttr = 0
                    for kcb in range(nbat):
                        slot = gb[0] % 2
                        gb[0] += 1
                        pslot = par * 8 + j * 4 + kcb
                        kcs = list(range(kcb * 4, min(kcb * 4 + 4, nkc)))
                        for idx, kc in enumerate(kcs):
                            wts = []
                            if idx == 0:
                                wts = [("a", exp_done[it]), trslot[slot],
                                       ("pe", ptb_war[pslot])]
                            lasttr = S.op("tensor", lambda e, par=par, j=j, kc=kc, slot=slot, idx=idx, n=len(kcs): e.matmul(
                                ps_tr[:, slot * 512 + (kc % 4) * 128: slot * 512 + (kc % 4) * 128 + 128],
                                pbuf[par][j][:, kc * 128:(kc + 1) * 128], ident_s[:, :],
                                start=(idx == 0), stop=(idx == n - 1)),
                                waits=wts, inc="pe" if idx == len(kcs) - 1 else None)
                        pend.append((j, kcb, slot, kcs, pslot, lasttr))
                        if slot == 1:
                            flush_evac()
                    pbuf_war[par][j] = lasttr
                flush_evac()
                tr_evs[it] = evs

            def emit_av(it):
                hp, qt = iters[it]
                ypar = it % 2
                nkc = qt + 1
                evs = tr_evs.pop(it)
                for (j, kcb, ev, kcs, pslot) in evs:
                    h = 2 * hp + j
                    for idx, kc in enumerate(kcs):
                        wts = [ev] if idx == 0 else []
                        if j == 0 and kc == 0:
                            wts = list(wts) + [("a", ysc_done.get(it - 2, 0))]
                        first = (j == 0 and kc == 0)
                        last = (j == 1 and kc == nkc - 1)
                        r = S.op("tensor", lambda e, pslot=pslot, ypar=ypar, j=j, kc=kc, h=h, first=first, last=last: e.matmul(
                            ps_y[:, ypar * 512 + j * 64: ypar * 512 + j * 64 + 64],
                            ptb2[pslot // 2][:, (pslot % 2) * 512 + (kc % 4) * 128: (pslot % 2) * 512 + (kc % 4) * 128 + 128],
                            Vt[:, kc * C + h * 64: kc * C + h * 64 + 64],
                            start=first, stop=last),
                            waits=wts, inc="pe" if last else None)
                        if last:
                            av_of[it] = r
                    ptb_war[pslot] = S.c("pe")

            def emit_ysc(it):
                hp, qt = iters[it]
                par = it % 3
                ypar = it % 2
                y = 0
                for j in range(2):
                    h = 2 * hp + j
                    wts_y = [("pe", av_of[it]), ("v", recip_done[it])]
                    y = S.op("scalar", lambda e, par=par, ypar=ypar, j=j, qt=qt, h=h: e.activation(
                        ytile[:, qt * C + h * 64: qt * C + (h + 1) * 64],
                        ps_y[:, ypar * 512 + j * 64: ypar * 512 + j * 64 + 64],
                        AF.Copy, scale=rsum[:, par * 2 + j: par * 2 + j + 1]),
                        waits=wts_y, inc="a")
                ysc_done[it] = y

            wpj_b = E4b(nc.sbuf_tensor("wpj_b", [128, DC * C], BF16, side="right"))
            yT_sb = E4b(nc.sbuf_tensor("yT_sb", [128, C], BF16, side="right"))
            xfst = E4b(nc.sbuf_tensor("xfst", [128, C], F32, side="right"))
            outsb = E4b(nc.sbuf_tensor("outsb", [128, C], F32, side="right"))
            q_hp = [E4b(nc.sbuf_tensor(f"q_hp{i}", [128, T], BF16, side="right")) for i in range(2)]
            k_hp = [E4b(nc.sbuf_tensor(f"k_hp{i}", [128, T], BF16, side="right")) for i in range(2)]
            abcp0 = E4b(nc.sbuf_tensor("abcp0", [128, 2 * T], F32, side="right"))
            abcp = [abcp0, abcp0]
            wat_fin = ("v", wat_cv)
            emit_group_load(0, early=(T >= 2048))
            pj_rel = [0]
            yT_war = [0]
            xf_war = [0]
            for cc in range(DC):
                S.dma("sync", lambda e, cc=cc: e.dma_start(
                    out=wpj_b[:, cc * 1024:(cc + 1) * 1024], in_=wpj_sc[cc, :, :]),
                    waits=conv_stores_fin + (qk_st_fin + [("pe", p3_pe), ("v", p3_v), ("a", p3_a)]
                                             if cc == 0 else []))
            wpj_ld = [("dl0", S.c("dl0")), ("dl1", S.c("dl1"))]

            def emit_proj_unit(qt, ydep):
                for b2 in range(2):
                    slot = b2
                    last = 0
                    for j in range(4):
                        cc = b2 * 4 + j
                        wts = []
                        if j == 0:
                            wts = [("a", ydep), trslot[slot]]
                        last = S.op("tensor", lambda e, qt=qt, cc=cc, slot=slot, j=j: e.matmul(
                            ps_tr[:, slot * 512 + j * 128: slot * 512 + (j + 1) * 128],
                            ytile[:, qt * C + cc * 128: qt * C + (cc + 1) * 128],
                            ident_s[:, :], start=(j == 0), stop=(j == 3)),
                            waits=wts, inc="pe" if j == 3 else None)
                    ev = S.op("vector", lambda e, b2=b2, slot=slot: e.tensor_copy(
                        yT_sb[:, b2 * 512:(b2 + 1) * 512],
                        ps_tr[:, slot * 512:(slot + 1) * 512]),
                        waits=[("pe", last), ("pe", yT_war[0])], inc="v")
                    trslot[slot] = ("v", ev)
                yT_done = S.c("v")
                xl = S.dma("sync", lambda e, qt=qt: e.dma_start(
                    out=xfst[:, :], in_=xf_sc[qt * 128:(qt + 1) * 128, :]),
                    waits=p2_xf_st + [("v", xf_war[0])])
                lastp = 0
                for cc in range(DC):
                    for half in range(2):
                        slot = half
                        wts = []
                        if cc == 0:
                            wts = wpj_ld + [("v", yT_done), trslot[slot],
                                            ("v", pj_rel[0]), trslot[1 - slot]]
                        lastp = S.op("tensor", lambda e, slot=slot, cc=cc, half=half: e.matmul(
                            ps_tr[:, slot * 512:(slot + 1) * 512],
                            yT_sb[:, cc * 128:(cc + 1) * 128],
                            wpj_b[:, cc * 1024 + half * 512: cc * 1024 + half * 512 + 512],
                            start=(cc == 0), stop=False),
                            waits=wts, inc="pe" if half == 1 else None)
                oas = []
                for half in range(2):
                    slot = half
                    lastb = S.op("tensor", lambda e, slot=slot, half=half: e.matmul(
                        ps_tr[:, slot * 512:(slot + 1) * 512], onesb_s[0:1, :],
                        bpj_b[0:1, half * 512:(half + 1) * 512],
                        start=False, stop=True),
                        waits=[("v", cv_bp)], inc="pe")
                    oa = S.op("vector", lambda e, slot=slot, half=half: e.tensor_tensor(
                        outsb[:, half * 512:(half + 1) * 512],
                        ps_tr[:, slot * 512:(slot + 1) * 512],
                        xfst[:, half * 512:(half + 1) * 512], AluOpType.add),
                        waits=[("pe", lastb), xl]
                              + ([out_war[0]] if out_war[0] else []), inc="v")
                    pj_rel[0] = oa
                    trslot[slot] = ("v", oa)
                yT_war[0] = S.c("pe")
                xf_war[0] = pj_rel[0]
                out_war[0] = S.dma("gpsimd", lambda e, qt=qt: e.dma_start(
                    out=out_d[qt * 128:(qt + 1) * 128, :], in_=outsb[:, :]),
                    waits=[("v", pj_rel[0])])

            nm_of = {}
            NHP = H // 2
            for step in range(NIT + 3):
                if step < NIT:
                    hp, qt = iters[step]
                    if qt == 0 and hp > 0:
                        emit_group_load(hp)
                    nm_of[step] = emit_scores_softmax(step)
                if 3 <= step <= NIT + 2:
                    emit_ysc(step - 3)
                if 1 <= step <= NIT:
                    emit_recip(step - 1)
                    emit_tr(step - 1)
                if 2 <= step <= NIT + 1:
                    emit_av(step - 2)
                if step < NIT:
                    emit_exp(step, nm_of[step])
                if 3 <= step <= NIT + 2:
                    it3 = step - 3
                    hp3, qt3 = iters[it3]
                    if hp3 == NHP - 1:
                        emit_proj_unit(qt3, ysc_done[it3])
            S.flush()
            s4b.close()

            fin = [("dg0", S.c("dg0")), ("dg1", S.c("dg1"))]
            S.op("gpsimd", lambda e: None, waits=fin)
            S.op("sync", lambda e: None, waits=fin)
            S.flush()

    return nc



def _consts(T):
    start = 2.0 ** (-(2.0 ** -(math.log2(H) - 3)))
    slopes = np.array([start * start ** i for i in range(H)], dtype=np.float32)
    d_rel = (np.arange(T, dtype=np.float32)[None, :] + np.float32(128 - T)
             - np.arange(128, dtype=np.float32)[:, None])
    alibi_bc = np.ascontiguousarray(slopes[:, None, None] * d_rel[None, :, :])
    alibi_bc[:, d_rel > 0] = np.float32(-1e38)
    return {
        "ident": np.eye(128, dtype=BF),
        "alibi_bc": alibi_bc,
        "ones_b": np.ones((1, 128), dtype=BF),
        "ones_f": np.ones((1, 128), dtype=np.float32),
    }


def make_inputs(xb, W_attn, b_attn, W_proj, b_proj, temperature, wg, wu, wd, T):
    con = _consts(T)
    wgr = np.ascontiguousarray(
        wg.reshape(HCP, 2, 128, DC, 128).transpose(0, 4, 1, 3, 2).reshape(HCP, 128, 2048))
    wur = np.ascontiguousarray(
        wu.reshape(HCP, 2, 128, DC, 128).transpose(0, 4, 1, 3, 2).reshape(HCP, 128, 2048))
    wdr = np.ascontiguousarray(
        wd.T.reshape(HCP, 2, 128, C).transpose(0, 2, 1, 3).reshape(HCP, 128, 2048))
    d = {
        "xin": np.ascontiguousarray(xb),
        "wattn": np.ascontiguousarray(W_attn),
        "battn_qk": np.ascontiguousarray(b_attn[:2048].reshape(16, 128).T),
        "battn_v": np.ascontiguousarray(b_attn[2048:].reshape(1, C)),
        "wproj": np.ascontiguousarray(W_proj),
        "bproj": np.ascontiguousarray(b_proj.reshape(1, C)),
        "temp": np.ascontiguousarray(temperature.reshape(1, 1)),
        "wg_r": wgr,
        "wu_r": wur,
        "wd_r": wdr,
    }
    d.update(con)
    return d


def kernel(L0_attn_x, L0_attn_W_attn, L0_attn_b_attn, L0_attn_W_proj,
           L0_attn_b_proj, L0_attn_temperature, L0_ffn_w_gate, L0_ffn_w_up,
           L0_ffn_w_down):
    T = int(L0_attn_x.shape[1])
    x = np.ascontiguousarray(np.asarray(L0_attn_x, dtype=np.float32))
    W_attn = np.asarray(L0_attn_W_attn, dtype=np.float32)
    b_attn = np.asarray(L0_attn_b_attn, dtype=np.float32)
    W_proj = np.asarray(L0_attn_W_proj, dtype=np.float32)
    b_proj = np.asarray(L0_attn_b_proj, dtype=np.float32)
    tempv = np.asarray(L0_attn_temperature, dtype=np.float32)
    wg = np.asarray(L0_ffn_w_gate, dtype=np.int32)
    wu = np.asarray(L0_ffn_w_up, dtype=np.int32)
    wd = np.asarray(L0_ffn_w_down, dtype=np.int32)

    nc = build_program(T)
    in_maps = [make_inputs(x[b], W_attn, b_attn, W_proj, b_proj, tempv, wg, wu, wd, T)
               for b in range(x.shape[0])]
    res = run_bass_kernel_spmd(nc, in_maps, core_ids=list(range(x.shape[0])))
    return np.stack([r["out"] for r in res.results], axis=0)


# version 36
# speedup vs baseline: 1.0555x; 1.0268x over previous
import contextlib
import math
import numpy as np
import ml_dtypes
import concourse.bass as bass
import concourse.mybir as mybir
from concourse.bass_utils import run_bass_kernel_spmd
from concourse.alu_op_type import AluOpType

F32 = mybir.dt.float32
BF16 = mybir.dt.bfloat16
I32 = mybir.dt.int32
AF = mybir.ActivationFunctionType
AX = mybir.AxisListType
BF = ml_dtypes.bfloat16

NCORES = 8
C = 1024
H = 16
D = 64
HID = 3072
DC = C // 128
HCN = HID // 128
HCP = HCN // 2

ENGS = ("sync", "gpsimd", "vector", "scalar", "tensor")


class KB:

    QSEMS = {"sync": ("dl0", "dl1"), "gpsimd": ("dg0", "dg1")}

    def __init__(self, nc, block, sem):
        self.nc = nc
        self.block = block
        self.sem = sem
        self.cnt = {k: 0 for k in sem}
        self.pend = []
        self.waited = {e: {} for e in ENGS}
        self.qidx = {"sync": 0, "gpsimd": 0}

    def dma(self, queue, fn, waits=()):
        sems = self.QSEMS[queue]
        i = self.qidx[queue]
        self.qidx[queue] ^= 1
        s = sems[i]
        auto = [(s, self.cnt[s])]
        cnt = self.op(queue, fn, waits=list(waits) + auto, inc=s, inc_n=16)
        return (s, cnt)

    def op(self, eng, fn, waits=(), inc=None, inc_n=1):
        waits = [(s, v) for (s, v) in waits if v and v > 0]
        incs = []
        if inc is not None:
            self.cnt[inc] += inc_n
            incs.append((inc, inc_n))
        self.pend.append((eng, fn, waits, incs))
        return self.cnt[inc] if inc is not None else 0

    def c(self, s):
        return self.cnt[s]

    def flush(self):
        for eng in ENGS:
            ops = [(fn, w, i) for (e, fn, w, i) in self.pend if e == eng]
            if not ops:
                continue
            waited = self.waited[eng]
            sem = self.sem

            def section(esec, ops=ops, waited=waited):
                for fn, waits, incs in ops:
                    for s, v in waits:
                        if v > waited.get(s, 0):
                            esec.wait_ge(sem[s], v)
                            waited[s] = v
                    instr = fn(esec)
                    for s, n in incs:
                        instr.then_inc(sem[s], n)

            getattr(self.block, eng)(section)
        self.pend = []


def build_program(T):
    NT = T // 128
    NTB = T // 512
    nc = bass.Bass("TRN2", target_bir_lowering=False)

    def din(name, shape, dt):
        return nc.dram_tensor(name, shape, dt, kind="ExternalInput").ap()

    xin = din("xin", [T, C], F32)
    wattn = din("wattn", [C, 3 * C], F32)
    battn_qk = din("battn_qk", [128, 16], F32)
    battn_v = din("battn_v", [1, C], F32)
    wproj = din("wproj", [C, C], F32)
    bproj = din("bproj", [1, C], F32)
    temp = din("temp", [1, 1], F32)
    wg_r = din("wg_r", [HCP, 128, 2048], I32)
    wu_r = din("wu_r", [HCP, 128, 2048], I32)
    wd_r = din("wd_r", [HCP, 128, 2048], I32)
    ident = din("ident", [128, 128], BF16)
    alibi_bc = din("alibi_bc", [H, 128, T], F32)
    ones_b = din("ones_b", [1, 128], BF16)
    ones_f = din("ones_f", [1, 128], F32)

    out_d = nc.dram_tensor("out", [T, C], F32, kind="ExternalOutput").ap()
    wg_sc = nc.dram_tensor("wg_sc", [HCP, 128, 2048], BF16, kind="Internal").ap()
    wu_sc = nc.dram_tensor("wu_sc", [HCP, 128, 2048], BF16, kind="Internal").ap()
    wd_sc = nc.dram_tensor("wd_sc", [HCP, 128, 2048], BF16, kind="Internal").ap()
    xf_sc = nc.dram_tensor("xf_sc", [T, C], F32, kind="Internal").ap()
    q_sc = nc.dram_tensor("q_sc", [DC, 128, T], BF16, kind="Internal").ap()
    wpj_sc = nc.dram_tensor("wpj_sc", [DC, 128, C], BF16, kind="Internal").ap()
    k_sc = nc.dram_tensor("k_sc", [DC, 128, T], BF16, kind="Internal").ap()

    with contextlib.ExitStack() as st:
        E = st.enter_context
        sem = {s: E(nc.semaphore("sem_" + s)) for s in
               ("dl0", "dl1", "dg0", "dg1", "pe", "v", "a")}
        block = E(nc.Block())
        S = KB(nc, block, sem)

        ident_s = E(nc.sbuf_tensor("ident_s", [128, 128], BF16))
        onesb_s = E(nc.sbuf_tensor("onesb_s", [1, 128], BF16))
        onesf_s = E(nc.sbuf_tensor("onesf_s", [1, 128], F32))
        bqk_s = E(nc.sbuf_tensor("bqk_s", [128, 16], F32))
        bv_b = E(nc.sbuf_tensor("bv_b", [1, C], BF16))
        bpj_b = E(nc.sbuf_tensor("bpj_b", [1, C], BF16))
        temp_s = E(nc.sbuf_tensor("temp_s", [1, 1], F32))
        t8_s = E(nc.sbuf_tensor("t8_s", [1, 1], F32))
        c11_s = E(nc.sbuf_tensor("c11_s", [1, 1], F32))
        cvec_s = E(nc.sbuf_tensor("cvec_s", [128, 1], F32))
        cbq_s = E(nc.sbuf_tensor("cbq_s", [128, 8], F32))
        Vt = E(nc.sbuf_tensor("Vt", [128, NT * C], BF16))

        cl = []
        for dst, src in [(ident_s, ident),
                         (onesb_s, ones_b), (onesf_s, ones_f),
                         (bqk_s, battn_qk), (temp_s, temp)]:
            cl.append(S.dma("sync", lambda e, d=dst, s=src: e.dma_start(out=d[:, :], in_=s)))
        c_consts = [cl[-1], cl[-2]]
        cv_t8 = S.op("vector", lambda e: e.tensor_scalar(
            t8_s[0:1, :], temp_s[0:1, :], 8.0, None, AluOpType.mult),
            waits=c_consts, inc="v")
        cv_c11 = S.op("vector", lambda e: e.reciprocal(c11_s[0:1, :], t8_s[0:1, :]),
                      waits=[("v", cv_t8)], inc="v")

        with contextlib.ExitStack() as sA:
            xnT = sA.enter_context(nc.sbuf_tensor("xnT", [128, DC * T], BF16, side="right"))
            wdres = sA.enter_context(nc.sbuf_tensor("wdres", [128, HCP * 2048], BF16, side="right"))

            with contextlib.ExitStack() as sB:
                E1 = sB.enter_context
                xstage = [E1(nc.sbuf_tensor(f"xstage{i}", [128, C], F32, side="right")) for i in range(4)]
                xn_st = [E1(nc.sbuf_tensor(f"xn_st{i}", [128, C], BF16, side="right")) for i in range(4)]
                sq_sc = [E1(nc.sbuf_tensor(f"sq_sc{i}", [128, C], F32, side="right")) for i in range(2)]
                wistage = [E1(nc.sbuf_tensor(f"wistage{i}", [128, 2048], I32, side="right")) for i in range(3)]
                cbuf = [E1(nc.sbuf_tensor(f"cbuf{i}", [128, 2048], BF16, side="right")) for i in range(2)]
                cbuf32 = [E1(nc.sbuf_tensor(f"cbuf32_{i}", [128, 2048], F32, side="right")) for i in range(2)]
                stat = {k: [E1(nc.sbuf_tensor(f"st_{k}{i}", [128, 1], F32, side="right")) for i in range(4)]
                        for k in ("s1", "s2", "m", "v", "r", "n")}
                bst_f = E1(nc.sbuf_tensor("bst_f", [1, C], F32, side="right"))
                wpst = E1(nc.sbuf_tensor("wpst", [128, C], F32, side="right"))
                ps1 = E1(nc.psum_tensor("ps1", [128, 4096], F32))

                bld1 = S.dma("sync", lambda e: e.dma_start(out=bst_f[0:1, 0:C], in_=battn_v))
                cv_bv = S.op("vector", lambda e: e.tensor_copy(bv_b[0:1, :], bst_f[0:1, 0:C]),
                             waits=[bld1], inc="v")
                bld2 = S.dma("sync", lambda e: e.dma_start(out=bst_f[0:1, 0:C], in_=bproj),
                             waits=[("v", cv_bv)])
                cv_bp = S.op("vector", lambda e: e.tensor_copy(bpj_b[0:1, :], bst_f[0:1, 0:C]),
                             waits=[bld2], inc="v")

                conv_tasks = []
                for m, src, dst_sc, width in (
                        [(0, wg_r, wg_sc, 2048), (1, wu_r, wu_sc, 2048),
                         (2, wd_r, None, 2048)]):
                    for j in range(HCP):
                        conv_tasks.append((src, dst_sc, j, width))
                for j in range(DC):
                    conv_tasks.append(("wproj", wpj_sc, j, 1024))
                n_tasks = len(conv_tasks)
                conv_i = [0]
                cbuf_store = [None, None, None]
                wist_conv = [0, 0, 0]
                c32_war = [0, 0, 0]
                wpst_war = [0]
                conv_stores = []

                def emit_conv(k):
                    src, dst_sc, j, width = conv_tasks[k]
                    bi = k % 3
                    b = k % 2
                    lq = "sync"
                    is_pj = (src == "wproj")
                    is_wd = dst_sc is None
                    if is_pj:
                        ld = S.dma("sync", lambda e, j=j: e.dma_start(
                            out=wpst[:, :], in_=wproj[j * 128:(j + 1) * 128, :]),
                            waits=[("a", wpst_war[0])])
                        cv = S.op("scalar", lambda e, b=b: e.activation(
                            cbuf[b][:, 0:1024], wpst[:, :], AF.Copy),
                            waits=[ld] + ([cbuf_store[b]] if cbuf_store[b] else []), inc="a")
                        wpst_war[0] = cv
                    else:
                        ld = S.dma(lq, lambda e, src=src, j=j, bi=bi, width=width: e.dma_start(
                            out=wistage[bi][:, 0:width], in_=src[j, :, :]),
                            waits=[("v", wist_conv[bi])])
                        cv0 = S.op("vector", lambda e, b=b, bi=bi, width=width: e.tensor_scalar(
                            cbuf32[b][:, 0:width], wistage[bi][:, 0:width], 1.0,
                            None, AluOpType.mult),
                            waits=[ld, ("a", c32_war[b])]
                                  + ([cbuf_store[b]] if cbuf_store[b] else []), inc="v")
                        cdst = (lambda e, j=j: wdres[:, j * 2048:(j + 1) * 2048]) if is_wd \
                            else (lambda e, b=b, width=width: cbuf[b][:, 0:width])
                        cv = S.op("scalar", lambda e, b=b, width=width, cdst=cdst: e.activation(
                            cdst(e), cbuf32[b][:, 0:width], AF.Copy, scale=1.0 / 127.0),
                            waits=[("v", cv0)], inc="a")
                        wist_conv[bi] = cv0
                        c32_war[b] = cv
                    if is_wd:
                        return
                    sto = S.dma("gpsimd", lambda e, dst_sc=dst_sc, j=j, b=b, width=width: e.dma_start(
                        out=dst_sc[j, :, :], in_=cbuf[b][:, 0:width]),
                        waits=[("a", cv)])
                    cbuf_store[b] = sto
                    conv_stores.append(sto)

                ln_xn = [0] * NT
                stile_tr = [0] * (NT // 4)
                tr_evac = [[(None, 0)] * DC for _ in range(NT // 4)]
                prev_xuse = [0, 0, 0, 0]
                stat_war = [0, 0, 0, 0]
                sq_war = [0, 0, 0, 0]
                for i in range(NT):
                    for _ in range(3):
                        if conv_i[0] < n_tasks:
                            emit_conv(conv_i[0])
                            conv_i[0] += 1
                    b = i % 4
                    p = i % 4
                    ld = S.dma("sync", lambda e, i=i, b=b: e.dma_start(
                        out=xstage[b][:, :], in_=xin[i * 128:(i + 1) * 128, :]),
                        waits=[("v", prev_xuse[b])])
                    rs = S.op("vector", lambda e, b=b, p=p: e.tensor_reduce(
                        stat["s1"][p][:, :], xstage[b][:, :], AX.X, AluOpType.add),
                        waits=[ld, ("v", stat_war[p])], inc="v")
                    sq = S.op("scalar", lambda e, b=b, p=p: e.activation(
                        sq_sc[p % 2][:, :], xstage[b][:, :], AF.Square,
                        accum_out=stat["s2"][p][:, :]),
                        waits=[ld, ("v", sq_war[p]), ("v", sq_war[(p + 2) % 4])], inc="a")
                    mn = S.op("vector", lambda e, p=p: e.tensor_scalar(
                        stat["m"][p][:, :], stat["s1"][p][:, :], 1.0 / C, None,
                        AluOpType.mult), waits=[("v", rs)], inc="v")
                    v1 = S.op("vector", lambda e, p=p: e.tensor_tensor(
                        stat["v"][p][:, :], stat["m"][p][:, :], stat["m"][p][:, :],
                        AluOpType.mult), waits=[("v", mn), ("a", sq)], inc="v")
                    v2 = S.op("vector", lambda e, p=p: e.tensor_scalar(
                        stat["s2"][p][:, :], stat["s2"][p][:, :], 1.0 / C, 1e-5,
                        AluOpType.mult, AluOpType.add), waits=[("a", sq)], inc="v")
                    v3 = S.op("vector", lambda e, p=p: e.tensor_tensor(
                        stat["v"][p][:, :], stat["s2"][p][:, :], stat["v"][p][:, :],
                        AluOpType.subtract), waits=[("v", v2)], inc="v")
                    sq_war[p] = v3
                    sd = S.op("scalar", lambda e, p=p: e.activation(
                        stat["v"][p][:, :], stat["v"][p][:, :], AF.Sqrt),
                        waits=[("v", v3)], inc="a")
                    rc = S.op("vector", lambda e, p=p: e.reciprocal(
                        stat["r"][p][:, :], stat["v"][p][:, :]),
                        waits=[("a", sd)], inc="v")
                    ng = S.op("vector", lambda e, p=p: e.tensor_scalar(
                        stat["n"][p][:, :], stat["m"][p][:, :], stat["r"][p][:, :],
                        -1.0, AluOpType.mult, AluOpType.mult),
                        waits=[("v", rc)], inc="v")
                    stile = i // 4
                    xn = S.op("vector", lambda e, i=i, b=b, p=p: e.tensor_scalar(
                        xn_st[i % 4][:, :], xstage[b][:, :], stat["r"][p][:, :],
                        stat["n"][p][:, :], AluOpType.mult, AluOpType.add),
                        waits=[("v", ng),
                               ("pe", stile_tr[stile - 1] if stile >= 1 else 0)],
                        inc="v")
                    prev_xuse[b] = xn
                    stat_war[p] = xn
                    ln_xn[i] = xn
                    if i % 4 == 3:
                        last = 0
                        for dc in range(DC):
                            for j in range(4):
                                wts = [("v", ln_xn[stile * 4 + j])]
                                if stile >= 1:
                                    es, ev = tr_evac[stile - 1][dc]
                                    wts.append((es, ev))
                                last = S.op("tensor", lambda e, dc=dc, j=j, stile=stile: e.matmul(
                                    ps1[:, dc * 512 + j * 128: dc * 512 + (j + 1) * 128],
                                    xn_st[(stile * 4 + j) % 4][:, dc * 128:(dc + 1) * 128],
                                    ident_s[:, :], start=(j == 0), stop=(j == 3)),
                                    waits=wts, inc="pe" if j == 3 else None)
                            if dc % 2 == 0:
                                ev = S.op("vector", lambda e, dc=dc, stile=stile: e.tensor_copy(
                                    xnT[:, dc * T + stile * 512: dc * T + (stile + 1) * 512],
                                    ps1[:, dc * 512:(dc + 1) * 512]),
                                    waits=[("pe", last)], inc="v")
                                tr_evac[stile][dc] = ("v", ev)
                            else:
                                ev = S.op("scalar", lambda e, dc=dc, stile=stile: e.activation(
                                    xnT[:, dc * T + stile * 512: dc * T + (stile + 1) * 512],
                                    ps1[:, dc * 512:(dc + 1) * 512], AF.Copy),
                                    waits=[("pe", last)], inc="a")
                                tr_evac[stile][dc] = ("a", ev)
                        stile_tr[stile] = last
                while conv_i[0] < n_tasks:
                    emit_conv(conv_i[0])
                    conv_i[0] += 1
                conv_stores_fin = [("dg0", S.c("dg0")), ("dg1", S.c("dg1"))]
                p1_v, p1_a, p1_pe = S.c("v"), S.c("a"), S.c("pe")
                S.flush()

            with contextlib.ExitStack() as sF:
                E2 = sF.enter_context
                wgbuf = [E2(nc.sbuf_tensor(f"wgbuf{i}", [128, 2048], BF16, side="right")) for i in range(2)]
                wubuf = [E2(nc.sbuf_tensor(f"wubuf{i}", [128, 2048], BF16, side="right")) for i in range(2)]
                hbuf = E2(nc.sbuf_tensor("hbuf", [128, HCN * 512], BF16, side="right"))
                gs_s = [E2(nc.sbuf_tensor(f"gs_s{i}", [128, 512], BF16, side="right")) for i in range(2)]
                us_s = [E2(nc.sbuf_tensor(f"us_s{i}", [128, 512], BF16, side="right")) for i in range(2)]
                x2st = [E2(nc.sbuf_tensor("x2st0", [128, C], F32, side="right"))] * 2
                xfout = [E2(nc.sbuf_tensor(f"xfout{i}", [128, C], F32, side="right")) for i in range(2)]
                ps_g = E2(nc.psum_tensor("ps_g", [128, 1024], F32))
                ps_u = E2(nc.psum_tensor("ps_u", [128, 1024], F32))
                ps_dn = E2(nc.psum_tensor("ps_dn", [128, 2048], F32))

                wgu_ld = {}
                wgld_war = [0, 0]
                grel = [p1_a, p1_a]
                urel = [p1_a, p1_a]
                gs_war = [0, 0]
                dnrel = [0, 0, 0, 0]
                hbuf_war = 0
                x2_war = [0, 0]
                xf_war = [None, None]
                for tb in range(NTB):
                    for hcp in range(HCP):
                        b = hcp % 2
                        wts = conv_stores_fin + [("pe", wgld_war[b])]
                        if tb == 0 and hcp == 0:
                            wts += [("pe", p1_pe), ("v", p1_v), ("a", p1_a)]
                        ldg = S.dma("sync", lambda e, hcp=hcp, b=b: e.dma_start(
                            out=wgbuf[b][:, :], in_=wg_sc[hcp, :, :]), waits=wts)
                        ldu = S.dma("sync", lambda e, hcp=hcp, b=b: e.dma_start(
                            out=wubuf[b][:, :], in_=wu_sc[hcp, :, :]))
                        wgu_ld[(tb, hcp)] = [ldg, ldu]
                        for hc2 in range(2):
                            hc = hcp * 2 + hc2
                            gp = hc % 2
                            wts_g = [ldg, ldu, ("a", grel[gp])]
                            if tb == 0 and hc == 0:
                                wts_g += [("v", p1_v), ("a", p1_a), ("pe", p1_pe)]
                            lg = lu = 0
                            for dc in range(DC):
                                lg = S.op("tensor", lambda e, hc2=hc2, dc=dc, gp=gp, b=b, tb=tb: e.matmul(
                                    ps_g[:, gp * 512:(gp + 1) * 512],
                                    wgbuf[b][:, hc2 * 1024 + dc * 128: hc2 * 1024 + (dc + 1) * 128],
                                    xnT[:, dc * T + tb * 512: dc * T + tb * 512 + 512],
                                    start=(dc == 0), stop=(dc == DC - 1)),
                                    waits=wts_g if dc == 0 else (),
                                    inc="pe" if dc == DC - 1 else None)
                            for dc in range(DC):
                                lu = S.op("tensor", lambda e, hc2=hc2, dc=dc, gp=gp, b=b, tb=tb: e.matmul(
                                    ps_u[:, gp * 512:(gp + 1) * 512],
                                    wubuf[b][:, hc2 * 1024 + dc * 128: hc2 * 1024 + (dc + 1) * 128],
                                    xnT[:, dc * T + tb * 512: dc * T + tb * 512 + 512],
                                    start=(dc == 0), stop=(dc == DC - 1)),
                                    waits=[("a", urel[gp])] if dc == 0 else (),
                                    inc="pe" if dc == DC - 1 else None)
                            wgld_war[b] = lu
                            sl = S.op("scalar", lambda e, gp=gp: e.activation(
                                gs_s[gp][:, :], ps_g[:, gp * 512:(gp + 1) * 512], AF.Silu),
                                waits=[("pe", lg), ("v", gs_war[gp])], inc="a")
                            grel[gp] = sl
                            ue = S.op("scalar", lambda e, gp=gp: e.activation(
                                us_s[gp][:, :], ps_u[:, gp * 512:(gp + 1) * 512], AF.Copy),
                                waits=[("pe", lu), ("v", gs_war[gp])], inc="a")
                            urel[gp] = ue
                            wts_m = [("a", ue)]
                            if tb > 0 and hc == 0:
                                wts_m.append(("pe", hbuf_war))
                            ml = S.op("vector", lambda e, hc=hc, gp=gp: e.tensor_tensor(
                                hbuf[:, hc * 512:(hc + 1) * 512], gs_s[gp][:, :],
                                us_s[gp][:, :], AluOpType.mult),
                                waits=wts_m, inc="v")
                            gs_war[gp] = ml
                    mults_done = S.c("v")
                    for pas in range(2):
                        lastd = [0] * 4
                        for hcp2 in range(HCN // 2):
                            for hc2 in range(2):
                                hc = 2 * hcp2 + hc2
                                for tsh in range(4):
                                    ts = pas * 2 + tsh // 2
                                    half = tsh % 2
                                    wts_d = []
                                    if hc == 0:
                                        wts_d += [("v", mults_done), ("v", dnrel[tsh])]
                                    r = S.op("tensor", lambda e, hc=hc, hcp2=hcp2, hc2=hc2, ts=ts, half=half, tsh=tsh: e.matmul(
                                        ps_dn[:, tsh * 512:(tsh + 1) * 512],
                                        hbuf[:, hc * 512 + ts * 128: hc * 512 + ts * 128 + 128],
                                        wdres[:, hcp2 * 2048 + hc2 * 1024 + half * 512: hcp2 * 2048 + hc2 * 1024 + (half + 1) * 512],
                                        start=(hc == 0), stop=(hc == HCN - 1)),
                                        waits=wts_d, inc="pe" if tsh == 3 else None)
                                    if tsh == 3:
                                        if hc == HCN - 1:
                                            lastd = [r] * 4
                        for tsh in range(4):
                            ts = pas * 2 + tsh // 2
                            half = tsh % 2
                            tt = tb * 4 + ts
                            xb = ts % 2
                            if half == 0:
                                wts_x2 = [("v", x2_war[xb])]
                                if tb == 0 and pas == 0 and tsh == 0:
                                    wts_x2 += [("pe", p1_pe), ("v", p1_v), ("a", p1_a)] + conv_stores_fin
                                x2_ld = S.dma("gpsimd", lambda e, tt=tt, xb=xb: e.dma_start(
                                    out=x2st[xb][:, :], in_=xin[tt * 128:(tt + 1) * 128, :]),
                                    waits=wts_x2)
                            xa = S.op("vector", lambda e, xb=xb, tsh=tsh, half=half: e.tensor_tensor(
                                xfout[xb][:, half * 512:(half + 1) * 512],
                                ps_dn[:, tsh * 512:(tsh + 1) * 512],
                                x2st[xb][:, half * 512:(half + 1) * 512], AluOpType.add),
                                waits=[("pe", lastd[tsh]), x2_ld]
                                      + ([xf_war[xb]] if xf_war[xb] else []), inc="v")
                            dnrel[tsh] = xa
                            if half == 1:
                                x2_war[xb] = xa
                                xf_war[xb] = S.dma("gpsimd", lambda e, tt=tt, xb=xb: e.dma_start(
                                    out=xf_sc[tt * 128:(tt + 1) * 128, :], in_=xfout[xb][:, :]),
                                    waits=[("v", xa)])
                    hbuf_war = S.c("pe")
                p2_v, p2_a, p2_pe = S.c("v"), S.c("a"), S.c("pe")
                p2_xf_st = [("dg0", S.c("dg0")), ("dg1", S.c("dg1"))]
                S.flush()

            with contextlib.ExitStack() as sQ:
                E3 = sQ.enter_context
                wattn_b = E3(nc.sbuf_tensor("wattn_b", [128, DC * 3 * C], BF16, side="right"))
                qkst = [E3(nc.sbuf_tensor(f"qkst{i}", [128, 512], BF16, side="right")) for i in range(4)]
                wst3 = [E3(nc.sbuf_tensor(f"wst3_{i}", [128, 3 * C], F32, side="right")) for i in range(2)]
                ps_qk = E3(nc.psum_tensor("ps_qk", [128, 2048], F32))
                ps_v = E3(nc.psum_tensor("ps_v", [128, 2048], F32))

                mc = S.op("tensor", lambda e: e.matmul(
                    ps_qk[:, 0:1], onesf_s[0:1, :], c11_s[0:1, :], start=True, stop=True),
                    waits=[("v", cv_c11), ("v", p2_v), ("a", p2_a), ("pe", p2_pe)]
                          + p2_xf_st,
                    inc="pe")
                ce = S.op("scalar", lambda e: e.activation(
                    cvec_s[:, :], ps_qk[:, 0:1], AF.Copy),
                    waits=[("pe", mc)], inc="a")
                cq = S.op("vector", lambda e: e.tensor_scalar(
                    cbq_s[:, :], bqk_s[:, 0:8], cvec_s[:, :], None, AluOpType.mult),
                    waits=[("a", ce)], inc="v")

                wat_cv = 0
                wat_cv_of = [0] * DC
                wst_war = [0, 0]
                for dc in range(DC):
                    wb = dc % 2
                    ld = S.dma("sync", lambda e, dc=dc, wb=wb: e.dma_start(
                        out=wst3[wb][:, :], in_=wattn[dc * 128:(dc + 1) * 128, :]),
                        waits=[("v", wst_war[wb])]
                              + ([("pe", p2_pe), ("v", p2_v), ("a", p2_a)] + p2_xf_st
                                 if dc < 2 else []))
                    wat_cv = S.op("vector", lambda e, dc=dc, wb=wb: e.tensor_copy(
                        wattn_b[:, dc * 3072:(dc + 1) * 3072], wst3[wb][:, :]),
                        waits=[ld], inc="v")
                    wat_cv_of[dc] = wat_cv
                    wst_war[wb] = wat_cv

                qk_rel = [("a", ce), ("v", 0), ("v", 0), ("v", 0)]
                qk_st_war = [("v", 0)] * 4
                store_of_fc = {}
                qk_st_fin = None
                for fc in range(16):
                    for tc in range(NTB):
                        bank = (fc * NTB + tc) % 4
                        lastm = 0
                        for dc in range(DC):
                            wts = [("v", wat_cv_of[dc])]
                            if dc == 0:
                                wts.append(qk_rel[bank])
                            lastm = S.op("tensor", lambda e, fc=fc, tc=tc, dc=dc, bank=bank: e.matmul(
                                ps_qk[:, bank * 512:(bank + 1) * 512],
                                wattn_b[:, dc * 3072 + fc * 128: dc * 3072 + (fc + 1) * 128],
                                xnT[:, dc * T + tc * 512: dc * T + (tc + 1) * 512],
                                start=(dc == 0), stop=(dc == DC - 1)),
                                waits=wts, inc="pe" if dc == DC - 1 else None)
                        stb = qkst[bank]
                        if fc < 8:
                            ev = S.op("vector", lambda e, stb=stb, bank=bank, fc=fc: e.tensor_scalar(
                                stb[:, :], ps_qk[:, bank * 512:(bank + 1) * 512],
                                cvec_s[:, :], cbq_s[:, fc:fc + 1],
                                AluOpType.mult, AluOpType.add),
                                waits=[("pe", lastm), ("v", cq), ("a", ce),
                                       qk_st_war[bank]], inc="v")
                            sto = S.dma("gpsimd", lambda e, stb=stb, fc=fc, tc=tc: e.dma_start(
                                out=q_sc[fc, :, tc * 512:(tc + 1) * 512], in_=stb[:, :]),
                                waits=[("v", ev)])
                        else:
                            ev = S.op("vector", lambda e, stb=stb, bank=bank, fc=fc: e.tensor_scalar(
                                stb[:, :], ps_qk[:, bank * 512:(bank + 1) * 512],
                                bqk_s[:, fc:fc + 1], None, AluOpType.add),
                                waits=[("pe", lastm), qk_st_war[bank]], inc="v")
                            sto = S.dma("gpsimd", lambda e, stb=stb, fc=fc, tc=tc: e.dma_start(
                                out=k_sc[fc - 8, :, tc * 512:(tc + 1) * 512], in_=stb[:, :]),
                                waits=[("v", ev)])
                        qk_st_war[bank] = sto
                        store_of_fc.setdefault(fc, []).append(sto)
                        store_of_fc[fc] = store_of_fc[fc][-2:]
                        qk_rel[bank] = ("v", ev)
                v_rel = [0, 0, 0, 0]
                for tt in range(NT):
                    for half in range(2):
                        bank = (tt * 2 + half) % 4
                        wts0 = [("a", v_rel[bank]), ("v", cv_bv)]
                        for dc in range(DC):
                            wts0dc = [("v", wat_cv_of[dc])] + (wts0 if dc == 0 else [])
                            S.op("tensor", lambda e, tt=tt, half=half, dc=dc, bank=bank: e.matmul(
                                ps_v[:, bank * 512:(bank + 1) * 512],
                                xnT[:, dc * T + tt * 128: dc * T + (tt + 1) * 128],
                                wattn_b[:, dc * 3072 + 2048 + half * 512: dc * 3072 + 2048 + (half + 1) * 512],
                                start=(dc == 0), stop=False),
                                waits=wts0dc)
                        lastm = S.op("tensor", lambda e, half=half, bank=bank: e.matmul(
                            ps_v[:, bank * 512:(bank + 1) * 512],
                            onesb_s[0:1, :], bv_b[0:1, half * 512:(half + 1) * 512],
                            start=False, stop=True), inc="pe")
                        ev = S.op("scalar", lambda e, tt=tt, half=half, bank=bank: e.activation(
                            Vt[:, tt * C + half * 512: tt * C + (half + 1) * 512],
                            ps_v[:, bank * 512:(bank + 1) * 512], AF.Copy),
                            waits=[("pe", lastm)], inc="a")
                        v_rel[bank] = ev
                qk_st_fin = [("dg0", S.c("dg0")), ("dg1", S.c("dg1"))]
                p3_a, p3_pe, p3_v = S.c("a"), S.c("pe"), S.c("v")
                S.flush()

        with contextlib.ExitStack() as s4:
            E4 = s4.enter_context
            ytile = E4(nc.sbuf_tensor("ytile", [128, NT * C], BF16, side="right"))
            s4b = contextlib.ExitStack()
            E4b = s4b.enter_context
            snegp = [E4b(nc.sbuf_tensor(f"snegp{p}", [128, 2 * T], F32, side="right")) for p in range(2)]
            sneg = [[snegp[p][:, j * T:(j + 1) * T] for j in range(2)] for p in range(2)]
            pbuf = [[E4b(nc.sbuf_tensor(f"pbuf{p}_{j}", [128, T], BF16, side="right")) for j in range(2)] for p in range(2)]
            ptb2 = [E4b(nc.sbuf_tensor(f"ptb2_{i}", [128, 1024], BF16, side="right")) for i in range(8)]
            ncmx = E4b(nc.sbuf_tensor("ncmx", [128, 24], F32, side="right"))
            negmax = E4b(nc.sbuf_tensor("negmax", [128, 6], F32, side="right"))
            sumr = E4b(nc.sbuf_tensor("sumr", [128, 6], F32, side="right"))
            rsum = E4b(nc.sbuf_tensor("rsum", [128, 6], F32, side="right"))
            ps_s = E4(nc.psum_tensor("ps_s", [128, 2048], F32))
            ps_tr = E4(nc.psum_tensor("ps_tr", [128, 1024], F32))
            ps_y = E4(nc.psum_tensor("ps_y", [128, 1024], F32))

            iters = [(hp, qt) for hp in range(H // 2) for qt in range(NT)]
            NIT = len(iters)
            sbank_rel = [0, 0]
            gchunk = [0]
            trslot = [("v", 0), ("v", 0)]
            ptb_war = [0] * 16
            exp_done = [0] * NIT
            mm_done = {}
            sneg_rel = [0, 0]
            pbuf_war = [[0, 0], [0, 0]]
            ysc_done = {}
            av_of = [0] * NIT
            recip_done = [0] * NIT
            abc_war = [0, 0]
            qk_war = [0, 0]
            group_ld = [None, None]
            out_war = [None]

            def chunks_of(L):
                out = []
                c0 = 0
                while c0 < L:
                    w = min(512, L - c0)
                    out.append((c0, w))
                    c0 += w
                return out

            def emit_group_load(hp, early=False):
                hb = hp % 2
                if early:
                    wts = [wat_fin, qk_st_war[2], qk_st_war[3]]
                    wq = qk_st_fin + [("pe", p3_pe)]
                else:
                    wts = [("v", abc_war[hb])]
                    if hp == 0:
                        wts += [("pe", p3_pe), ("v", p3_v), ("a", p3_a)] + qk_st_fin
                    wq = [("pe", qk_war[hb])] if qk_war[hb] else []
                l1 = S.dma("sync", lambda e, hp=hp, hb=hb: e.dma_start(
                    out=abcp[hb][:, 0:T], in_=alibi_bc[2 * hp, :, :]), waits=wts)
                l2 = S.dma("sync", lambda e, hp=hp, hb=hb: e.dma_start(
                    out=abcp[hb][:, T:2 * T], in_=alibi_bc[2 * hp + 1, :, :]))
                l3 = S.dma("sync", lambda e, hp=hp, hb=hb: e.dma_start(
                    out=q_hp[hb][:, :], in_=q_sc[hp, :, :]), waits=wq)
                l4 = S.dma("sync", lambda e, hp=hp, hb=hb: e.dma_start(
                    out=k_hp[hb][:, :], in_=k_sc[hp, :, :]))
                group_ld[hb] = [l1, l2, l3, l4]

            def emit_scores_softmax(it):
                hp, qt = iters[it]
                L = (qt + 1) * 128
                par = it % 3
                sp = it % 2
                ch = chunks_of(L)
                nch = len(ch)
                cd = 0
                pend_red = []

                def emit_red():
                    nonlocal cd
                    if not pend_red:
                        return
                    ci, c0, w, dep = pend_red.pop(0)
                    cd = S.op("vector", lambda e, par=par, sp=sp, c0=c0, w=w, ci=ci: e.tensor_reduce(
                        ncmx[:, par * 8:par * 8 + 8].rearrange("p (j c) -> p j c", j=2)[:, :, ci:ci + 1],
                        snegp[sp].rearrange("p (j t) -> p j t", j=2)[:, :, c0:c0 + w],
                        AX.X, AluOpType.max),
                        waits=[("v", dep)], inc="v")

                for ci, (c0, w) in enumerate(ch):
                    cs = gchunk[0] % 2
                    gchunk[0] += 1
                    rA = cs * 1024
                    rB = cs * 1024 + 512
                    wts = []
                    if it == 0 and ci == 0:
                        wts += [("a", p3_a), ("v", p3_v)]
                    if sbank_rel[cs]:
                        wts.append(("v", sbank_rel[cs]))
                    hb = hp % 2
                    if qt == 0 and ci == 0:
                        wts += group_ld[hb][2:4]
                    S.op("tensor", lambda e, hb=hb, qt=qt, c0=c0, w=w, rA=rA: e.matmul(
                        ps_s[:, rA: rA + w],
                        q_hp[hb][0:64, qt * 128:(qt + 1) * 128],
                        k_hp[hb][0:64, c0:c0 + w],
                        start=True, stop=True, tile_position=(0, 0)),
                        waits=wts)
                    md = S.op("tensor", lambda e, hb=hb, qt=qt, c0=c0, w=w, rB=rB: e.matmul(
                        ps_s[:, rB: rB + w],
                        q_hp[hb][64:128, qt * 128:(qt + 1) * 128],
                        k_hp[hb][64:128, c0:c0 + w],
                        start=True, stop=True, tile_position=(64, 0)), inc="pe")
                    mm_done[(it, ci)] = md
                    qk_war[hb] = md
                    wts2 = [("pe", md)]
                    if qt == 0 and ci == 0:
                        wts2 += group_ld[hp % 2][0:2]
                    if sneg_rel[sp]:
                        wts2.append(("a", sneg_rel[sp]))
                    aoff = T - 128 + c0 - qt * 128
                    stt = S.op("vector", lambda e, sp=sp, hb=hp % 2, aoff=aoff, c0=c0, w=w, rA=rA: e.scalar_tensor_tensor(
                        snegp[sp].rearrange("p (j t) -> p j t", j=2)[:, :, c0:c0 + w],
                        ps_s[:, rA:rA + 1024].rearrange("p (j t) -> p j t", j=2)[:, :, 0:w],
                        1.0,
                        abcp[hb].rearrange("p (j t) -> p j t", j=2)[:, :, aoff:aoff + w],
                        AluOpType.mult, AluOpType.add),
                        waits=wts2, inc="v")
                    abc_war[hp % 2] = S.c("v")
                    sbank_rel[cs] = stt
                    pend_red.append((ci, c0, w, stt))
                    if ci == nch - 1:
                        while pend_red:
                            emit_red()
                    elif len(pend_red) >= 2:
                        emit_red()
                nms = []
                for j in range(2):
                    nms.append(S.op("vector", lambda e, par=par, j=j, nch=nch: e.tensor_reduce(
                        negmax[:, par * 2 + j: par * 2 + j + 1],
                        ncmx[:, par * 8 + j * 4: par * 8 + j * 4 + nch], AX.X,
                        AluOpType.max, negate=True),
                        waits=[("v", cd), ("a", exp_done[it - 3] if it >= 3 else 0)], inc="v"))
                return nms

            def emit_exp(it, nms):
                hp, qt = iters[it]
                L = (qt + 1) * 128
                par = it % 3
                pb = it % 2
                ed = 0
                for j in range(2):
                    wts = [("v", nms[j])]
                    if it >= 2:
                        wts.append(("pe", pbuf_war[pb][j]))
                    if it >= 3:
                        wts.append(("v", recip_done[it - 3]))
                    ed = S.op("scalar", lambda e, par=par, pb=pb, j=j, L=L: e.activation(
                        pbuf[pb][j][:, 0:L], sneg[pb][j][:, 0:L], AF.Exp,
                        bias=negmax[:, par * 2 + j: par * 2 + j + 1],
                        accum_out=sumr[:, par * 2 + j: par * 2 + j + 1]),
                        waits=wts, inc="a")
                exp_done[it] = ed
                sneg_rel[pb] = ed

            def emit_recip(it):
                par = it % 3
                r = 0
                for j in range(2):
                    r = S.op("vector", lambda e, par=par, j=j: e.reciprocal(
                        rsum[:, par * 2 + j: par * 2 + j + 1],
                        sumr[:, par * 2 + j: par * 2 + j + 1]),
                        waits=[("a", exp_done[it]), ("a", ysc_done.get(it - 3, 0))], inc="v")
                recip_done[it] = r

            tr_evs = {}
            evac_cnt = [0]

            def emit_tr(it):
                hp, qt = iters[it]
                par = it % 2
                nkc = qt + 1
                nbat = (nkc + 3) // 4
                evs = []
                pend = []
                gb = [0]

                def flush_evac():
                    if not pend:
                        return
                    use_v = False
                    evac_cnt[0] += 1
                    eng, tag = ("vector", "v") if use_v else ("scalar", "a")

                    def copy_op(e, dst, src):
                        if use_v:
                            return e.tensor_copy(dst, src)
                        return e.activation(dst, src, AF.Copy)

                    if False and len(pend) == 2 and pend[0][2] == 0 and pend[1][2] == 1 and \
                       pend[0][4] % 2 == 0 and pend[0][4] + 1 == pend[1][4]:
                        (j0, k0, s0, kcs0, p0, lt0), (j1, k1, s1, kcs1, p1, lt1) = pend
                        wb1 = len(kcs1) * 128
                        ev = S.op(eng, lambda e, p0=p0, wb1=wb1: copy_op(
                            e, ptb2[p0 // 2][:, 0:512 + wb1], ps_tr[:, 0:512 + wb1]),
                            waits=[("pe", lt1)], inc=tag)
                        trslot[0] = (tag, ev)
                        trslot[1] = (tag, ev)
                        evs.append((j0, k0, (tag, ev), kcs0, p0))
                        evs.append((j1, k1, (tag, ev), kcs1, p1))
                    else:
                        for (j, kcb, slot, kcs, pslot, lt) in pend:
                            wb = len(kcs) * 128
                            ev = S.op(eng, lambda e, pslot=pslot, slot=slot, wb=wb: copy_op(
                                e, ptb2[pslot // 2][:, (pslot % 2) * 512: (pslot % 2) * 512 + wb],
                                ps_tr[:, slot * 512: slot * 512 + wb]),
                                waits=[("pe", lt)], inc=tag)
                            trslot[slot] = (tag, ev)
                            evs.append((j, kcb, (tag, ev), kcs, pslot))
                    pend.clear()

                for j in range(2):
                    lasttr = 0
                    for kcb in range(nbat):
                        slot = gb[0] % 2
                        gb[0] += 1
                        pslot = par * 8 + j * 4 + kcb
                        kcs = list(range(kcb * 4, min(kcb * 4 + 4, nkc)))
                        for idx, kc in enumerate(kcs):
                            wts = []
                            if idx == 0:
                                wts = [("a", exp_done[it]), trslot[slot],
                                       ("pe", ptb_war[pslot])]
                            lasttr = S.op("tensor", lambda e, par=par, j=j, kc=kc, slot=slot, idx=idx, n=len(kcs): e.matmul(
                                ps_tr[:, slot * 512 + (kc % 4) * 128: slot * 512 + (kc % 4) * 128 + 128],
                                pbuf[par][j][:, kc * 128:(kc + 1) * 128], ident_s[:, :],
                                start=(idx == 0), stop=(idx == n - 1)),
                                waits=wts, inc="pe" if idx == len(kcs) - 1 else None)
                        pend.append((j, kcb, slot, kcs, pslot, lasttr))
                        if slot == 1:
                            flush_evac()
                    pbuf_war[par][j] = lasttr
                flush_evac()
                tr_evs[it] = evs

            def emit_av(it):
                hp, qt = iters[it]
                ypar = it % 2
                nkc = qt + 1
                evs = tr_evs.pop(it)
                for (j, kcb, ev, kcs, pslot) in evs:
                    h = 2 * hp + j
                    for idx, kc in enumerate(kcs):
                        wts = [ev] if idx == 0 else []
                        if j == 0 and kc == 0:
                            wts = list(wts) + [("a", ysc_done.get(it - 2, 0))]
                        first = (j == 0 and kc == 0)
                        last = (j == 1 and kc == nkc - 1)
                        r = S.op("tensor", lambda e, pslot=pslot, ypar=ypar, j=j, kc=kc, h=h, first=first, last=last: e.matmul(
                            ps_y[:, ypar * 512 + j * 64: ypar * 512 + j * 64 + 64],
                            ptb2[pslot // 2][:, (pslot % 2) * 512 + (kc % 4) * 128: (pslot % 2) * 512 + (kc % 4) * 128 + 128],
                            Vt[:, kc * C + h * 64: kc * C + h * 64 + 64],
                            start=first, stop=last),
                            waits=wts, inc="pe" if last else None)
                        if last:
                            av_of[it] = r
                    ptb_war[pslot] = S.c("pe")

            def emit_ysc(it):
                hp, qt = iters[it]
                par = it % 3
                ypar = it % 2
                y = 0
                for j in range(2):
                    h = 2 * hp + j
                    wts_y = [("pe", av_of[it]), ("v", recip_done[it])]
                    y = S.op("scalar", lambda e, par=par, ypar=ypar, j=j, qt=qt, h=h: e.activation(
                        ytile[:, qt * C + h * 64: qt * C + (h + 1) * 64],
                        ps_y[:, ypar * 512 + j * 64: ypar * 512 + j * 64 + 64],
                        AF.Copy, scale=rsum[:, par * 2 + j: par * 2 + j + 1]),
                        waits=wts_y, inc="a")
                ysc_done[it] = y

            wpj_b = E4b(nc.sbuf_tensor("wpj_b", [128, DC * C], BF16, side="right"))
            yT_sb = E4b(nc.sbuf_tensor("yT_sb", [128, C], BF16, side="right"))
            xfst = E4b(nc.sbuf_tensor("xfst", [128, C], F32, side="right"))
            outsb = E4b(nc.sbuf_tensor("outsb", [128, C], F32, side="right"))
            q_hp = [E4b(nc.sbuf_tensor(f"q_hp{i}", [128, T], BF16, side="right")) for i in range(2)]
            k_hp = [E4b(nc.sbuf_tensor(f"k_hp{i}", [128, T], BF16, side="right")) for i in range(2)]
            abcp = [E4b(nc.sbuf_tensor(f"abcp{i}", [128, 2 * T], F32, side="right")) for i in range(2)]
            wat_fin = ("v", wat_cv)
            emit_group_load(0, early=(T >= 2048))
            pj_rel = [0]
            yT_war = [0]
            xf_war = [0]
            for cc in range(DC):
                S.dma("sync", lambda e, cc=cc: e.dma_start(
                    out=wpj_b[:, cc * 1024:(cc + 1) * 1024], in_=wpj_sc[cc, :, :]),
                    waits=conv_stores_fin + (qk_st_fin + [("pe", p3_pe), ("v", p3_v), ("a", p3_a)]
                                             if cc == 0 else []))
            wpj_ld = [("dl0", S.c("dl0")), ("dl1", S.c("dl1"))]

            def emit_proj_unit(qt, ydep):
                for b2 in range(2):
                    slot = b2
                    last = 0
                    for j in range(4):
                        cc = b2 * 4 + j
                        wts = []
                        if j == 0:
                            wts = [("a", ydep), trslot[slot]]
                        last = S.op("tensor", lambda e, qt=qt, cc=cc, slot=slot, j=j: e.matmul(
                            ps_tr[:, slot * 512 + j * 128: slot * 512 + (j + 1) * 128],
                            ytile[:, qt * C + cc * 128: qt * C + (cc + 1) * 128],
                            ident_s[:, :], start=(j == 0), stop=(j == 3)),
                            waits=wts, inc="pe" if j == 3 else None)
                    ev = S.op("vector", lambda e, b2=b2, slot=slot: e.tensor_copy(
                        yT_sb[:, b2 * 512:(b2 + 1) * 512],
                        ps_tr[:, slot * 512:(slot + 1) * 512]),
                        waits=[("pe", last), ("pe", yT_war[0])], inc="v")
                    trslot[slot] = ("v", ev)
                yT_done = S.c("v")
                xl = S.dma("sync", lambda e, qt=qt: e.dma_start(
                    out=xfst[:, :], in_=xf_sc[qt * 128:(qt + 1) * 128, :]),
                    waits=p2_xf_st + [("v", xf_war[0])])
                lastp = 0
                for cc in range(DC):
                    for half in range(2):
                        slot = half
                        wts = []
                        if cc == 0:
                            wts = wpj_ld + [("v", yT_done), trslot[slot],
                                            ("v", pj_rel[0]), trslot[1 - slot]]
                        lastp = S.op("tensor", lambda e, slot=slot, cc=cc, half=half: e.matmul(
                            ps_tr[:, slot * 512:(slot + 1) * 512],
                            yT_sb[:, cc * 128:(cc + 1) * 128],
                            wpj_b[:, cc * 1024 + half * 512: cc * 1024 + half * 512 + 512],
                            start=(cc == 0), stop=False),
                            waits=wts, inc="pe" if half == 1 else None)
                oas = []
                for half in range(2):
                    slot = half
                    lastb = S.op("tensor", lambda e, slot=slot, half=half: e.matmul(
                        ps_tr[:, slot * 512:(slot + 1) * 512], onesb_s[0:1, :],
                        bpj_b[0:1, half * 512:(half + 1) * 512],
                        start=False, stop=True),
                        waits=[("v", cv_bp)], inc="pe")
                    oa = S.op("vector", lambda e, slot=slot, half=half: e.tensor_tensor(
                        outsb[:, half * 512:(half + 1) * 512],
                        ps_tr[:, slot * 512:(slot + 1) * 512],
                        xfst[:, half * 512:(half + 1) * 512], AluOpType.add),
                        waits=[("pe", lastb), xl]
                              + ([out_war[0]] if out_war[0] else []), inc="v")
                    pj_rel[0] = oa
                    trslot[slot] = ("v", oa)
                yT_war[0] = S.c("pe")
                xf_war[0] = pj_rel[0]
                out_war[0] = S.dma("gpsimd", lambda e, qt=qt: e.dma_start(
                    out=out_d[qt * 128:(qt + 1) * 128, :], in_=outsb[:, :]),
                    waits=[("v", pj_rel[0])])

            nm_of = {}
            NHP = H // 2
            for step in range(NIT + 3):
                if step < NIT:
                    hp, qt = iters[step]
                    if qt == 0 and hp > 0:
                        emit_group_load(hp)
                    nm_of[step] = emit_scores_softmax(step)
                if 3 <= step <= NIT + 2:
                    emit_ysc(step - 3)
                if 1 <= step <= NIT:
                    emit_recip(step - 1)
                    emit_tr(step - 1)
                if 2 <= step <= NIT + 1:
                    emit_av(step - 2)
                if step < NIT:
                    emit_exp(step, nm_of[step])
                if 3 <= step <= NIT + 2:
                    it3 = step - 3
                    hp3, qt3 = iters[it3]
                    if hp3 == NHP - 1:
                        emit_proj_unit(qt3, ysc_done[it3])
            S.flush()
            s4b.close()

            fin = [("dg0", S.c("dg0")), ("dg1", S.c("dg1"))]
            S.op("gpsimd", lambda e: None, waits=fin)
            S.op("sync", lambda e: None, waits=fin)
            S.flush()

    return nc



def _consts(T):
    start = 2.0 ** (-(2.0 ** -(math.log2(H) - 3)))
    slopes = np.array([start * start ** i for i in range(H)], dtype=np.float32)
    d_rel = (np.arange(T, dtype=np.float32)[None, :] + np.float32(128 - T)
             - np.arange(128, dtype=np.float32)[:, None])
    alibi_bc = np.ascontiguousarray(slopes[:, None, None] * d_rel[None, :, :])
    alibi_bc[:, d_rel > 0] = np.float32(-1e38)
    return {
        "ident": np.eye(128, dtype=BF),
        "alibi_bc": alibi_bc,
        "ones_b": np.ones((1, 128), dtype=BF),
        "ones_f": np.ones((1, 128), dtype=np.float32),
    }


def make_inputs(xb, W_attn, b_attn, W_proj, b_proj, temperature, wg, wu, wd, T):
    con = _consts(T)
    wgr = np.ascontiguousarray(
        wg.reshape(HCP, 2, 128, DC, 128).transpose(0, 4, 1, 3, 2).reshape(HCP, 128, 2048))
    wur = np.ascontiguousarray(
        wu.reshape(HCP, 2, 128, DC, 128).transpose(0, 4, 1, 3, 2).reshape(HCP, 128, 2048))
    wdr = np.ascontiguousarray(
        wd.T.reshape(HCP, 2, 128, C).transpose(0, 2, 1, 3).reshape(HCP, 128, 2048))
    d = {
        "xin": np.ascontiguousarray(xb),
        "wattn": np.ascontiguousarray(W_attn),
        "battn_qk": np.ascontiguousarray(b_attn[:2048].reshape(16, 128).T),
        "battn_v": np.ascontiguousarray(b_attn[2048:].reshape(1, C)),
        "wproj": np.ascontiguousarray(W_proj),
        "bproj": np.ascontiguousarray(b_proj.reshape(1, C)),
        "temp": np.ascontiguousarray(temperature.reshape(1, 1)),
        "wg_r": wgr,
        "wu_r": wur,
        "wd_r": wdr,
    }
    d.update(con)
    return d


def kernel(L0_attn_x, L0_attn_W_attn, L0_attn_b_attn, L0_attn_W_proj,
           L0_attn_b_proj, L0_attn_temperature, L0_ffn_w_gate, L0_ffn_w_up,
           L0_ffn_w_down):
    T = int(L0_attn_x.shape[1])
    x = np.ascontiguousarray(np.asarray(L0_attn_x, dtype=np.float32))
    W_attn = np.asarray(L0_attn_W_attn, dtype=np.float32)
    b_attn = np.asarray(L0_attn_b_attn, dtype=np.float32)
    W_proj = np.asarray(L0_attn_W_proj, dtype=np.float32)
    b_proj = np.asarray(L0_attn_b_proj, dtype=np.float32)
    tempv = np.asarray(L0_attn_temperature, dtype=np.float32)
    wg = np.asarray(L0_ffn_w_gate, dtype=np.int32)
    wu = np.asarray(L0_ffn_w_up, dtype=np.int32)
    wd = np.asarray(L0_ffn_w_down, dtype=np.int32)

    nc = build_program(T)
    in_maps = [make_inputs(x[b], W_attn, b_attn, W_proj, b_proj, tempv, wg, wu, wd, T)
               for b in range(x.shape[0])]
    res = run_bass_kernel_spmd(nc, in_maps, core_ids=list(range(x.shape[0])))
    return np.stack([r["out"] for r in res.results], axis=0)


# version 37
# speedup vs baseline: 1.0704x; 1.0141x over previous
import contextlib
import math
import numpy as np
import ml_dtypes
import concourse.bass as bass
import concourse.mybir as mybir
from concourse.bass_utils import run_bass_kernel_spmd
from concourse.alu_op_type import AluOpType

F32 = mybir.dt.float32
BF16 = mybir.dt.bfloat16
I32 = mybir.dt.int32
AF = mybir.ActivationFunctionType
AX = mybir.AxisListType
BF = ml_dtypes.bfloat16

NCORES = 8
C = 1024
H = 16
D = 64
HID = 3072
DC = C // 128
HCN = HID // 128
HCP = HCN // 2

ENGS = ("sync", "gpsimd", "vector", "scalar", "tensor")


class KB:

    QSEMS = {"sync": ("dl0", "dl1"), "gpsimd": ("dg0", "dg1")}

    def __init__(self, nc, block, sem):
        self.nc = nc
        self.block = block
        self.sem = sem
        self.cnt = {k: 0 for k in sem}
        self.pend = []
        self.waited = {e: {} for e in ENGS}
        self.qidx = {"sync": 0, "gpsimd": 0}

    def dma(self, queue, fn, waits=()):
        sems = self.QSEMS[queue]
        i = self.qidx[queue]
        self.qidx[queue] ^= 1
        s = sems[i]
        auto = [(s, self.cnt[s])]
        cnt = self.op(queue, fn, waits=list(waits) + auto, inc=s, inc_n=16)
        return (s, cnt)

    def op(self, eng, fn, waits=(), inc=None, inc_n=1):
        waits = [(s, v) for (s, v) in waits if v and v > 0]
        incs = []
        if inc is not None:
            self.cnt[inc] += inc_n
            incs.append((inc, inc_n))
        self.pend.append((eng, fn, waits, incs))
        return self.cnt[inc] if inc is not None else 0

    def c(self, s):
        return self.cnt[s]

    def flush(self):
        for eng in ENGS:
            ops = [(fn, w, i) for (e, fn, w, i) in self.pend if e == eng]
            if not ops:
                continue
            waited = self.waited[eng]
            sem = self.sem

            def section(esec, ops=ops, waited=waited):
                for fn, waits, incs in ops:
                    for s, v in waits:
                        if v > waited.get(s, 0):
                            esec.wait_ge(sem[s], v)
                            waited[s] = v
                    instr = fn(esec)
                    for s, n in incs:
                        instr.then_inc(sem[s], n)

            getattr(self.block, eng)(section)
        self.pend = []


def build_program(T):
    NT = T // 128
    NTB = T // 512
    nc = bass.Bass("TRN2", target_bir_lowering=False)

    def din(name, shape, dt):
        return nc.dram_tensor(name, shape, dt, kind="ExternalInput").ap()

    xin = din("xin", [T, C], F32)
    wattn = din("wattn", [C, 3 * C], F32)
    battn_qk = din("battn_qk", [128, 16], F32)
    battn_v = din("battn_v", [1, C], F32)
    wproj = din("wproj", [C, C], F32)
    bproj = din("bproj", [1, C], F32)
    temp = din("temp", [1, 1], F32)
    wg_r = din("wg_r", [HCP, 128, 2048], I32)
    wu_r = din("wu_r", [HCP, 128, 2048], I32)
    wd_r = din("wd_r", [HCP, 128, 2048], I32)
    ident = din("ident", [128, 128], BF16)
    alibi_bc = din("alibi_bc", [H, 128, T], F32)
    ones_b = din("ones_b", [1, 128], BF16)
    ones_f = din("ones_f", [1, 128], F32)

    out_d = nc.dram_tensor("out", [T, C], F32, kind="ExternalOutput").ap()
    wg_sc = nc.dram_tensor("wg_sc", [HCP, 128, 2048], BF16, kind="Internal").ap()
    wu_sc = nc.dram_tensor("wu_sc", [HCP, 128, 2048], BF16, kind="Internal").ap()
    wd_sc = nc.dram_tensor("wd_sc", [HCP, 128, 2048], BF16, kind="Internal").ap()
    xf_sc = nc.dram_tensor("xf_sc", [T, C], F32, kind="Internal").ap()
    q_sc = nc.dram_tensor("q_sc", [DC, 128, T], BF16, kind="Internal").ap()
    wpj_sc = nc.dram_tensor("wpj_sc", [DC, 128, C], BF16, kind="Internal").ap()
    k_sc = nc.dram_tensor("k_sc", [DC, 128, T], BF16, kind="Internal").ap()

    with contextlib.ExitStack() as st:
        E = st.enter_context
        sem = {s: E(nc.semaphore("sem_" + s)) for s in
               ("dl0", "dl1", "dg0", "dg1", "pe", "v", "a")}
        block = E(nc.Block())
        S = KB(nc, block, sem)

        ident_s = E(nc.sbuf_tensor("ident_s", [128, 128], BF16))
        onesb_s = E(nc.sbuf_tensor("onesb_s", [1, 128], BF16))
        onesf_s = E(nc.sbuf_tensor("onesf_s", [1, 128], F32))
        bqk_s = E(nc.sbuf_tensor("bqk_s", [128, 16], F32))
        bv_b = E(nc.sbuf_tensor("bv_b", [1, C], BF16))
        bpj_b = E(nc.sbuf_tensor("bpj_b", [1, C], BF16))
        temp_s = E(nc.sbuf_tensor("temp_s", [1, 1], F32))
        t8_s = E(nc.sbuf_tensor("t8_s", [1, 1], F32))
        c11_s = E(nc.sbuf_tensor("c11_s", [1, 1], F32))
        cvec_s = E(nc.sbuf_tensor("cvec_s", [128, 1], F32))
        cbq_s = E(nc.sbuf_tensor("cbq_s", [128, 8], F32))
        Vt = E(nc.sbuf_tensor("Vt", [128, NT * C], BF16))

        cl = []
        for dst, src in [(ident_s, ident),
                         (onesb_s, ones_b), (onesf_s, ones_f),
                         (bqk_s, battn_qk), (temp_s, temp)]:
            cl.append(S.dma("sync", lambda e, d=dst, s=src: e.dma_start(out=d[:, :], in_=s)))
        c_consts = [cl[-1], cl[-2]]
        cv_t8 = S.op("vector", lambda e: e.tensor_scalar(
            t8_s[0:1, :], temp_s[0:1, :], 8.0, None, AluOpType.mult),
            waits=c_consts, inc="v")
        cv_c11 = S.op("vector", lambda e: e.reciprocal(c11_s[0:1, :], t8_s[0:1, :]),
                      waits=[("v", cv_t8)], inc="v")

        with contextlib.ExitStack() as sA:
            xnT = sA.enter_context(nc.sbuf_tensor("xnT", [128, DC * T], BF16, side="right"))
            wdres = sA.enter_context(nc.sbuf_tensor("wdres", [128, HCP * 2048], BF16, side="right"))

            with contextlib.ExitStack() as sB:
                E1 = sB.enter_context
                xstage = [E1(nc.sbuf_tensor(f"xstage{i}", [128, C], F32, side="right")) for i in range(4)]
                xn_st = [E1(nc.sbuf_tensor(f"xn_st{i}", [128, C], BF16, side="right")) for i in range(4)]
                sq_sc = [E1(nc.sbuf_tensor(f"sq_sc{i}", [128, C], F32, side="right")) for i in range(2)]
                wistage = [E1(nc.sbuf_tensor(f"wistage{i}", [128, 2048], I32, side="right")) for i in range(3)]
                cbuf = [E1(nc.sbuf_tensor(f"cbuf{i}", [128, 2048], BF16, side="right")) for i in range(2)]
                cbuf32 = [E1(nc.sbuf_tensor(f"cbuf32_{i}", [128, 2048], F32, side="right")) for i in range(2)]
                stat = {k: [E1(nc.sbuf_tensor(f"st_{k}{i}", [128, 1], F32, side="right")) for i in range(4)]
                        for k in ("s1", "s2", "m", "v", "r", "n")}
                bst_f = E1(nc.sbuf_tensor("bst_f", [1, C], F32, side="right"))
                wpst = E1(nc.sbuf_tensor("wpst", [128, C], F32, side="right"))
                ps1 = E1(nc.psum_tensor("ps1", [128, 4096], F32))

                bld1 = S.dma("sync", lambda e: e.dma_start(out=bst_f[0:1, 0:C], in_=battn_v))
                cv_bv = S.op("vector", lambda e: e.tensor_copy(bv_b[0:1, :], bst_f[0:1, 0:C]),
                             waits=[bld1], inc="v")
                bld2 = S.dma("sync", lambda e: e.dma_start(out=bst_f[0:1, 0:C], in_=bproj),
                             waits=[("v", cv_bv)])
                cv_bp = S.op("vector", lambda e: e.tensor_copy(bpj_b[0:1, :], bst_f[0:1, 0:C]),
                             waits=[bld2], inc="v")

                conv_tasks = []
                for m, src, dst_sc, width in (
                        [(0, wg_r, wg_sc, 2048), (1, wu_r, wu_sc, 2048),
                         (2, wd_r, None, 2048)]):
                    for j in range(HCP):
                        conv_tasks.append((src, dst_sc, j, width))
                for j in range(DC):
                    conv_tasks.append(("wproj", wpj_sc, j, 1024))
                n_tasks = len(conv_tasks)
                conv_i = [0]
                cbuf_store = [None, None, None]
                wist_conv = [0, 0, 0]
                c32_war = [0, 0, 0]
                wpst_war = [0]
                conv_stores = []

                def emit_conv(k):
                    src, dst_sc, j, width = conv_tasks[k]
                    bi = k % 3
                    b = k % 2
                    lq = "sync"
                    is_pj = (src == "wproj")
                    is_wd = dst_sc is None
                    if is_pj:
                        ld = S.dma("sync", lambda e, j=j: e.dma_start(
                            out=wpst[:, :], in_=wproj[j * 128:(j + 1) * 128, :]),
                            waits=[("a", wpst_war[0])])
                        cv = S.op("scalar", lambda e, b=b: e.activation(
                            cbuf[b][:, 0:1024], wpst[:, :], AF.Copy),
                            waits=[ld] + ([cbuf_store[b]] if cbuf_store[b] else []), inc="a")
                        wpst_war[0] = cv
                    else:
                        ld = S.dma(lq, lambda e, src=src, j=j, bi=bi, width=width: e.dma_start(
                            out=wistage[bi][:, 0:width], in_=src[j, :, :]),
                            waits=[("v", wist_conv[bi])])
                        cv0 = S.op("vector", lambda e, b=b, bi=bi, width=width: e.tensor_scalar(
                            cbuf32[b][:, 0:width], wistage[bi][:, 0:width], 1.0,
                            None, AluOpType.mult),
                            waits=[ld, ("a", c32_war[b])]
                                  + ([cbuf_store[b]] if cbuf_store[b] else []), inc="v")
                        cdst = (lambda e, j=j: wdres[:, j * 2048:(j + 1) * 2048]) if is_wd \
                            else (lambda e, b=b, width=width: cbuf[b][:, 0:width])
                        cv = S.op("scalar", lambda e, b=b, width=width, cdst=cdst: e.activation(
                            cdst(e), cbuf32[b][:, 0:width], AF.Copy, scale=1.0 / 127.0),
                            waits=[("v", cv0)], inc="a")
                        wist_conv[bi] = cv0
                        c32_war[b] = cv
                    if is_wd:
                        return
                    sto = S.dma("gpsimd", lambda e, dst_sc=dst_sc, j=j, b=b, width=width: e.dma_start(
                        out=dst_sc[j, :, :], in_=cbuf[b][:, 0:width]),
                        waits=[("a", cv)])
                    cbuf_store[b] = sto
                    conv_stores.append(sto)

                ln_xn = [0] * NT
                stile_tr = [0] * (NT // 4)
                tr_evac = [[(None, 0)] * DC for _ in range(NT // 4)]
                prev_xuse = [0, 0, 0, 0]
                stat_war = [0, 0, 0, 0]
                sq_war = [0, 0, 0, 0]
                for i in range(NT):
                    for _ in range(3):
                        if conv_i[0] < n_tasks:
                            emit_conv(conv_i[0])
                            conv_i[0] += 1
                    b = i % 4
                    p = i % 4
                    ld = S.dma("sync", lambda e, i=i, b=b: e.dma_start(
                        out=xstage[b][:, :], in_=xin[i * 128:(i + 1) * 128, :]),
                        waits=[("v", prev_xuse[b])])
                    rs = S.op("vector", lambda e, b=b, p=p: e.tensor_reduce(
                        stat["s1"][p][:, :], xstage[b][:, :], AX.X, AluOpType.add),
                        waits=[ld, ("v", stat_war[p])], inc="v")
                    sq = S.op("scalar", lambda e, b=b, p=p: e.activation(
                        sq_sc[p % 2][:, :], xstage[b][:, :], AF.Square,
                        accum_out=stat["s2"][p][:, :]),
                        waits=[ld, ("v", sq_war[p]), ("v", sq_war[(p + 2) % 4])], inc="a")
                    mn = S.op("vector", lambda e, p=p: e.tensor_scalar(
                        stat["m"][p][:, :], stat["s1"][p][:, :], 1.0 / C, None,
                        AluOpType.mult), waits=[("v", rs)], inc="v")
                    v1 = S.op("vector", lambda e, p=p: e.tensor_tensor(
                        stat["v"][p][:, :], stat["m"][p][:, :], stat["m"][p][:, :],
                        AluOpType.mult), waits=[("v", mn), ("a", sq)], inc="v")
                    v2 = S.op("vector", lambda e, p=p: e.tensor_scalar(
                        stat["s2"][p][:, :], stat["s2"][p][:, :], 1.0 / C, 1e-5,
                        AluOpType.mult, AluOpType.add), waits=[("a", sq)], inc="v")
                    v3 = S.op("vector", lambda e, p=p: e.tensor_tensor(
                        stat["v"][p][:, :], stat["s2"][p][:, :], stat["v"][p][:, :],
                        AluOpType.subtract), waits=[("v", v2)], inc="v")
                    sq_war[p] = v3
                    sd = S.op("scalar", lambda e, p=p: e.activation(
                        stat["v"][p][:, :], stat["v"][p][:, :], AF.Sqrt),
                        waits=[("v", v3)], inc="a")
                    rc = S.op("vector", lambda e, p=p: e.reciprocal(
                        stat["r"][p][:, :], stat["v"][p][:, :]),
                        waits=[("a", sd)], inc="v")
                    ng = S.op("vector", lambda e, p=p: e.tensor_scalar(
                        stat["n"][p][:, :], stat["m"][p][:, :], stat["r"][p][:, :],
                        -1.0, AluOpType.mult, AluOpType.mult),
                        waits=[("v", rc)], inc="v")
                    stile = i // 4
                    xn = S.op("vector", lambda e, i=i, b=b, p=p: e.tensor_scalar(
                        xn_st[i % 4][:, :], xstage[b][:, :], stat["r"][p][:, :],
                        stat["n"][p][:, :], AluOpType.mult, AluOpType.add),
                        waits=[("v", ng),
                               ("pe", stile_tr[stile - 1] if stile >= 1 else 0)],
                        inc="v")
                    prev_xuse[b] = xn
                    stat_war[p] = xn
                    ln_xn[i] = xn
                    if i % 4 == 3:
                        last = 0
                        for dc in range(DC):
                            for j in range(4):
                                wts = [("v", ln_xn[stile * 4 + j])]
                                if stile >= 1:
                                    es, ev = tr_evac[stile - 1][dc]
                                    wts.append((es, ev))
                                last = S.op("tensor", lambda e, dc=dc, j=j, stile=stile: e.matmul(
                                    ps1[:, dc * 512 + j * 128: dc * 512 + (j + 1) * 128],
                                    xn_st[(stile * 4 + j) % 4][:, dc * 128:(dc + 1) * 128],
                                    ident_s[:, :], start=(j == 0), stop=(j == 3)),
                                    waits=wts, inc="pe" if j == 3 else None)
                            if dc % 2 == 0:
                                ev = S.op("vector", lambda e, dc=dc, stile=stile: e.tensor_copy(
                                    xnT[:, dc * T + stile * 512: dc * T + (stile + 1) * 512],
                                    ps1[:, dc * 512:(dc + 1) * 512]),
                                    waits=[("pe", last)], inc="v")
                                tr_evac[stile][dc] = ("v", ev)
                            else:
                                ev = S.op("scalar", lambda e, dc=dc, stile=stile: e.activation(
                                    xnT[:, dc * T + stile * 512: dc * T + (stile + 1) * 512],
                                    ps1[:, dc * 512:(dc + 1) * 512], AF.Copy),
                                    waits=[("pe", last)], inc="a")
                                tr_evac[stile][dc] = ("a", ev)
                        stile_tr[stile] = last
                while conv_i[0] < n_tasks:
                    emit_conv(conv_i[0])
                    conv_i[0] += 1
                conv_stores_fin = [("dg0", S.c("dg0")), ("dg1", S.c("dg1"))]
                p1_v, p1_a, p1_pe = S.c("v"), S.c("a"), S.c("pe")
                S.flush()

            with contextlib.ExitStack() as sF:
                E2 = sF.enter_context
                wgbuf = [E2(nc.sbuf_tensor(f"wgbuf{i}", [128, 2048], BF16, side="right")) for i in range(2)]
                wubuf = [E2(nc.sbuf_tensor(f"wubuf{i}", [128, 2048], BF16, side="right")) for i in range(2)]
                hbuf = E2(nc.sbuf_tensor("hbuf", [128, HCN * 512], BF16, side="right"))
                gs_s = [E2(nc.sbuf_tensor(f"gs_s{i}", [128, 512], BF16, side="right")) for i in range(2)]
                us_s = [E2(nc.sbuf_tensor(f"us_s{i}", [128, 512], BF16, side="right")) for i in range(2)]
                x2st = [E2(nc.sbuf_tensor("x2st0", [128, C], F32, side="right"))] * 2
                xfout = [E2(nc.sbuf_tensor(f"xfout{i}", [128, C], F32, side="right")) for i in range(2)]
                ps_g = E2(nc.psum_tensor("ps_g", [128, 1024], F32))
                ps_u = E2(nc.psum_tensor("ps_u", [128, 1024], F32))
                ps_dn = E2(nc.psum_tensor("ps_dn", [128, 2048], F32))

                wgu_ld = {}
                wgld_war = [0, 0]
                grel = [p1_a, p1_a]
                urel = [p1_a, p1_a]
                gs_war = [0, 0]
                dnrel = [0, 0, 0, 0]
                hbuf_war = 0
                x2_war = [0, 0]
                xf_war = [None, None]
                for tb in range(NTB):
                    for hcp in range(HCP):
                        b = hcp % 2
                        wts = conv_stores_fin + [("pe", wgld_war[b])]
                        if tb == 0 and hcp == 0:
                            wts += [("pe", p1_pe), ("v", p1_v), ("a", p1_a)]
                        ldg = S.dma("sync", lambda e, hcp=hcp, b=b: e.dma_start(
                            out=wgbuf[b][:, :], in_=wg_sc[hcp, :, :]), waits=wts)
                        ldu = S.dma("sync", lambda e, hcp=hcp, b=b: e.dma_start(
                            out=wubuf[b][:, :], in_=wu_sc[hcp, :, :]))
                        wgu_ld[(tb, hcp)] = [ldg, ldu]
                        for hc2 in range(2):
                            hc = hcp * 2 + hc2
                            gp = hc % 2
                            wts_g = [ldg, ldu, ("a", grel[gp])]
                            if tb == 0 and hc == 0:
                                wts_g += [("v", p1_v), ("a", p1_a), ("pe", p1_pe)]
                            lg = lu = 0
                            for dc in range(DC):
                                lg = S.op("tensor", lambda e, hc2=hc2, dc=dc, gp=gp, b=b, tb=tb: e.matmul(
                                    ps_g[:, gp * 512:(gp + 1) * 512],
                                    wgbuf[b][:, hc2 * 1024 + dc * 128: hc2 * 1024 + (dc + 1) * 128],
                                    xnT[:, dc * T + tb * 512: dc * T + tb * 512 + 512],
                                    start=(dc == 0), stop=(dc == DC - 1)),
                                    waits=wts_g if dc == 0 else (),
                                    inc="pe" if dc == DC - 1 else None)
                            for dc in range(DC):
                                lu = S.op("tensor", lambda e, hc2=hc2, dc=dc, gp=gp, b=b, tb=tb: e.matmul(
                                    ps_u[:, gp * 512:(gp + 1) * 512],
                                    wubuf[b][:, hc2 * 1024 + dc * 128: hc2 * 1024 + (dc + 1) * 128],
                                    xnT[:, dc * T + tb * 512: dc * T + tb * 512 + 512],
                                    start=(dc == 0), stop=(dc == DC - 1)),
                                    waits=[("a", urel[gp])] if dc == 0 else (),
                                    inc="pe" if dc == DC - 1 else None)
                            wgld_war[b] = lu
                            sl = S.op("scalar", lambda e, gp=gp: e.activation(
                                gs_s[gp][:, :], ps_g[:, gp * 512:(gp + 1) * 512], AF.Silu),
                                waits=[("pe", lg), ("v", gs_war[gp])], inc="a")
                            grel[gp] = sl
                            ue = S.op("scalar", lambda e, gp=gp: e.activation(
                                us_s[gp][:, :], ps_u[:, gp * 512:(gp + 1) * 512], AF.Copy),
                                waits=[("pe", lu), ("v", gs_war[gp])], inc="a")
                            urel[gp] = ue
                            wts_m = [("a", ue)]
                            if tb > 0 and hc == 0:
                                wts_m.append(("pe", hbuf_war))
                            ml = S.op("vector", lambda e, hc=hc, gp=gp: e.tensor_tensor(
                                hbuf[:, hc * 512:(hc + 1) * 512], gs_s[gp][:, :],
                                us_s[gp][:, :], AluOpType.mult),
                                waits=wts_m, inc="v")
                            gs_war[gp] = ml
                    mults_done = S.c("v")
                    for pas in range(2):
                        lastd = [0] * 4
                        for hcp2 in range(HCN // 2):
                            for hc2 in range(2):
                                hc = 2 * hcp2 + hc2
                                for tsh in range(4):
                                    ts = pas * 2 + tsh // 2
                                    half = tsh % 2
                                    wts_d = []
                                    if hc == 0:
                                        wts_d += [("v", mults_done), ("v", dnrel[tsh])]
                                    r = S.op("tensor", lambda e, hc=hc, hcp2=hcp2, hc2=hc2, ts=ts, half=half, tsh=tsh: e.matmul(
                                        ps_dn[:, tsh * 512:(tsh + 1) * 512],
                                        hbuf[:, hc * 512 + ts * 128: hc * 512 + ts * 128 + 128],
                                        wdres[:, hcp2 * 2048 + hc2 * 1024 + half * 512: hcp2 * 2048 + hc2 * 1024 + (half + 1) * 512],
                                        start=(hc == 0), stop=(hc == HCN - 1)),
                                        waits=wts_d, inc="pe" if tsh == 3 else None)
                                    if tsh == 3:
                                        if hc == HCN - 1:
                                            lastd = [r] * 4
                        for tsh in range(4):
                            ts = pas * 2 + tsh // 2
                            half = tsh % 2
                            tt = tb * 4 + ts
                            xb = ts % 2
                            if half == 0:
                                wts_x2 = [("v", x2_war[xb])]
                                if tb == 0 and pas == 0 and tsh == 0:
                                    wts_x2 += [("pe", p1_pe), ("v", p1_v), ("a", p1_a)] + conv_stores_fin
                                x2_ld = S.dma("gpsimd", lambda e, tt=tt, xb=xb: e.dma_start(
                                    out=x2st[xb][:, :], in_=xin[tt * 128:(tt + 1) * 128, :]),
                                    waits=wts_x2)
                            xa = S.op("vector", lambda e, xb=xb, tsh=tsh, half=half: e.tensor_tensor(
                                xfout[xb][:, half * 512:(half + 1) * 512],
                                ps_dn[:, tsh * 512:(tsh + 1) * 512],
                                x2st[xb][:, half * 512:(half + 1) * 512], AluOpType.add),
                                waits=[("pe", lastd[tsh]), x2_ld]
                                      + ([xf_war[xb]] if xf_war[xb] else []), inc="v")
                            dnrel[tsh] = xa
                            if half == 1:
                                x2_war[xb] = xa
                                xf_war[xb] = S.dma("gpsimd", lambda e, tt=tt, xb=xb: e.dma_start(
                                    out=xf_sc[tt * 128:(tt + 1) * 128, :], in_=xfout[xb][:, :]),
                                    waits=[("v", xa)])
                    hbuf_war = S.c("pe")
                p2_v, p2_a, p2_pe = S.c("v"), S.c("a"), S.c("pe")
                p2_xf_st = [("dg0", S.c("dg0")), ("dg1", S.c("dg1"))]
                S.flush()

            with contextlib.ExitStack() as sQ:
                E3 = sQ.enter_context
                wattn_b = E3(nc.sbuf_tensor("wattn_b", [128, DC * 3 * C], BF16, side="right"))
                qkst = [E3(nc.sbuf_tensor(f"qkst{i}", [128, 512], BF16, side="right")) for i in range(4)]
                wst3 = [E3(nc.sbuf_tensor(f"wst3_{i}", [128, 3 * C], F32, side="right")) for i in range(2)]
                ps_qk = E3(nc.psum_tensor("ps_qk", [128, 2048], F32))
                ps_v = E3(nc.psum_tensor("ps_v", [128, 2048], F32))

                mc = S.op("tensor", lambda e: e.matmul(
                    ps_qk[:, 0:1], onesf_s[0:1, :], c11_s[0:1, :], start=True, stop=True),
                    waits=[("v", cv_c11), ("v", p2_v), ("a", p2_a), ("pe", p2_pe)]
                          + p2_xf_st,
                    inc="pe")
                ce = S.op("scalar", lambda e: e.activation(
                    cvec_s[:, :], ps_qk[:, 0:1], AF.Copy),
                    waits=[("pe", mc)], inc="a")
                cq = S.op("vector", lambda e: e.tensor_scalar(
                    cbq_s[:, :], bqk_s[:, 0:8], cvec_s[:, :], None, AluOpType.mult),
                    waits=[("a", ce)], inc="v")

                wat_cv = 0
                wat_cv_of = [0] * DC
                wst_war = [0, 0]
                for dc in range(DC):
                    wb = dc % 2
                    ld = S.dma("sync", lambda e, dc=dc, wb=wb: e.dma_start(
                        out=wst3[wb][:, :], in_=wattn[dc * 128:(dc + 1) * 128, :]),
                        waits=[("v", wst_war[wb])]
                              + ([("pe", p2_pe), ("v", p2_v), ("a", p2_a)] + p2_xf_st
                                 if dc < 2 else []))
                    wat_cv = S.op("vector", lambda e, dc=dc, wb=wb: e.tensor_copy(
                        wattn_b[:, dc * 3072:(dc + 1) * 3072], wst3[wb][:, :]),
                        waits=[ld], inc="v")
                    wat_cv_of[dc] = wat_cv
                    wst_war[wb] = wat_cv

                qk_rel = [("a", ce), ("v", 0), ("v", 0), ("v", 0)]
                qk_st_war = [("v", 0)] * 4
                store_of_fc = {}
                qk_st_fin = None
                for fc in range(16):
                    for tc in range(NTB):
                        bank = (fc * NTB + tc) % 4
                        lastm = 0
                        for dc in range(DC):
                            wts = [("v", wat_cv_of[dc])]
                            if dc == 0:
                                wts.append(qk_rel[bank])
                            lastm = S.op("tensor", lambda e, fc=fc, tc=tc, dc=dc, bank=bank: e.matmul(
                                ps_qk[:, bank * 512:(bank + 1) * 512],
                                wattn_b[:, dc * 3072 + fc * 128: dc * 3072 + (fc + 1) * 128],
                                xnT[:, dc * T + tc * 512: dc * T + (tc + 1) * 512],
                                start=(dc == 0), stop=(dc == DC - 1)),
                                waits=wts, inc="pe" if dc == DC - 1 else None)
                        stb = qkst[bank]
                        if fc < 8:
                            ev = S.op("vector", lambda e, stb=stb, bank=bank, fc=fc: e.tensor_scalar(
                                stb[:, :], ps_qk[:, bank * 512:(bank + 1) * 512],
                                cvec_s[:, :], cbq_s[:, fc:fc + 1],
                                AluOpType.mult, AluOpType.add),
                                waits=[("pe", lastm), ("v", cq), ("a", ce),
                                       qk_st_war[bank]], inc="v")
                            sto = S.dma("gpsimd", lambda e, stb=stb, fc=fc, tc=tc: e.dma_start(
                                out=q_sc[fc, :, tc * 512:(tc + 1) * 512], in_=stb[:, :]),
                                waits=[("v", ev)])
                        else:
                            ev = S.op("vector", lambda e, stb=stb, bank=bank, fc=fc: e.tensor_scalar(
                                stb[:, :], ps_qk[:, bank * 512:(bank + 1) * 512],
                                bqk_s[:, fc:fc + 1], None, AluOpType.add),
                                waits=[("pe", lastm), qk_st_war[bank]], inc="v")
                            sto = S.dma("gpsimd", lambda e, stb=stb, fc=fc, tc=tc: e.dma_start(
                                out=k_sc[fc - 8, :, tc * 512:(tc + 1) * 512], in_=stb[:, :]),
                                waits=[("v", ev)])
                        qk_st_war[bank] = sto
                        store_of_fc.setdefault(fc, []).append(sto)
                        store_of_fc[fc] = store_of_fc[fc][-2:]
                        qk_rel[bank] = ("v", ev)
                v_rel = [0, 0, 0, 0]
                for tt in range(NT):
                    for half in range(2):
                        bank = (tt * 2 + half) % 4
                        wts0 = [("a", v_rel[bank]), ("v", cv_bv)]
                        for dc in range(DC):
                            wts0dc = [("v", wat_cv_of[dc])] + (wts0 if dc == 0 else [])
                            S.op("tensor", lambda e, tt=tt, half=half, dc=dc, bank=bank: e.matmul(
                                ps_v[:, bank * 512:(bank + 1) * 512],
                                xnT[:, dc * T + tt * 128: dc * T + (tt + 1) * 128],
                                wattn_b[:, dc * 3072 + 2048 + half * 512: dc * 3072 + 2048 + (half + 1) * 512],
                                start=(dc == 0), stop=False),
                                waits=wts0dc)
                        lastm = S.op("tensor", lambda e, half=half, bank=bank: e.matmul(
                            ps_v[:, bank * 512:(bank + 1) * 512],
                            onesb_s[0:1, :], bv_b[0:1, half * 512:(half + 1) * 512],
                            start=False, stop=True), inc="pe")
                        ev = S.op("scalar", lambda e, tt=tt, half=half, bank=bank: e.activation(
                            Vt[:, tt * C + half * 512: tt * C + (half + 1) * 512],
                            ps_v[:, bank * 512:(bank + 1) * 512], AF.Copy),
                            waits=[("pe", lastm)], inc="a")
                        v_rel[bank] = ev
                qk_st_fin = [("dg0", S.c("dg0")), ("dg1", S.c("dg1"))]
                p3_a, p3_pe, p3_v = S.c("a"), S.c("pe"), S.c("v")
                S.flush()

        with contextlib.ExitStack() as s4:
            E4 = s4.enter_context
            ytile = E4(nc.sbuf_tensor("ytile", [128, NT * C], BF16, side="right"))
            s4b = contextlib.ExitStack()
            E4b = s4b.enter_context
            snegp = [E4b(nc.sbuf_tensor(f"snegp{p}", [128, 2 * T], F32, side="right")) for p in range(2)]
            sneg = [[snegp[p][:, j * T:(j + 1) * T] for j in range(2)] for p in range(2)]
            pbuf = [[E4b(nc.sbuf_tensor(f"pbuf{p}_{j}", [128, T], BF16, side="right")) for j in range(2)] for p in range(2)]
            ptb2 = [E4b(nc.sbuf_tensor(f"ptb2_{i}", [128, 1024], BF16, side="right")) for i in range(8)]
            ncmx = E4b(nc.sbuf_tensor("ncmx", [128, 24], F32, side="right"))
            negmax = E4b(nc.sbuf_tensor("negmax", [128, 6], F32, side="right"))
            sumr = E4b(nc.sbuf_tensor("sumr", [128, 6], F32, side="right"))
            rsum = E4b(nc.sbuf_tensor("rsum", [128, 6], F32, side="right"))
            ps_s = E4(nc.psum_tensor("ps_s", [128, 2048], F32))
            ps_tr = E4(nc.psum_tensor("ps_tr", [128, 1024], F32))
            ps_y = E4(nc.psum_tensor("ps_y", [128, 1024], F32))

            iters = [(hp, qt) for hp in range(H // 2) for qt in range(NT)]
            NIT = len(iters)
            sbank_rel = [0, 0]
            gchunk = [0]
            trslot = [("v", 0), ("v", 0)]
            ptb_war = [0] * 16
            exp_done = [0] * NIT
            mm_done = {}
            sneg_rel = [0, 0]
            pbuf_war = [[0, 0], [0, 0]]
            ysc_done = {}
            av_of = [0] * NIT
            recip_done = [0] * NIT
            abc_war = [0, 0]
            qk_war = [0, 0]
            group_ld = [None, None]
            out_war = [None]

            def chunks_of(L):
                out = []
                c0 = 0
                while c0 < L:
                    w = min(512, L - c0)
                    out.append((c0, w))
                    c0 += w
                return out

            def emit_group_load(hp, early=False):
                hb = hp % 2
                if early:
                    wts = [wat_fin, qk_st_war[2], qk_st_war[3]]
                    wq = [wat_fin] + store_of_fc[0] + store_of_fc[8]
                else:
                    wts = [("v", abc_war[hb])]
                    if hp == 0:
                        wts += [("pe", p3_pe), ("v", p3_v), ("a", p3_a)] + qk_st_fin
                    wq = [("pe", qk_war[hb])] if qk_war[hb] else []
                l1 = S.dma("sync", lambda e, hp=hp, hb=hb: e.dma_start(
                    out=abcp[hb][:, 0:T], in_=alibi_bc[2 * hp, :, :]), waits=wts)
                l2 = S.dma("sync", lambda e, hp=hp, hb=hb: e.dma_start(
                    out=abcp[hb][:, T:2 * T], in_=alibi_bc[2 * hp + 1, :, :]))
                l3 = S.dma("sync", lambda e, hp=hp, hb=hb: e.dma_start(
                    out=q_hp[hb][:, :], in_=q_sc[hp, :, :]), waits=wq)
                l4 = S.dma("sync", lambda e, hp=hp, hb=hb: e.dma_start(
                    out=k_hp[hb][:, :], in_=k_sc[hp, :, :]))
                group_ld[hb] = [l1, l2, l3, l4]

            def emit_scores_softmax(it):
                hp, qt = iters[it]
                L = (qt + 1) * 128
                par = it % 3
                sp = it % 2
                ch = chunks_of(L)
                nch = len(ch)
                cd = 0
                pend_red = []

                def emit_red():
                    nonlocal cd
                    if not pend_red:
                        return
                    ci, c0, w, dep = pend_red.pop(0)
                    cd = S.op("vector", lambda e, par=par, sp=sp, c0=c0, w=w, ci=ci: e.tensor_reduce(
                        ncmx[:, par * 8:par * 8 + 8].rearrange("p (j c) -> p j c", j=2)[:, :, ci:ci + 1],
                        snegp[sp].rearrange("p (j t) -> p j t", j=2)[:, :, c0:c0 + w],
                        AX.X, AluOpType.max),
                        waits=[("v", dep)], inc="v")

                for ci, (c0, w) in enumerate(ch):
                    cs = gchunk[0] % 2
                    gchunk[0] += 1
                    rA = cs * 1024
                    rB = cs * 1024 + 512
                    wts = []
                    if it == 0 and ci == 0:
                        wts += [("a", p3_a), ("v", p3_v)]
                    if sbank_rel[cs]:
                        wts.append(("v", sbank_rel[cs]))
                    hb = hp % 2
                    if qt == 0 and ci == 0:
                        wts += group_ld[hb][2:4]
                    S.op("tensor", lambda e, hb=hb, qt=qt, c0=c0, w=w, rA=rA: e.matmul(
                        ps_s[:, rA: rA + w],
                        q_hp[hb][0:64, qt * 128:(qt + 1) * 128],
                        k_hp[hb][0:64, c0:c0 + w],
                        start=True, stop=True, tile_position=(0, 0)),
                        waits=wts)
                    md = S.op("tensor", lambda e, hb=hb, qt=qt, c0=c0, w=w, rB=rB: e.matmul(
                        ps_s[:, rB: rB + w],
                        q_hp[hb][64:128, qt * 128:(qt + 1) * 128],
                        k_hp[hb][64:128, c0:c0 + w],
                        start=True, stop=True, tile_position=(64, 0)), inc="pe")
                    mm_done[(it, ci)] = md
                    qk_war[hb] = md
                    wts2 = [("pe", md)]
                    if qt == 0 and ci == 0:
                        wts2 += group_ld[hp % 2][0:2]
                    if sneg_rel[sp]:
                        wts2.append(("a", sneg_rel[sp]))
                    aoff = T - 128 + c0 - qt * 128
                    stt = S.op("vector", lambda e, sp=sp, hb=hp % 2, aoff=aoff, c0=c0, w=w, rA=rA: e.scalar_tensor_tensor(
                        snegp[sp].rearrange("p (j t) -> p j t", j=2)[:, :, c0:c0 + w],
                        ps_s[:, rA:rA + 1024].rearrange("p (j t) -> p j t", j=2)[:, :, 0:w],
                        1.0,
                        abcp[hb].rearrange("p (j t) -> p j t", j=2)[:, :, aoff:aoff + w],
                        AluOpType.mult, AluOpType.add),
                        waits=wts2, inc="v")
                    abc_war[hp % 2] = S.c("v")
                    sbank_rel[cs] = stt
                    pend_red.append((ci, c0, w, stt))
                    if ci == nch - 1:
                        while pend_red:
                            emit_red()
                    elif len(pend_red) >= 2:
                        emit_red()
                nms = []
                for j in range(2):
                    nms.append(S.op("vector", lambda e, par=par, j=j, nch=nch: e.tensor_reduce(
                        negmax[:, par * 2 + j: par * 2 + j + 1],
                        ncmx[:, par * 8 + j * 4: par * 8 + j * 4 + nch], AX.X,
                        AluOpType.max, negate=True),
                        waits=[("v", cd), ("a", exp_done[it - 3] if it >= 3 else 0)], inc="v"))
                return nms

            def emit_exp(it, nms):
                hp, qt = iters[it]
                L = (qt + 1) * 128
                par = it % 3
                pb = it % 2
                ed = 0
                for j in range(2):
                    wts = [("v", nms[j])]
                    if it >= 2:
                        wts.append(("pe", pbuf_war[pb][j]))
                    if it >= 3:
                        wts.append(("v", recip_done[it - 3]))
                    ed = S.op("scalar", lambda e, par=par, pb=pb, j=j, L=L: e.activation(
                        pbuf[pb][j][:, 0:L], sneg[pb][j][:, 0:L], AF.Exp,
                        bias=negmax[:, par * 2 + j: par * 2 + j + 1],
                        accum_out=sumr[:, par * 2 + j: par * 2 + j + 1]),
                        waits=wts, inc="a")
                exp_done[it] = ed
                sneg_rel[pb] = ed

            def emit_recip(it):
                par = it % 3
                r = 0
                for j in range(2):
                    r = S.op("vector", lambda e, par=par, j=j: e.reciprocal(
                        rsum[:, par * 2 + j: par * 2 + j + 1],
                        sumr[:, par * 2 + j: par * 2 + j + 1]),
                        waits=[("a", exp_done[it]), ("a", ysc_done.get(it - 3, 0))], inc="v")
                recip_done[it] = r

            tr_evs = {}
            evac_cnt = [0]

            def emit_tr(it):
                hp, qt = iters[it]
                par = it % 2
                nkc = qt + 1
                nbat = (nkc + 3) // 4
                evs = []
                pend = []
                gb = [0]

                def flush_evac():
                    if not pend:
                        return
                    use_v = False
                    evac_cnt[0] += 1
                    eng, tag = ("vector", "v") if use_v else ("scalar", "a")

                    def copy_op(e, dst, src):
                        if use_v:
                            return e.tensor_copy(dst, src)
                        return e.activation(dst, src, AF.Copy)

                    if False and len(pend) == 2 and pend[0][2] == 0 and pend[1][2] == 1 and \
                       pend[0][4] % 2 == 0 and pend[0][4] + 1 == pend[1][4]:
                        (j0, k0, s0, kcs0, p0, lt0), (j1, k1, s1, kcs1, p1, lt1) = pend
                        wb1 = len(kcs1) * 128
                        ev = S.op(eng, lambda e, p0=p0, wb1=wb1: copy_op(
                            e, ptb2[p0 // 2][:, 0:512 + wb1], ps_tr[:, 0:512 + wb1]),
                            waits=[("pe", lt1)], inc=tag)
                        trslot[0] = (tag, ev)
                        trslot[1] = (tag, ev)
                        evs.append((j0, k0, (tag, ev), kcs0, p0))
                        evs.append((j1, k1, (tag, ev), kcs1, p1))
                    else:
                        for (j, kcb, slot, kcs, pslot, lt) in pend:
                            wb = len(kcs) * 128
                            ev = S.op(eng, lambda e, pslot=pslot, slot=slot, wb=wb: copy_op(
                                e, ptb2[pslot // 2][:, (pslot % 2) * 512: (pslot % 2) * 512 + wb],
                                ps_tr[:, slot * 512: slot * 512 + wb]),
                                waits=[("pe", lt)], inc=tag)
                            trslot[slot] = (tag, ev)
                            evs.append((j, kcb, (tag, ev), kcs, pslot))
                    pend.clear()

                for j in range(2):
                    lasttr = 0
                    for kcb in range(nbat):
                        slot = gb[0] % 2
                        gb[0] += 1
                        pslot = par * 8 + j * 4 + kcb
                        kcs = list(range(kcb * 4, min(kcb * 4 + 4, nkc)))
                        for idx, kc in enumerate(kcs):
                            wts = []
                            if idx == 0:
                                wts = [("a", exp_done[it]), trslot[slot],
                                       ("pe", ptb_war[pslot])]
                            lasttr = S.op("tensor", lambda e, par=par, j=j, kc=kc, slot=slot, idx=idx, n=len(kcs): e.matmul(
                                ps_tr[:, slot * 512 + (kc % 4) * 128: slot * 512 + (kc % 4) * 128 + 128],
                                pbuf[par][j][:, kc * 128:(kc + 1) * 128], ident_s[:, :],
                                start=(idx == 0), stop=(idx == n - 1)),
                                waits=wts, inc="pe" if idx == len(kcs) - 1 else None)
                        pend.append((j, kcb, slot, kcs, pslot, lasttr))
                        if slot == 1:
                            flush_evac()
                    pbuf_war[par][j] = lasttr
                flush_evac()
                tr_evs[it] = evs

            def emit_av(it):
                hp, qt = iters[it]
                ypar = it % 2
                nkc = qt + 1
                evs = tr_evs.pop(it)
                for (j, kcb, ev, kcs, pslot) in evs:
                    h = 2 * hp + j
                    for idx, kc in enumerate(kcs):
                        wts = [ev] if idx == 0 else []
                        if j == 0 and kc == 0:
                            wts = list(wts) + [("a", ysc_done.get(it - 2, 0))]
                        first = (j == 0 and kc == 0)
                        last = (j == 1 and kc == nkc - 1)
                        r = S.op("tensor", lambda e, pslot=pslot, ypar=ypar, j=j, kc=kc, h=h, first=first, last=last: e.matmul(
                            ps_y[:, ypar * 512 + j * 64: ypar * 512 + j * 64 + 64],
                            ptb2[pslot // 2][:, (pslot % 2) * 512 + (kc % 4) * 128: (pslot % 2) * 512 + (kc % 4) * 128 + 128],
                            Vt[:, kc * C + h * 64: kc * C + h * 64 + 64],
                            start=first, stop=last),
                            waits=wts, inc="pe" if last else None)
                        if last:
                            av_of[it] = r
                    ptb_war[pslot] = S.c("pe")

            def emit_ysc(it):
                hp, qt = iters[it]
                par = it % 3
                ypar = it % 2
                y = 0
                for j in range(2):
                    h = 2 * hp + j
                    wts_y = [("pe", av_of[it]), ("v", recip_done[it])]
                    y = S.op("scalar", lambda e, par=par, ypar=ypar, j=j, qt=qt, h=h: e.activation(
                        ytile[:, qt * C + h * 64: qt * C + (h + 1) * 64],
                        ps_y[:, ypar * 512 + j * 64: ypar * 512 + j * 64 + 64],
                        AF.Copy, scale=rsum[:, par * 2 + j: par * 2 + j + 1]),
                        waits=wts_y, inc="a")
                ysc_done[it] = y

            wpj_b = E4b(nc.sbuf_tensor("wpj_b", [128, DC * C], BF16, side="right"))
            yT_sb = E4b(nc.sbuf_tensor("yT_sb", [128, C], BF16, side="right"))
            xfst = E4b(nc.sbuf_tensor("xfst", [128, C], F32, side="right"))
            outsb = E4b(nc.sbuf_tensor("outsb", [128, C], F32, side="right"))
            abcp1 = E4b(nc.sbuf_tensor("abcp1", [128, 2 * T], F32, side="right"))
            abcp0 = E4b(nc.sbuf_tensor("abcp0", [128, 2 * T], F32, side="right"))
            abcp = [abcp0, abcp1]
            q_hp = [E4b(nc.sbuf_tensor(f"q_hp{i}", [128, T], BF16, side="right")) for i in range(2)]
            k_hp = [E4b(nc.sbuf_tensor(f"k_hp{i}", [128, T], BF16, side="right")) for i in range(2)]
            wat_fin = ("v", wat_cv)
            emit_group_load(0, early=(T >= 2048))
            pj_rel = [0]
            yT_war = [0]
            xf_war = [0]
            for cc in range(DC):
                S.dma("sync", lambda e, cc=cc: e.dma_start(
                    out=wpj_b[:, cc * 1024:(cc + 1) * 1024], in_=wpj_sc[cc, :, :]),
                    waits=conv_stores_fin + (qk_st_fin + [("pe", p3_pe), ("v", p3_v), ("a", p3_a)]
                                             if cc == 0 else []))
            wpj_ld = [("dl0", S.c("dl0")), ("dl1", S.c("dl1"))]

            def emit_proj_unit(qt, ydep):
                for b2 in range(2):
                    slot = b2
                    last = 0
                    for j in range(4):
                        cc = b2 * 4 + j
                        wts = []
                        if j == 0:
                            wts = [("a", ydep), trslot[slot]]
                        last = S.op("tensor", lambda e, qt=qt, cc=cc, slot=slot, j=j: e.matmul(
                            ps_tr[:, slot * 512 + j * 128: slot * 512 + (j + 1) * 128],
                            ytile[:, qt * C + cc * 128: qt * C + (cc + 1) * 128],
                            ident_s[:, :], start=(j == 0), stop=(j == 3)),
                            waits=wts, inc="pe" if j == 3 else None)
                    ev = S.op("vector", lambda e, b2=b2, slot=slot: e.tensor_copy(
                        yT_sb[:, b2 * 512:(b2 + 1) * 512],
                        ps_tr[:, slot * 512:(slot + 1) * 512]),
                        waits=[("pe", last), ("pe", yT_war[0])], inc="v")
                    trslot[slot] = ("v", ev)
                yT_done = S.c("v")
                xl = S.dma("sync", lambda e, qt=qt: e.dma_start(
                    out=xfst[:, :], in_=xf_sc[qt * 128:(qt + 1) * 128, :]),
                    waits=p2_xf_st + [("v", xf_war[0])])
                lastp = 0
                for cc in range(DC):
                    for half in range(2):
                        slot = half
                        wts = []
                        if cc == 0:
                            wts = wpj_ld + [("v", yT_done), trslot[slot],
                                            ("v", pj_rel[0]), trslot[1 - slot]]
                        lastp = S.op("tensor", lambda e, slot=slot, cc=cc, half=half: e.matmul(
                            ps_tr[:, slot * 512:(slot + 1) * 512],
                            yT_sb[:, cc * 128:(cc + 1) * 128],
                            wpj_b[:, cc * 1024 + half * 512: cc * 1024 + half * 512 + 512],
                            start=(cc == 0), stop=False),
                            waits=wts, inc="pe" if half == 1 else None)
                oas = []
                for half in range(2):
                    slot = half
                    lastb = S.op("tensor", lambda e, slot=slot, half=half: e.matmul(
                        ps_tr[:, slot * 512:(slot + 1) * 512], onesb_s[0:1, :],
                        bpj_b[0:1, half * 512:(half + 1) * 512],
                        start=False, stop=True),
                        waits=[("v", cv_bp)], inc="pe")
                    oa = S.op("vector", lambda e, slot=slot, half=half: e.tensor_tensor(
                        outsb[:, half * 512:(half + 1) * 512],
                        ps_tr[:, slot * 512:(slot + 1) * 512],
                        xfst[:, half * 512:(half + 1) * 512], AluOpType.add),
                        waits=[("pe", lastb), xl]
                              + ([out_war[0]] if out_war[0] else []), inc="v")
                    pj_rel[0] = oa
                    trslot[slot] = ("v", oa)
                yT_war[0] = S.c("pe")
                xf_war[0] = pj_rel[0]
                out_war[0] = S.dma("gpsimd", lambda e, qt=qt: e.dma_start(
                    out=out_d[qt * 128:(qt + 1) * 128, :], in_=outsb[:, :]),
                    waits=[("v", pj_rel[0])])

            nm_of = {}
            NHP = H // 2
            for step in range(NIT + 3):
                if step < NIT:
                    hp, qt = iters[step]
                    if qt == 0 and hp > 0:
                        emit_group_load(hp)
                    nm_of[step] = emit_scores_softmax(step)
                if 3 <= step <= NIT + 2:
                    emit_ysc(step - 3)
                if 1 <= step <= NIT:
                    emit_recip(step - 1)
                    emit_tr(step - 1)
                if 2 <= step <= NIT + 1:
                    emit_av(step - 2)
                if step < NIT:
                    emit_exp(step, nm_of[step])
                if 3 <= step <= NIT + 2:
                    it3 = step - 3
                    hp3, qt3 = iters[it3]
                    if hp3 == NHP - 1:
                        emit_proj_unit(qt3, ysc_done[it3])
            S.flush()
            s4b.close()

            fin = [("dg0", S.c("dg0")), ("dg1", S.c("dg1"))]
            S.op("gpsimd", lambda e: None, waits=fin)
            S.op("sync", lambda e: None, waits=fin)
            S.flush()

    return nc



def _consts(T):
    start = 2.0 ** (-(2.0 ** -(math.log2(H) - 3)))
    slopes = np.array([start * start ** i for i in range(H)], dtype=np.float32)
    d_rel = (np.arange(T, dtype=np.float32)[None, :] + np.float32(128 - T)
             - np.arange(128, dtype=np.float32)[:, None])
    alibi_bc = np.ascontiguousarray(slopes[:, None, None] * d_rel[None, :, :])
    alibi_bc[:, d_rel > 0] = np.float32(-1e38)
    return {
        "ident": np.eye(128, dtype=BF),
        "alibi_bc": alibi_bc,
        "ones_b": np.ones((1, 128), dtype=BF),
        "ones_f": np.ones((1, 128), dtype=np.float32),
    }


def make_inputs(xb, W_attn, b_attn, W_proj, b_proj, temperature, wg, wu, wd, T):
    con = _consts(T)
    wgr = np.ascontiguousarray(
        wg.reshape(HCP, 2, 128, DC, 128).transpose(0, 4, 1, 3, 2).reshape(HCP, 128, 2048))
    wur = np.ascontiguousarray(
        wu.reshape(HCP, 2, 128, DC, 128).transpose(0, 4, 1, 3, 2).reshape(HCP, 128, 2048))
    wdr = np.ascontiguousarray(
        wd.T.reshape(HCP, 2, 128, C).transpose(0, 2, 1, 3).reshape(HCP, 128, 2048))
    d = {
        "xin": np.ascontiguousarray(xb),
        "wattn": np.ascontiguousarray(W_attn),
        "battn_qk": np.ascontiguousarray(b_attn[:2048].reshape(16, 128).T),
        "battn_v": np.ascontiguousarray(b_attn[2048:].reshape(1, C)),
        "wproj": np.ascontiguousarray(W_proj),
        "bproj": np.ascontiguousarray(b_proj.reshape(1, C)),
        "temp": np.ascontiguousarray(temperature.reshape(1, 1)),
        "wg_r": wgr,
        "wu_r": wur,
        "wd_r": wdr,
    }
    d.update(con)
    return d


def kernel(L0_attn_x, L0_attn_W_attn, L0_attn_b_attn, L0_attn_W_proj,
           L0_attn_b_proj, L0_attn_temperature, L0_ffn_w_gate, L0_ffn_w_up,
           L0_ffn_w_down):
    T = int(L0_attn_x.shape[1])
    x = np.ascontiguousarray(np.asarray(L0_attn_x, dtype=np.float32))
    W_attn = np.asarray(L0_attn_W_attn, dtype=np.float32)
    b_attn = np.asarray(L0_attn_b_attn, dtype=np.float32)
    W_proj = np.asarray(L0_attn_W_proj, dtype=np.float32)
    b_proj = np.asarray(L0_attn_b_proj, dtype=np.float32)
    tempv = np.asarray(L0_attn_temperature, dtype=np.float32)
    wg = np.asarray(L0_ffn_w_gate, dtype=np.int32)
    wu = np.asarray(L0_ffn_w_up, dtype=np.int32)
    wd = np.asarray(L0_ffn_w_down, dtype=np.int32)

    nc = build_program(T)
    in_maps = [make_inputs(x[b], W_attn, b_attn, W_proj, b_proj, tempv, wg, wu, wd, T)
               for b in range(x.shape[0])]
    res = run_bass_kernel_spmd(nc, in_maps, core_ids=list(range(x.shape[0])))
    return np.stack([r["out"] for r in res.results], axis=0)
